# version 01
# baseline (speedup 1.0000x reference)
import math
import numpy as np
import ml_dtypes
import concourse.bass as bass
import concourse.mybir as mybir
from concourse.bass_utils import run_bass_kernel_spmd

F32 = mybir.dt.float32
BF16 = mybir.dt.bfloat16
AF = mybir.ActivationFunctionType
ALU = mybir.AluOpType
AX = mybir.AxisListType

B, S, D, H, DK = 2, 4096, 2048, 16, 128
NH = 4
NT = 1024
NE, DF = 8, 1024
NDC = D // 128
NSC = S // 512
NQC = S // 512
NQT = 4
NFC = DF // 128
NFB = 8
EGRPS = [(0, 1), (2, 3), (4, 5), (6, 7)]
ISQ = 1.0 / math.sqrt(DK)
EPS = 1e-5
ABLATE = set()
ENGS = ("sync", "tensor", "vector", "scalar", "gpsimd")


class Sem:
    def __init__(self, h):
        self.h = h
        self.n = 0


class Prog:
    def __init__(self):
        self.items = {e: [] for e in ENGS}
        self.last_wait = {}

    def op(self, eng_name, build_fn, waits=(), inc=None, inc_by=1):
        wl = []
        for (s, v) in waits:
            if v <= 0:
                continue
            key = (eng_name, id(s))
            if self.last_wait.get(key, -1) < v:
                self.last_wait[key] = v
                wl.append((s, v))
        if inc is not None:
            inc.n += inc_by
        wl = tuple(wl)

        def f(eng, build_fn=build_fn, wl=wl, inc=inc, inc_by=inc_by):
            for (s, v) in wl:
                eng.wait_ge(s.h, v)
            ins = build_fn(eng)
            if inc is not None and ins is not None:
                ins.then_inc(inc.h, inc_by)
        self.items[eng_name].append(f)
        return inc.n if inc is not None else 0

    def wait(self, eng_name, *waits):
        self.op(eng_name, lambda e: None, waits=waits)

    def replay(self, block):
        for name in ENGS:
            ops = self.items[name]
            if not ops:
                continue

            def body(eng, ops=ops):
                for f in ops:
                    f(eng)
            getattr(block, name)(body)


def build_program(taps=(), stop_after=None):
    taps = set(taps)
    nc = bass.Bass("TRN2", target_bir_lowering=False, debug=False, num_devices=8)

    xT = nc.dram_tensor("xT", [D, S], F32, kind="ExternalInput").ap()
    xo = nc.dram_tensor("xo", [NT, D], F32, kind="ExternalInput").ap()
    wqT = nc.dram_tensor("wqT", [D, NH * DK], F32, kind="ExternalInput").ap()
    wkT = nc.dram_tensor("wkT", [D, NH * DK], F32, kind="ExternalInput").ap()
    wvT = nc.dram_tensor("wvT", [D, NH * DK], F32, kind="ExternalInput").ap()
    woP = nc.dram_tensor("woP", [NH * DK, D], F32, kind="ExternalInput").ap()
    rwT = nc.dram_tensor("rwT", [D, NE], F32, kind="ExternalInput").ap()
    w1t = nc.dram_tensor("w1t", [NE, NFB, 128, NDC, 128], F32, kind="ExternalInput").ap()
    w2t = nc.dram_tensor("w2t", [NE, NFB, 128, NDC, 128], F32, kind="ExternalInput").ap()
    w3n = nc.dram_tensor("w3n", [NE, DF, D], F32, kind="ExternalInput").ap()
    cs = nc.dram_tensor("cs", [2, 128, S], F32, kind="ExternalInput").ap()
    msk = nc.dram_tensor("msk", [NQT, 128, 512], F32, kind="ExternalInput").ap()
    idin = nc.dram_tensor("idin", [128, 128], F32, kind="ExternalInput").ap()
    outT = nc.dram_tensor("outT", [NT, D], F32, kind="ExternalOutput").ap()

    qb = nc.dram_tensor("qb", [NH, 128, S], BF16, kind="Internal").ap()
    kb = nc.dram_tensor("kb", [NH, 128, S], BF16, kind="Internal").ap()
    vb = nc.dram_tensor("vb", [NH, S, 128], BF16, kind="Internal").ap()
    rsin = [nc.dram_tensor(f"rsin{i}", [S, 512], F32, kind="Internal").ap() for i in range(4)]
    rsout = [nc.dram_tensor(f"rsout{i}", [NT, 512], F32, kind="Internal").ap() for i in range(4)]
    w1b = nc.dram_tensor("w1b", [NE, NFB, 128, NDC, 128], BF16, kind="Internal").ap()
    w2b = nc.dram_tensor("w2b", [NE, NFB, 128, NDC, 128], BF16, kind="Internal").ap()
    w3b = nc.dram_tensor("w3b", [NE, DF, D], BF16, kind="Internal").ap()

    tap_out = {}
    if "qkv" in taps:
        tap_out["tap_q"] = nc.dram_tensor("tap_q", [NH, 128, S], BF16, kind="ExternalOutput").ap()
        tap_out["tap_k"] = nc.dram_tensor("tap_k", [NH, 128, S], BF16, kind="ExternalOutput").ap()
        tap_out["tap_v"] = nc.dram_tensor("tap_v", [NH, S, 128], BF16, kind="ExternalOutput").ap()
    if "ctx" in taps:
        tap_out["tap_ctx"] = nc.dram_tensor("tap_ctx", [128, NH * S], BF16, kind="ExternalOutput").ap()
    if "rs" in taps:
        tap_out["tap_rs"] = nc.dram_tensor("tap_rs", [4, NT, 512], F32, kind="ExternalOutput").ap()
    if "h" in taps:
        tap_out["tap_h"] = nc.dram_tensor("tap_h", [NT, D], F32, kind="ExternalOutput").ap()
    if "r" in taps:
        tap_out["tap_r"] = nc.dram_tensor("tap_r", [128, 64], F32, kind="ExternalOutput").ap()
    if "ffn" in taps:
        tap_out["tap_ffn"] = nc.dram_tensor("tap_ffn", [NT, D], F32, kind="ExternalOutput").ap()

    ab = nc.alloc_sbuf_tensor("arena_bf", [128, 49152], BF16).ap()
    af = nc.alloc_sbuf_tensor("arena_f32", [128, 11264], F32).ap()
    acc = nc.alloc_sbuf_tensor("acc", [128, 8 * D], F32).ap()
    identb = nc.alloc_sbuf_tensor("identb", [128, 128], BF16).ap()
    identf = nc.alloc_sbuf_tensor("identf", [128, 128], F32).ap()
    ones1f = nc.alloc_sbuf_tensor("ones1f", [1, 128], F32).ap()
    rw_s = nc.alloc_sbuf_tensor("rw_s", [128, NDC * NE], BF16).ap()
    rrec_s = nc.alloc_sbuf_tensor("rrec_s", [128, 8], F32).ap()
    rT_s = nc.alloc_sbuf_tensor("rT_s", [1, 512], F32).ap()

    wq_s = ab[:, 0:8192]
    wk_s = ab[:, 8192:16384]
    wv_s = ab[:, 16384:24576]
    xbf = [ab[:, 24576 + i * 8192:24576 + (i + 1) * 8192] for i in (0, 1)]
    qstage = ab[:, 40960:43008]
    kstage = ab[:, 43008:45056]
    vstage = ab[:, 45056:47104]
    cst = ab[:, 0:16384]
    qh = ab[:, 16384:20480]
    kh = ab[:, 20480:24576]
    vh = ab[:, 24576:28672]
    attn = ab[:, 28672:45056]
    atb = ab[:, 45056:47104]
    woP_s = ab[:, 28672:36864]
    zbuf = ab[:, 0:16384]
    h_bf = ab[:, 16384:32768]
    w1buf = [ab[:, 32768 + i * 2048:32768 + (i + 1) * 2048] for i in (0, 1)]
    w2buf = [ab[:, 36864 + i * 2048:36864 + (i + 1) * 2048] for i in (0, 1)]
    w3buf = [ab[:, 40960 + i * 4096:40960 + (i + 1) * 4096] for i in (0, 1)]

    ropB = af[:, 0:512]
    c2 = af[:, 2048:6144]
    s2 = af[:, 6144:10240]
    swp = af[:, 10240:10752]
    ropA = af[:, 10752:11264]
    msk_s = af[:, 0:2048]
    scd = af[:, 10240:10752]
    rs = af[:, 10752:10816]
    recb = af[:, 2048:2560]
    wostg = [af[:, 6144 + i * 512:6144 + (i + 1) * 512] for i in (0, 1)]
    rT2 = af[0:1, 0:1024]
    tiny = af[:, 1024:1088]
    rta = af[:, 1088:1152]
    lg = af[:, 1152:1160]
    m1 = af[:, 1160:1161]
    l2 = af[:, 1168:1176]
    m2 = af[:, 1176:1177]
    dd = af[:, 1177:1178]
    wA = af[:, 1178:1179]
    wB = af[:, 1179:1180]
    mk1 = af[:, 1184:1192]
    mk2 = af[:, 1192:1200]
    sqs = af[:, 2048:4096]
    stg = af[:, 4096:6144]
    silA = af[:, 6144:6656]
    silB = af[:, 6656:7168]
    rb_f = af[:, 7168:8192]

    p0 = nc.alloc_psum_tensor("p0", [128, 512], F32).ap()
    p1 = nc.alloc_psum_tensor("p1", [128, 512], F32).ap()
    p2 = nc.alloc_psum_tensor("p2", [128, 512], F32).ap()
    p7 = nc.alloc_psum_tensor("p7", [128, 512], F32).ap()
    t0 = nc.alloc_psum_tensor("t0", [128, 512], BF16).ap()
    t1 = nc.alloc_psum_tensor("t1", [128, 512], BF16).ap()

    import contextlib
    stack = contextlib.ExitStack()
    SN = ["sDM", "sGD", "sGD2", "sPE", "sPE2", "sDV", "sDV2", "sDV3",
          "sAC", "sAC2", "sGP", "sCC", "sOUT",
          "sXB0", "sXB1", "sSTQ", "sSTK", "sSTV", "sHLD",
          "sWS0", "sWS1", "sW10", "sW11", "sW30", "sW31"]
    sem = {n: Sem(stack.enter_context(nc.semaphore(n))) for n in SN}
    sDM, sGD, sGD2 = sem["sDM"], sem["sGD"], sem["sGD2"]
    sPE, sPE2 = sem["sPE"], sem["sPE2"]
    sDV, sDV2, sDV3 = sem["sDV"], sem["sDV2"], sem["sDV3"]
    sAC, sAC2, sGP = sem["sAC"], sem["sAC2"], sem["sGP"]
    sCC, sOUT = sem["sCC"], sem["sOUT"]
    sXB = [sem["sXB0"], sem["sXB1"]]
    sSTQ, sSTK, sSTV = sem["sSTQ"], sem["sSTK"], sem["sSTV"]
    sHLD = sem["sHLD"]
    sWS = [sem["sWS0"], sem["sWS1"]]
    sW1 = [sem["sW10"], sem["sW11"]]
    sW3 = [sem["sW30"], sem["sW31"]]

    P = Prog()

    def dma(eng, out, in_, s, waits=(), **kw):
        return P.op(eng, lambda e: e.dma_start(out=out, in_=in_, **kw),
                    waits=waits, inc=s, inc_by=16)

    dma("gpsimd", identb[:, :], idin, sGD)
    dma("gpsimd", rw_s.rearrange("p (dc e) -> p dc e", dc=NDC),
        rwT.rearrange("(dc p) e -> p dc e", p=128), sGD)
    dma("gpsimd", wq_s.rearrange("p (dc o) -> p dc o", dc=NDC),
        wqT.rearrange("(dc p) o -> p dc o", p=128), sGD)
    dma("gpsimd", wk_s.rearrange("p (dc o) -> p dc o", dc=NDC),
        wkT.rearrange("(dc p) o -> p dc o", p=128), sGD)
    dma("gpsimd", wv_s.rearrange("p (dc o) -> p dc o", dc=NDC),
        wvT.rearrange("(dc p) o -> p dc o", p=128), sGD)
    dma("sync", identf[:, :], idin, sDM)
    dma("sync", c2, cs[0], sDM)
    dma("sync", s2, cs[1], sDM)
    P.op("gpsimd", lambda e: e.memset(ones1f[:, :], 1.0), inc=sGP)
    prolog_gd, prolog_dm, prolog_gp = sGD.n, sDM.n, sGP.n
    for e_ in range(NE):
        dma("gpsimd", w1b[e_].rearrange("a b c d -> (a b c d)"),
            w1t[e_].rearrange("a b c d -> (a b c d)"), sGD2)
        dma("gpsimd", w2b[e_].rearrange("a b c d -> (a b c d)"),
            w2t[e_].rearrange("a b c d -> (a b c d)"), sGD2)
        dma("gpsimd", w3b[e_].rearrange("a b -> (a b)"),
            w3n[e_].rearrange("a b -> (a b)"), sGD2)
    conv_gd2 = sGD2.n

    if stop_after == "P":
        P.wait("sync", (sGD, sGD.n), (sDM, sDM.n), (sGD2, sGD2.n), (sGP, sGP.n))
        return _finish(nc, P, stack, tap_out)

    xld = []
    xfree = []
    bank_free = {0: (0, 0), 1: (0, 0)}
    stage_dm = {"q": (sSTQ, 0), "k": (sSTK, 0), "v": (sSTV, 0)}
    swp_gate = 0
    ropB_gate = 0
    for sc in range(NSC):
        bi = sc % 2
        w = [(sGD, prolog_gd), (sDM, prolog_dm)]
        if sc >= 2:
            w.append((sPE, xfree[sc - 2]))
        dma("gpsimd", xbf[bi].rearrange("p (dc c) -> p dc c", dc=NDC),
            xT.rearrange("(dc p) s -> p dc s", p=128)[:, :, sc * 512:(sc + 1) * 512],
            sXB[bi], waits=w)
        xld.append((sXB[bi], sXB[bi].n))
        col = sc * 512
        gi = 0
        for (stage, w_s, name, dst) in (() if "noqk" in ABLATE else ((qstage, wq_s, "q", qb), (kstage, wk_s, "k", kb))):
            fin = []
            for ot in range(NH):
                psum = p0 if (gi % 2 == 0) else p1
                bfree = bank_free[gi % 2]
                for dc in range(NDC):
                    ww = [xld[sc]]
                    if dc == 0:
                        ww += [(sAC, bfree[0]), (sDV, bfree[1])]

                    def mm(e, psum=psum, w_s=w_s, dc=dc, ot=ot, bi=bi):
                        return e.matmul(
                            psum, w_s[:, dc * 512 + ot * 128:dc * 512 + (ot + 1) * 128],
                            xbf[bi][:, dc * 512:(dc + 1) * 512],
                            start=(dc == 0), stop=(dc == NDC - 1))
                    P.op("tensor", mm, waits=ww, inc=sPE if dc == NDC - 1 else None)
                pe_v = sPE.n
                if "norope" in ABLATE:
                    sl = stage[:, ot * 512:(ot + 1) * 512]
                    P.op("vector", lambda e, sl=sl, psum=psum: e.tensor_copy(sl, psum),
                         waits=[(sPE, pe_v), stage_dm[name], (sDV, ropB_gate)], inc=sDV)
                    ropB_gate = sDV.n
                    fin.append(sDV.n)
                    bank_free[gi % 2] = (sAC.n, sDV.n)
                    gi += 1
                    continue
                P.op("scalar", lambda e, psum=psum: e.copy(swp[0:64, :], psum[64:128, :]),
                     waits=[(sPE, pe_v), (sGP, swp_gate)], inc=sAC)
                P.op("scalar", lambda e, psum=psum: e.copy(swp[64:128, :], psum[0:64, :]),
                     inc=sAC)
                ac_v = sAC.n
                P.op("vector", lambda e, psum=psum, col=col: e.tensor_mul(
                    ropA, psum, c2[:, col:col + 512]),
                    waits=[(sPE, pe_v), (sDM, prolog_dm), (sDV, ropB_gate),
                           (sAC, ac_v)], inc=sDV)
                dv_a = sDV.n
                if "r3" in ABLATE:
                    P.op("gpsimd", lambda e, col=col: e.tensor_mul(
                        ropB, c2[:, col:col + 512], s2[:, col:col + 512]),
                        waits=[(sAC, ac_v), (sDV, ropB_gate)], inc=sGP)
                else:
                    P.op("gpsimd", lambda e, col=col: e.tensor_mul(
                        ropB, swp, s2[:, col:col + 512]),
                        waits=[(sAC, ac_v), (sDV, ropB_gate)], inc=sGP)
                swp_gate = sGP.n
                sl = stage[:, ot * 512:(ot + 1) * 512]
                P.op("vector", lambda e, sl=sl: e.tensor_add(sl, ropA, ropB),
                     waits=[(sDV, dv_a), (sGP, sGP.n), stage_dm[name]], inc=sDV)
                ropB_gate = sDV.n
                fin.append(sDV.n)
                bank_free[gi % 2] = (sAC.n, sDV.n)
                gi += 1
            sch = sSTQ if name == "q" else sSTK
            if "nostage" not in ABLATE:
                dma("sync", dst.rearrange("h p s -> p h s")[:, :, sc * 512:(sc + 1) * 512],
                    stage.rearrange("p (h c) -> p h c", h=NH), sch, waits=[(sDV, fin[-1])])
                stage_dm[name] = (sch, sch.n)
        vfin = []
        for st in range(0 if "nov" in ABLATE else 4):
            psum = p0 if (gi % 2 == 0) else p1
            bfree = bank_free[gi % 2]
            for dc in range(NDC):
                ww = [xld[sc]]
                if dc == 0:
                    ww += [(sAC, bfree[0]), (sDV, bfree[1])]

                def mmv(e, psum=psum, dc=dc, st=st, bi=bi):
                    return e.matmul(
                        psum, xbf[bi][:, dc * 512 + st * 128:dc * 512 + (st + 1) * 128],
                        wv_s[:, dc * 512:(dc + 1) * 512],
                        start=(dc == 0), stop=(dc == NDC - 1))
                P.op("tensor", mmv, waits=ww, inc=sPE if dc == NDC - 1 else None)
            pe_v = sPE.n
            P.op("vector", lambda e, psum=psum, st=st: e.tensor_copy(
                vstage[:, st * 512:(st + 1) * 512], psum),
                waits=[(sPE, pe_v), stage_dm["v"]], inc=sDV)
            vfin.append(sDV.n)
            bank_free[gi % 2] = (sAC.n, sDV.n)
            gi += 1
        for hh in range(0 if ("nov" in ABLATE or "nostage" in ABLATE) else NH):
            dma("sync",
                vb[hh].rearrange("(st p) d -> p st d", p=128)[:, sc * 4:(sc + 1) * 4, :],
                vstage.rearrange("p (st h d) -> p st h d", st=4, h=NH)[:, :, hh, :],
                sSTV, waits=[(sDV, vfin[-1])])
        stage_dm["v"] = (sSTV, sSTV.n)
        xfree.append(sPE.n)
    phA_bar = [(sSTQ, sSTQ.n), (sSTK, sSTK.n), (sSTV, sSTV.n)]
    phA_pe = sPE.n

    if "qkv" in taps:
        dma("sync", tap_out["tap_q"].rearrange("h p s -> (h p) s"),
            qb.rearrange("h p s -> (h p) s"), sOUT, waits=phA_bar)
        dma("sync", tap_out["tap_k"].rearrange("h p s -> (h p) s"),
            kb.rearrange("h p s -> (h p) s"), sOUT, waits=phA_bar)
        dma("sync", tap_out["tap_v"].rearrange("h s d -> (h s) d"),
            vb.rearrange("h s d -> (h s) d"), sOUT, waits=phA_bar)

    if stop_after == "A":
        P.wait("sync", (sOUT, sOUT.n))
        return _finish(nc, P, stack, tap_out)

    dma("sync", msk_s.rearrange("p (q c) -> p q c", q=NQT),
        msk.rearrange("q p c -> p q c"), sDM,
        waits=[(sDV, sDV.n), (sGP, sGP.n)])
    mask_dm = sDM.n
    dma("sync", acc.rearrange("p (st d) -> p st d", st=8),
        xo.rearrange("(st p) d -> p st d", p=128), sDM)
    xo_dm = sDM.n

    head_pe_end = phA_pe
    ctx_evac_dv = 0
    exp_attn_gate = 0
    rs_gate = 0
    recb_v = 0
    for hl in range(NH):
        dma("sync", qh, qb[hl], sHLD,
            waits=phA_bar + [(sPE, head_pe_end)])
        dma("sync", kh, kb[hl], sHLD)
        dma("sync", vh.rearrange("p (st d) -> p st d", st=32),
            vb[hl].rearrange("(st p) d -> p st d", p=128), sHLD)
        hload = sHLD.n
        sc_gate = {0: (0, 0), 1: (0, 0)}
        for qc in range(NQC):
            nkc = qc + 1
            exp_done = {}
            for qt in range(NQT):
                for kc in range(nkc):
                    bk = kc % 2
                    psum = p0 if bk == 0 else p1
                    ww = [(sHLD, hload), (sAC, sc_gate[bk][0]), (sDV, sc_gate[bk][1])]

                    def mms(e, psum=psum, qc=qc, qt=qt, kc=kc):
                        return e.matmul(
                            psum, qh[:, qc * 512 + qt * 128:qc * 512 + (qt + 1) * 128],
                            kh[:, kc * 512:(kc + 1) * 512], start=True, stop=True)
                    P.op("tensor", mms, waits=ww, inc=sPE)
                    pe_v = sPE.n
                    acol = qt * 4096 + kc * 512
                    slot = rs[:, (qt * 16 + kc):(qt * 16 + kc) + 1]
                    if kc == qc:
                        P.op("vector", lambda e, psum=psum, qt=qt: e.tensor_add(
                            scd, psum, msk_s[:, qt * 512:(qt + 1) * 512]),
                            waits=[(sPE, pe_v), (sDM, xo_dm), (sAC, sAC.n)], inc=sDV)
                        dv_v = sDV.n
                        P.op("scalar", lambda e, acol=acol, slot=slot: e.activation(
                            attn[:, acol:acol + 512], scd, AF.Exp, scale=ISQ,
                            accum_out=slot),
                            waits=[(sDV, dv_v), (sPE2, exp_attn_gate), (sDV2, rs_gate)],
                            inc=sAC)
                        sc_gate[bk] = (sc_gate[bk][0], dv_v)
                    else:
                        P.op("scalar", lambda e, psum=psum, acol=acol, slot=slot: e.activation(
                            attn[:, acol:acol + 512], psum, AF.Exp, scale=ISQ,
                            accum_out=slot),
                            waits=[(sPE, pe_v), (sPE2, exp_attn_gate), (sDV2, rs_gate)],
                            inc=sAC)
                        sc_gate[bk] = (sAC.n, sc_gate[bk][1])
                    exp_done[(qt, kc)] = sAC.n
            for qt in range(NQT):
                P.op("vector", lambda e, qt=qt, nkc=nkc: e.reduce_sum(
                    rrec_s[:, qt:qt + 1], rs[:, qt * 16:qt * 16 + nkc], axis=AX.X),
                    waits=[(sAC, exp_done[(qt, nkc - 1)]), (sDV2, sDV2.n)], inc=sDV2)
            rs_gate = sDV2.n
            P.op("vector", lambda e: e.reciprocal(rrec_s[:, 4:8], rrec_s[:, 0:4]),
                 waits=[(sDV2, sDV2.n)], inc=sDV2)
            rec_v = sDV2.n
            for qt in range(NQT):
                P.op("tensor", lambda e, qt=qt: e.transpose(
                    p7[0:1, qt * 128:(qt + 1) * 128], rrec_s[:, 4 + qt:5 + qt],
                    identf[:, :]),
                    waits=[(sDV2, rec_v), (sDV3, sDV3.n)], inc=sPE2)
            P.op("vector", lambda e: e.tensor_copy(rT_s[:, :], p7[0:1, 0:512]),
                 waits=[(sPE2, sPE2.n)], inc=sDV3)
            P.op("tensor", lambda e: e.matmul(p7, ones1f[:, :], rT_s[:, :],
                                              start=True, stop=True),
                 waits=[(sDV3, sDV3.n)], inc=sPE2)
            P.op("vector", lambda e: e.tensor_copy(recb, p7),
                 waits=[(sPE2, sPE2.n)], inc=sDV3)
            recb_v = sDV3.n
            tp_gate = {0: sDV.n, 1: sDV.n}
            atb_gate = [sPE.n] * 4
            for kc in range(nkc):
                for skb in range(4):
                    tb = skb % 2
                    tp = t0 if tb == 0 else t1
                    for qt in range(NQT):
                        acol = qt * 4096 + kc * 512 + skb * 128
                        ww = [(sAC, exp_done[(qt, kc)])]
                        if qt == 0:
                            ww.append((sDV, tp_gate[tb]))

                        def mtr(e, tp=tp, qt=qt, acol=acol):
                            return e.transpose(tp[:, qt * 128:(qt + 1) * 128],
                                               attn[:, acol:acol + 128], identb[:, :])
                        P.op("tensor", mtr, waits=ww,
                             inc=sPE2 if qt == NQT - 1 else None)
                    tr_v = sPE2.n
                    P.op("vector", lambda e, tp=tp, skb=skb: e.tensor_copy(
                        atb[:, skb * 512:(skb + 1) * 512], tp),
                        waits=[(sPE2, tr_v), (sPE, atb_gate[skb])], inc=sDV)
                    ev_v = sDV.n
                    tp_gate[tb] = ev_v

                    def mc(e, kc=kc, skb=skb,
                           first=(kc == 0 and skb == 0),
                           last=(kc == nkc - 1 and skb == 3)):
                        return e.matmul(p2,
                                        vh[:, (kc * 4 + skb) * 128:(kc * 4 + skb + 1) * 128],
                                        atb[:, skb * 512:(skb + 1) * 512],
                                        start=first, stop=last)
                    ww = [(sDV, ev_v)]
                    if kc == 0 and skb == 0:
                        ww.append((sDV, ctx_evac_dv))
                    P.op("tensor", mc, waits=ww, inc=sPE)
                    atb_gate[skb] = sPE.n
            exp_attn_gate = sPE2.n
            P.op("vector", lambda e, hl=hl, qc=qc: e.tensor_mul(
                cst[:, hl * 4096 + qc * 512:hl * 4096 + (qc + 1) * 512], p2, recb),
                waits=[(sPE, sPE.n), (sDV3, recb_v)], inc=sDV)
            ctx_evac_dv = sDV.n
        head_pe_end = sPE.n

    if "ctx" in taps:
        dma("sync", tap_out["tap_ctx"], cst, sOUT, waits=[(sDV, ctx_evac_dv)])

    if stop_after == "B":
        P.wait("sync", (sOUT, sOUT.n))
        return _finish(nc, P, stack, tap_out)

    dma("gpsimd", woP_s.rearrange("p (dc o) -> p dc o", dc=NH),
        woP.rearrange("(dc p) o -> p dc o", p=128), sGD,
        waits=[(sAC, sAC.n), (sPE2, sPE2.n), (sPE, head_pe_end)])
    wop_gd = sGD.n
    cst_ready = ctx_evac_dv
    wo_gate = {0: sc_gate[0][0], 1: sc_gate[1][0]}
    wo_dv_gate = {0: 0, 1: 0}
    wostg_dm = [0, 0]
    rs_cc = []
    for oc in range(4):
        for st in range(32):
            bk = st % 2
            psum = p0 if bk == 0 else p1
            for hl in range(NH):
                ww = [(sGD, wop_gd), (sDV, cst_ready)]
                if hl == 0:
                    ww.append((sDV, wo_dv_gate[bk]))

                def mw(e, psum=psum, hl=hl, st=st, oc=oc):
                    return e.matmul(
                        psum, cst[:, hl * 4096 + st * 128:hl * 4096 + (st + 1) * 128],
                        woP_s[:, hl * 2048 + oc * 512:hl * 2048 + (oc + 1) * 512],
                        start=(hl == 0), stop=(hl == NH - 1))
                P.op("tensor", mw, waits=ww, inc=sPE if hl == NH - 1 else None)
            pe_v = sPE.n
            sg = st % 2
            P.op("vector", lambda e, psum=psum, sg=sg: e.tensor_copy(wostg[sg], psum),
                 waits=[(sPE, pe_v), (sWS[sg], wostg_dm[sg])], inc=sDV)
            wo_dv_gate[bk] = sDV.n
            dma("sync", rsin[oc][st * 128:(st + 1) * 128, :], wostg[sg], sWS[sg],
                waits=[(sDV, sDV.n)])
            wostg_dm[sg] = sWS[sg].n
        P.op("gpsimd", lambda e, oc=oc: e.collective_compute(
            "ReduceScatter", ALU.add, replica_groups=[[0, 1, 2, 3], [4, 5, 6, 7]],
            ins=[rsin[oc]], outs=[rsout[oc]]),
            waits=[(sWS[0], sWS[0].n), (sWS[1], sWS[1].n)], inc=sCC)
        rs_cc.append(sCC.n)
    wo_pe_end = sPE.n

    if "rs" in taps:
        for oc in range(4):
            dma("sync", tap_out["tap_rs"][oc], rsout[oc], sOUT, waits=[(sCC, rs_cc[oc])])

    for oc in range(4):
        dma("gpsimd", acc.rearrange("p (st d) -> p st d", st=8)[:, :, oc * 512:(oc + 1) * 512],
            rsout[oc].rearrange("(st p) d -> p st d", p=128), sGD,
            waits=[(sCC, rs_cc[oc]), (sDM, xo_dm)], accum_op=ALU.add)
    acc_gd = sGD.n

    if stop_after == "wo":
        P.wait("sync", (sOUT, sOUT.n))
        return _finish(nc, P, stack, tap_out)

    tap_gate = [0]
    stg_pe_gate = [0]
    tp_c_gate = {0: sDV3.n, 1: sDV3.n}
    stg_gp_gate = [0]

    def emit_ln(src_gates, write_hbf, out_to=None):
        for st in range(8):
            a_sl = acc[:, st * D:(st + 1) * D]
            mu = tiny[:, st * 8 + 0:st * 8 + 1]
            ng = tiny[:, st * 8 + 1:st * 8 + 2]
            ms = tiny[:, st * 8 + 2:st * 8 + 3]
            vv = tiny[:, st * 8 + 3:st * 8 + 4]
            sq = tiny[:, st * 8 + 4:st * 8 + 5]
            sd = tiny[:, st * 8 + 5:st * 8 + 6]
            rho = tiny[:, st * 8 + 6:st * 8 + 7]
            P.op("scalar", lambda e, a_sl=a_sl, sq=sq: e.activation(
                sqs, a_sl, AF.Square, accum_out=sq),
                waits=list(src_gates) + [(sAC2, sAC2.n)], inc=sAC2)
            sq_v = sAC2.n
            P.op("vector", lambda e, a_sl=a_sl, mu=mu: e.reduce_sum(mu, a_sl, axis=AX.X),
                 waits=list(src_gates) + [(sDV2, sDV2.n)], inc=sDV2)
            P.op("vector", lambda e, ng=ng, mu=mu: e.tensor_scalar_mul(ng, mu, -1.0 / D),
                 waits=[(sDV2, sDV2.n)], inc=sDV2)
            ng_v = sDV2.n
            P.op("gpsimd", lambda e, ms=ms, ng=ng: e.tensor_mul(ms, ng, ng),
                 waits=[(sDV2, ng_v), (sGP, sGP.n)], inc=sGP)
            ms_v = sGP.n
            P.op("vector", lambda e, vv=vv, sq=sq: e.tensor_scalar_mul(vv, sq, 1.0 / D),
                 waits=[(sAC2, sq_v), (sDV2, sDV2.n)], inc=sDV2)
            P.op("vector", lambda e, vv=vv, ms=ms: e.scalar_tensor_tensor(
                vv, vv, EPS, ms, op0=ALU.add, op1=ALU.subtract),
                 waits=[(sGP, ms_v), (sDV2, sDV2.n)], inc=sDV2)
            vv_v = sDV2.n
            P.op("scalar", lambda e, sd=sd, vv=vv: e.activation(sd, vv, AF.Sqrt),
                 waits=[(sDV2, vv_v), (sAC2, sAC2.n)], inc=sAC2)
            sd_v = sAC2.n
            P.op("vector", lambda e, rho=rho, sd=sd: e.reciprocal(rho, sd),
                 waits=[(sAC2, sd_v), (sDV2, sDV2.n)], inc=sDV2)
            P.op("vector", lambda e, a_sl=a_sl, ng=ng, rho=rho: e.tensor_scalar(
                stg, a_sl, ng, rho, op0=ALU.add, op1=ALU.mult),
                waits=[(sDV2, sDV2.n), (sOUT, tap_gate[0]),
                       (sPE2, stg_pe_gate[0]), (sGP, stg_gp_gate[0])], inc=sDV2)
            nrm_v = sDV2.n
            if write_hbf:
                for dc in range(NDC):
                    bk = dc % 2
                    pp = p2 if bk == 0 else p7
                    ww = [(sDV2, nrm_v), (sDV3, tp_c_gate[bk]), (sDV, ctx_evac_dv)]

                    def mt(e, pp=pp, dc=dc):
                        return e.transpose(pp[:, 0:128], stg[:, dc * 128:(dc + 1) * 128],
                                           identf[:, :])
                    P.op("tensor", mt, waits=ww, inc=sPE2)
                    tr_v = sPE2.n
                    P.op("vector", lambda e, pp=pp, dc=dc, st=st: e.tensor_copy(
                        h_bf[:, dc * 1024 + st * 128:dc * 1024 + (st + 1) * 128],
                        pp[:, 0:128]),
                        waits=[(sPE2, tr_v), (sDV3, sDV3.n)], inc=sDV3)
                    tp_c_gate[bk] = sDV3.n
                stg_pe_gate[0] = sPE2.n
                P.op("gpsimd", lambda e, a_sl=a_sl: e.tensor_copy(a_sl, stg),
                     waits=[(sDV2, nrm_v), (sGP, sGP.n), (sDV3, sDV3.n)], inc=sGP)
                stg_gp_gate[0] = sGP.n
            else:
                dma("sync", out_to[st * 128:(st + 1) * 128, :], stg, sOUT,
                    waits=[(sDV2, nrm_v)])
                tap_gate[0] = sOUT.n
                stg_gp_gate[0] = 0
                stg_pe_gate[0] = 0
                P.wait("vector", (sOUT, sOUT.n))

    emit_ln([(sGD, acc_gd)], write_hbf=True)
    h_ready_gp = sGP.n
    h_bf_ready = sDV3.n

    if "h" in taps:
        dma("sync", tap_out["tap_h"].rearrange("(st p) d -> p st d", p=128),
            acc.rearrange("p (st d) -> p st d", st=8), sOUT,
            waits=[(sGP, h_ready_gp)])
        tap_gate[0] = sOUT.n

    if stop_after == "ln1":
        P.wait("sync", (sOUT, sOUT.n))
        return _finish(nc, P, stack, tap_out)

    rt_gate = 0
    for st in range(8):
        for dc in range(NDC):
            ww = [(sDV3, h_bf_ready), (sGD, prolog_gd)]
            if dc == 0:
                ww.append((sDV2, rt_gate))

            def ml(e, dc=dc, st=st):
                return e.matmul(p2[:, 0:8],
                                h_bf[:, dc * 1024 + st * 128:dc * 1024 + (st + 1) * 128],
                                rw_s[:, dc * 8:(dc + 1) * 8],
                                start=(dc == 0), stop=(dc == NDC - 1))
            P.op("tensor", ml, waits=ww, inc=sPE if dc == NDC - 1 else None)
        pe_v = sPE.n
        seq = [
            ("vector", lambda e: e.tensor_copy(lg, p2[:, 0:8]), [(sPE, pe_v)]),
            ("vector", lambda e: e.reduce_max(m1, lg, axis=AX.X), None),
            ("vector", lambda e: e.tensor_scalar(mk1, lg, m1, None, op0=ALU.is_equal), None),
            ("vector", lambda e: e.scalar_tensor_tensor(l2, mk1, -1e30, lg,
                                                        op0=ALU.mult, op1=ALU.add), None),
            ("vector", lambda e: e.reduce_max(m2, l2, axis=AX.X), None),
            ("vector", lambda e: e.tensor_sub(dd, m1, m2), None),
            ("scalar", lambda e: e.activation(wA, dd, AF.Sigmoid), None),
            ("vector", lambda e: e.tensor_scalar(wB, wA, -1.0, 1.0,
                                                 op0=ALU.mult, op1=ALU.add), None),
            ("vector", lambda e: e.tensor_scalar(mk2, l2, m2, None, op0=ALU.is_equal), None),
            ("vector", lambda e, st=st: e.tensor_scalar(
                rta[:, st * 8:(st + 1) * 8], mk1, wA, None, op0=ALU.mult), None),
            ("vector", lambda e, st=st: e.scalar_tensor_tensor(
                rta[:, st * 8:(st + 1) * 8], mk2, wB, rta[:, st * 8:(st + 1) * 8],
                op0=ALU.mult, op1=ALU.add), None),
        ]
        for (eng, fn, extra) in seq:
            base = [(sDV2, sDV2.n), (sAC2, sAC2.n)]
            if extra:
                base += extra
            P.op(eng, fn, waits=base, inc=sDV2 if eng == "vector" else sAC2)
        rt_gate = sDV2.n
    rta_v = sDV2.n

    if "r" in taps:
        dma("sync", tap_out["tap_r"], rta, sOUT, waits=[(sDV2, rta_v)])
        tap_gate[0] = sOUT.n

    if stop_after == "router":
        P.wait("sync", (sOUT, sOUT.n))
        return _finish(nc, P, stack, tap_out)

    w12_pe_gate = {0: 0, 1: 0}
    w12_dm_gate = {0: 0, 1: 0}
    w3_pe_gate = {0: wo_pe_end, 1: wo_pe_end}
    w3_dm_gate = {0: 0, 1: 0}
    w3_acc_gate = {0: wo_dv_gate[0], 1: wo_dv_gate[1]}
    sil_gate = 0
    zmul_gate = 0
    z_pe_gate = [wo_pe_end] * 16
    ffn_pe_end = wo_pe_end
    gate_p0_dv = wo_dv_gate[0]
    val_p1_dv = wo_dv_gate[1]
    for (gidx, grp) in enumerate(EGRPS):
        z_slot_dv = {}
        for (le, e_) in enumerate(grp):
            for half in range(2):
                for j2 in range(4):
                    st = half * 4 + j2

                    def mtr(e, st=st, e_=e_, j2=j2):
                        return e.transpose(p2[0:1, j2 * 128:(j2 + 1) * 128],
                                           rta[:, st * 8 + e_:st * 8 + e_ + 1],
                                           identf[:, :])
                    P.op("tensor", mtr,
                         waits=[(sDV2, rta_v), (sDV3, sDV3.n)], inc=sPE2)
                P.op("vector", lambda e, half=half: e.tensor_copy(
                    rT2[:, half * 512:(half + 1) * 512], p2[0:1, 0:512]),
                    waits=[(sPE2, sPE2.n)], inc=sDV3)
            for half in range(2):
                P.op("tensor", lambda e, half=half: e.matmul(
                    p2, ones1f[:, :], rT2[:, half * 512:(half + 1) * 512],
                    start=True, stop=True),
                    waits=[(sDV3, sDV3.n), (sGP, sGP.n)], inc=sPE2)
                P.op("vector", lambda e, half=half: e.tensor_copy(
                    rb_f[:, half * 512:(half + 1) * 512], p2),
                    waits=[(sPE2, sPE2.n)], inc=sDV3)
            rb_v = sDV3.n
            for fb in range(NFB):
                bidx = fb % 2
                ww = [(sGD2, conv_gd2), (sPE, w12_pe_gate[bidx]), (sPE, wo_pe_end)]
                dma("sync", w1buf[bidx],
                    w1b[e_, fb].rearrange("p dc f -> p (dc f)"), sW1[bidx], waits=ww)
                dma("sync", w2buf[bidx],
                    w2b[e_, fb].rearrange("p dc f -> p (dc f)"), sW1[bidx])
                wld = sW1[bidx].n
                for sh in range(2):
                    for dc in range(NDC):
                        ww = [(sW1[bidx], wld), (sDV3, h_bf_ready)]
                        if dc == 0:
                            ww += [(sAC, sil_gate), (sDV, gate_p0_dv)]

                        def mg(e, dc=dc, sh=sh, bidx=bidx):
                            return e.matmul(
                                p0, w1buf[bidx][:, dc * 128:(dc + 1) * 128],
                                h_bf[:, dc * 1024 + sh * 512:dc * 1024 + (sh + 1) * 512],
                                start=(dc == 0), stop=(dc == NDC - 1))
                        P.op("tensor", mg, waits=ww, inc=sPE if dc == NDC - 1 else None)
                    g_pe = sPE.n
                    for dc in range(NDC):
                        ww = [(sW1[bidx], wld)]
                        if dc == 0:
                            ww.append((sDV, val_p1_dv))

                        def mv(e, dc=dc, sh=sh, bidx=bidx):
                            return e.matmul(
                                p1, w2buf[bidx][:, dc * 128:(dc + 1) * 128],
                                h_bf[:, dc * 1024 + sh * 512:dc * 1024 + (sh + 1) * 512],
                                start=(dc == 0), stop=(dc == NDC - 1))
                        P.op("tensor", mv, waits=ww, inc=sPE if dc == NDC - 1 else None)
                    v_pe = sPE.n
                    P.op("scalar", lambda e: e.activation(silA, p0, AF.Silu),
                         waits=[(sPE, g_pe), (sGP, sGP.n)], inc=sAC)
                    sil_gate = sAC.n
                    gate_p0_dv = 0
                    P.op("gpsimd", lambda e, sh=sh: e.tensor_mul(
                        silB, silA, rb_f[:, sh * 512:(sh + 1) * 512]),
                        waits=[(sAC, sil_gate), (sDV3, rb_v), (sDV, zmul_gate)], inc=sGP)
                    gp_v = sGP.n
                    zi = le * 8 + fb
                    zsl = zbuf[:, zi * 1024 + sh * 512:zi * 1024 + (sh + 1) * 512]
                    P.op("vector", lambda e, zsl=zsl: e.tensor_mul(zsl, p1, silB),
                         waits=[(sGP, gp_v), (sPE, v_pe), (sPE, z_pe_gate[zi])],
                         inc=sDV)
                    zmul_gate = sDV.n
                    val_p1_dv = sDV.n
                    z_slot_dv[zi] = sDV.n
                w12_pe_gate[bidx] = sPE.n
        z_all = sDV.n
        for dchunk in range(4):
            for (le, e_) in enumerate(grp):
                bidx = (dchunk * 2 + le) % 2
                dma("sync", w3buf[bidx].rearrange("p (fc d) -> p fc d", fc=NFC),
                    w3b[e_].rearrange("(fc p) d -> p fc d", p=128)[:, :, dchunk * 512:(dchunk + 1) * 512],
                    sW3[bidx], waits=[(sGD2, conv_gd2), (sPE, w3_pe_gate[bidx]),
                                      (sPE, wo_pe_end)])
            w3ld = [(sW3[0], sW3[0].n), (sW3[1], sW3[1].n)]
            for st in range(8):
                bk = st % 2
                psum = p0 if bk == 0 else p1
                nmm = 0
                for (le, e_) in enumerate(grp):
                    bidx = (dchunk * 2 + le) % 2
                    for fc in range(NFC):
                        ww = list(w3ld) + [(sDV, z_all)]
                        if nmm == 0:
                            ww.append((sDV, w3_acc_gate[bk]))
                        zi = le * 8 + fc

                        def m3(e, psum=psum, zi=zi, fc=fc, st=st, bidx=bidx, nmm=nmm):
                            return e.matmul(
                                psum,
                                zbuf[:, zi * 1024 + st * 128:zi * 1024 + (st + 1) * 128],
                                w3buf[bidx][:, fc * 512:(fc + 1) * 512],
                                start=(nmm == 0), stop=(nmm == 15))
                        P.op("tensor", m3, waits=ww, inc=sPE if nmm == 15 else None)
                        nmm += 1
                pe_v = sPE.n
                a_sl = acc[:, st * D + dchunk * 512:st * D + (dchunk + 1) * 512]
                P.op("vector", lambda e, a_sl=a_sl, psum=psum: e.tensor_add(
                    a_sl, psum, a_sl),
                    waits=[(sPE, pe_v), (sGP, h_ready_gp), (sOUT, tap_gate[0]),
                           (sDV, sDV.n)], inc=sDV)
                w3_acc_gate[bk] = sDV.n
            w3_pe_gate[0] = sPE.n
            w3_pe_gate[1] = sPE.n
        for zi in range(16):
            z_pe_gate[zi] = sPE.n
        gate_p0_dv = w3_acc_gate[0]
        val_p1_dv = w3_acc_gate[1]
        ffn_pe_end = sPE.n
    ffn_done_dv = sDV.n

    if "ffn" in taps:
        dma("sync", tap_out["tap_ffn"].rearrange("(st p) d -> p st d", p=128),
            acc.rearrange("p (st d) -> p st d", st=8), sOUT, waits=[(sDV, ffn_done_dv)])
        tap_gate[0] = sOUT.n

    if stop_after == "ffn":
        P.wait("sync", (sOUT, sOUT.n))
        return _finish(nc, P, stack, tap_out)

    stg_pe_gate[0] = 0
    stg_gp_gate[0] = 0
    emit_ln([(sDV, ffn_done_dv)], write_hbf=False, out_to=outT)

    P.wait("sync", (sOUT, sOUT.n))

    return _finish(nc, P, stack, tap_out)


def _finish(nc, P, stack, tap_out):
    with nc.Block() as block:
        P.replay(block)
    stack.close()
    return nc, list(tap_out.keys())


_HPERM = np.concatenate([np.arange(0, 128, 2), np.arange(1, 128, 2)])


def _host_inputs(L0_attn_x, Wq, Wk, Wv, Wo, router_w, w1, w2, w3):
    x = np.asarray(L0_attn_x, dtype=np.float32)
    Wq = np.asarray(Wq, np.float32); Wk = np.asarray(Wk, np.float32)
    Wv = np.asarray(Wv, np.float32); Wo = np.asarray(Wo, np.float32)
    router_w = np.asarray(router_w, np.float32)
    w1 = np.asarray(w1, np.float32); w2 = np.asarray(w2, np.float32)
    w3 = np.asarray(w3, np.float32)

    xT = [np.ascontiguousarray(x[b].T) for b in range(B)]
    j = np.arange(64, dtype=np.float64)
    inv = np.power(10000.0, -2.0 * j / DK)
    pos = np.arange(S, dtype=np.float64)
    ang = pos[None, :] * inv[:, None]
    cosv = np.cos(ang); sinv = np.sin(ang)
    c2 = np.concatenate([cosv, cosv], 0).astype(np.float32)
    s2 = np.concatenate([-sinv, sinv], 0).astype(np.float32)
    csh = np.ascontiguousarray(np.stack([c2, s2], 0))
    ii = np.arange(128)[:, None]
    jj = np.arange(512)[None, :]
    mskh = np.ascontiguousarray(np.stack(
        [np.where(jj <= qt * 128 + ii, 0.0, -1e30).astype(np.float32)
         for qt in range(NQT)], 0))
    idin = np.eye(128, dtype=np.float32)
    w1th = np.ascontiguousarray(
        w1.reshape(NE, NDC, 128, NFB, 128).transpose(0, 3, 2, 1, 4))
    w2th = np.ascontiguousarray(
        w2.reshape(NE, NDC, 128, NFB, 128).transpose(0, 3, 2, 1, 4))

    in_maps = []
    for c in range(8):
        b, r = c // 4, c % 4
        heads = [4 * r + i for i in range(NH)]
        qrows = np.concatenate([h * 128 + _HPERM for h in heads])
        vrows = np.concatenate([h * 128 + np.arange(128) for h in heads])
        in_maps.append({
            "xT": xT[b],
            "xo": np.ascontiguousarray(x[b][r * NT:(r + 1) * NT]),
            "wqT": np.ascontiguousarray(Wq[qrows].T),
            "wkT": np.ascontiguousarray(Wk[qrows].T),
            "wvT": np.ascontiguousarray(Wv[vrows].T),
            "woP": np.ascontiguousarray(Wo[:, vrows].T),
            "rwT": np.ascontiguousarray(router_w.T),
            "w1t": w1th, "w2t": w2th, "w3n": w3,
            "cs": csh, "msk": mskh, "idin": idin,
        })
    return in_maps


def kernel(L0_attn_x, L0_attn_Wq, L0_attn_Wk, L0_attn_Wv, L0_attn_Wo,
           L0_ffn_router_w, L0_ffn_w1, L0_ffn_w2, L0_ffn_w3):
    in_maps = _host_inputs(L0_attn_x, L0_attn_Wq, L0_attn_Wk, L0_attn_Wv,
                           L0_attn_Wo, L0_ffn_router_w, L0_ffn_w1, L0_ffn_w2,
                           L0_ffn_w3)
    nc, _ = build_program()
    res = run_bass_kernel_spmd(nc, in_maps, core_ids=list(range(8)))
    out = np.empty((B, S, D), np.float32)
    for c in range(8):
        b, r = c // 4, c % 4
        out[b, r * NT:(r + 1) * NT, :] = res.results[c]["outT"]
    return out


# version 02
# speedup vs baseline: 1.0655x; 1.0655x over previous
import math
import numpy as np
import ml_dtypes
import concourse.bass as bass
import concourse.mybir as mybir
from concourse.bass_utils import run_bass_kernel_spmd

F32 = mybir.dt.float32
BF16 = mybir.dt.bfloat16
AF = mybir.ActivationFunctionType
ALU = mybir.AluOpType
AX = mybir.AxisListType

B, S, D, H, DK = 2, 4096, 2048, 16, 128
NH = 4
NT = 1024
NE, DF = 8, 1024
NDC = D // 128
NSC = S // 512
NQC = S // 512
NQT = 4
NFC = DF // 128
NFB = 8
EGRPS = [(0, 1), (2, 3), (4, 5), (6, 7)]
ISQ = 1.0 / math.sqrt(DK)
EPS = 1e-5
ABLATE = set()
ENGS = ("sync", "tensor", "vector", "scalar", "gpsimd")


class Sem:
    def __init__(self, h):
        self.h = h
        self.n = 0


class Prog:
    def __init__(self):
        self.items = {e: [] for e in ENGS}
        self.last_wait = {}

    def op(self, eng_name, build_fn, waits=(), inc=None, inc_by=1):
        wl = []
        for (s, v) in waits:
            if v <= 0:
                continue
            key = (eng_name, id(s))
            if self.last_wait.get(key, -1) < v:
                self.last_wait[key] = v
                wl.append((s, v))
        if inc is not None:
            inc.n += inc_by
        wl = tuple(wl)

        def f(eng, build_fn=build_fn, wl=wl, inc=inc, inc_by=inc_by):
            for (s, v) in wl:
                eng.wait_ge(s.h, v)
            ins = build_fn(eng)
            if inc is not None and ins is not None:
                ins.then_inc(inc.h, inc_by)
        self.items[eng_name].append(f)
        return inc.n if inc is not None else 0

    def wait(self, eng_name, *waits):
        self.op(eng_name, lambda e: None, waits=waits)

    def replay(self, block):
        for name in ENGS:
            ops = self.items[name]
            if not ops:
                continue

            def body(eng, ops=ops):
                for f in ops:
                    f(eng)
            getattr(block, name)(body)


def build_program(taps=(), stop_after=None):
    taps = set(taps)
    nc = bass.Bass("TRN2", target_bir_lowering=False, debug=False, num_devices=8)

    xT = nc.dram_tensor("xT", [D, S], F32, kind="ExternalInput").ap()
    xo = nc.dram_tensor("xo", [NT, D], F32, kind="ExternalInput").ap()
    wqT = nc.dram_tensor("wqT", [D, NH * DK], F32, kind="ExternalInput").ap()
    wkT = nc.dram_tensor("wkT", [D, NH * DK], F32, kind="ExternalInput").ap()
    wvT = nc.dram_tensor("wvT", [D, NH * DK], F32, kind="ExternalInput").ap()
    woP = nc.dram_tensor("woP", [NH * DK, D], F32, kind="ExternalInput").ap()
    rwT = nc.dram_tensor("rwT", [D, NE], F32, kind="ExternalInput").ap()
    w1t = nc.dram_tensor("w1t", [NE, NFB, 128, NDC, 128], F32, kind="ExternalInput").ap()
    w2t = nc.dram_tensor("w2t", [NE, NFB, 128, NDC, 128], F32, kind="ExternalInput").ap()
    w3n = nc.dram_tensor("w3n", [NE, DF, D], F32, kind="ExternalInput").ap()
    cs = nc.dram_tensor("cs", [2, 128, S], F32, kind="ExternalInput").ap()
    msk = nc.dram_tensor("msk", [NQT, 128, 512], F32, kind="ExternalInput").ap()
    idin = nc.dram_tensor("idin", [128, 128], F32, kind="ExternalInput").ap()
    outT = nc.dram_tensor("outT", [NT, D], F32, kind="ExternalOutput").ap()

    qb = nc.dram_tensor("qb", [NH, 128, S], BF16, kind="Internal").ap()
    kb = nc.dram_tensor("kb", [NH, 128, S], BF16, kind="Internal").ap()
    vb = nc.dram_tensor("vb", [NH, S, 128], BF16, kind="Internal").ap()
    rsin = [nc.dram_tensor(f"rsin{i}", [S, 512], F32, kind="Internal").ap() for i in range(4)]
    rsout = [nc.dram_tensor(f"rsout{i}", [NT, 512], F32, kind="Internal").ap() for i in range(4)]

    tap_out = {}
    if "qkv" in taps:
        tap_out["tap_q"] = nc.dram_tensor("tap_q", [NH, 128, S], BF16, kind="ExternalOutput").ap()
        tap_out["tap_k"] = nc.dram_tensor("tap_k", [NH, 128, S], BF16, kind="ExternalOutput").ap()
        tap_out["tap_v"] = nc.dram_tensor("tap_v", [NH, S, 128], BF16, kind="ExternalOutput").ap()
    if "ctx" in taps:
        tap_out["tap_ctx"] = nc.dram_tensor("tap_ctx", [128, NH * S], BF16, kind="ExternalOutput").ap()
    if "rs" in taps:
        tap_out["tap_rs"] = nc.dram_tensor("tap_rs", [4, NT, 512], F32, kind="ExternalOutput").ap()
    if "h" in taps:
        tap_out["tap_h"] = nc.dram_tensor("tap_h", [NT, D], F32, kind="ExternalOutput").ap()
    if "r" in taps:
        tap_out["tap_r"] = nc.dram_tensor("tap_r", [128, 64], F32, kind="ExternalOutput").ap()
    if "ffn" in taps:
        tap_out["tap_ffn"] = nc.dram_tensor("tap_ffn", [NT, D], F32, kind="ExternalOutput").ap()

    ab = nc.alloc_sbuf_tensor("arena_bf", [128, 49152], BF16).ap()
    af = nc.alloc_sbuf_tensor("arena_f32", [128, 11264], F32).ap()
    acc = nc.alloc_sbuf_tensor("acc", [128, 8 * D], F32).ap()
    identb = nc.alloc_sbuf_tensor("identb", [128, 128], BF16).ap()
    identf = nc.alloc_sbuf_tensor("identf", [128, 128], F32).ap()
    ones1f = nc.alloc_sbuf_tensor("ones1f", [1, 128], F32).ap()
    rw_s = nc.alloc_sbuf_tensor("rw_s", [128, NDC * NE], BF16).ap()
    rrec_s = nc.alloc_sbuf_tensor("rrec_s", [128, 8], F32).ap()
    rT_s = nc.alloc_sbuf_tensor("rT_s", [1, 512], F32).ap()

    wq_s = ab[:, 0:8192]
    wk_s = ab[:, 8192:16384]
    wv_s = ab[:, 16384:24576]
    xbf = [ab[:, 24576 + i * 8192:24576 + (i + 1) * 8192] for i in (0, 1)]
    qstage = ab[:, 40960:43008]
    kstage = ab[:, 43008:45056]
    vstage = ab[:, 45056:47104]
    cst = ab[:, 0:16384]
    qh = ab[:, 16384:20480]
    kh = ab[:, 20480:24576]
    vh = ab[:, 24576:28672]
    attn = ab[:, 28672:45056]
    atb = ab[:, 45056:47104]
    woP_s = ab[:, 28672:36864]
    zbuf = ab[:, 0:16384]
    h_bf = ab[:, 16384:32768]
    w1buf = [ab[:, 32768 + i * 2048:32768 + (i + 1) * 2048] for i in (0, 1)]
    w2buf = [ab[:, 36864 + i * 2048:36864 + (i + 1) * 2048] for i in (0, 1)]
    w3buf = [ab[:, 40960 + i * 4096:40960 + (i + 1) * 4096] for i in (0, 1)]

    ropB = af[:, 0:512]
    c2 = af[:, 2048:6144]
    s2 = af[:, 6144:10240]
    swp = af[:, 10240:10752]
    ropA = af[:, 10752:11264]
    msk_s = af[:, 0:2048]
    scd = af[:, 10240:10752]
    rs = af[:, 10752:10816]
    recb = af[:, 2048:2560]
    wostg = [af[:, 6144 + i * 512:6144 + (i + 1) * 512] for i in (0, 1)]
    rT2 = af[0:1, 0:1024]
    tiny = af[:, 1024:1088]
    rta = af[:, 1088:1152]
    lg = af[:, 1152:1160]
    m1 = af[:, 1160:1161]
    l2 = af[:, 1168:1176]
    m2 = af[:, 1176:1177]
    dd = af[:, 1177:1178]
    wA = af[:, 1178:1179]
    wB = af[:, 1179:1180]
    mk1 = af[:, 1184:1192]
    mk2 = af[:, 1192:1200]
    sqs = af[:, 2048:4096]
    stg = af[:, 4096:6144]
    silA = af[:, 6144:6656]
    silB = af[:, 6656:7168]
    rb_f = af[:, 7168:8192]

    p0 = nc.alloc_psum_tensor("p0", [128, 512], F32).ap()
    p1 = nc.alloc_psum_tensor("p1", [128, 512], F32).ap()
    p2 = nc.alloc_psum_tensor("p2", [128, 512], F32).ap()
    p7 = nc.alloc_psum_tensor("p7", [128, 512], F32).ap()
    t0 = nc.alloc_psum_tensor("t0", [128, 512], BF16).ap()
    t1 = nc.alloc_psum_tensor("t1", [128, 512], BF16).ap()

    import contextlib
    stack = contextlib.ExitStack()
    SN = ["sDM", "sGD", "sGD2", "sPE", "sPE2", "sDV", "sDV2", "sDV3",
          "sAC", "sAC2", "sGP", "sCC", "sOUT",
          "sXB0", "sXB1", "sSTQ", "sSTK", "sSTV", "sHLD",
          "sWS0", "sWS1", "sW10", "sW11", "sW30", "sW31"]
    sem = {n: Sem(stack.enter_context(nc.semaphore(n))) for n in SN}
    sDM, sGD, sGD2 = sem["sDM"], sem["sGD"], sem["sGD2"]
    sPE, sPE2 = sem["sPE"], sem["sPE2"]
    sDV, sDV2, sDV3 = sem["sDV"], sem["sDV2"], sem["sDV3"]
    sAC, sAC2, sGP = sem["sAC"], sem["sAC2"], sem["sGP"]
    sCC, sOUT = sem["sCC"], sem["sOUT"]
    sXB = [sem["sXB0"], sem["sXB1"]]
    sSTQ, sSTK, sSTV = sem["sSTQ"], sem["sSTK"], sem["sSTV"]
    sHLD = sem["sHLD"]
    sWS = [sem["sWS0"], sem["sWS1"]]
    sW1 = [sem["sW10"], sem["sW11"]]
    sW3 = [sem["sW30"], sem["sW31"]]

    P = Prog()

    def dma(eng, out, in_, s, waits=(), **kw):
        return P.op(eng, lambda e: e.dma_start(out=out, in_=in_, **kw),
                    waits=waits, inc=s, inc_by=16)

    dma("gpsimd", identb[:, :], idin, sGD)
    dma("gpsimd", rw_s.rearrange("p (dc e) -> p dc e", dc=NDC),
        rwT.rearrange("(dc p) e -> p dc e", p=128), sGD)
    dma("gpsimd", wq_s.rearrange("p (dc o) -> p dc o", dc=NDC),
        wqT.rearrange("(dc p) o -> p dc o", p=128), sGD)
    dma("gpsimd", wk_s.rearrange("p (dc o) -> p dc o", dc=NDC),
        wkT.rearrange("(dc p) o -> p dc o", p=128), sGD)
    dma("gpsimd", wv_s.rearrange("p (dc o) -> p dc o", dc=NDC),
        wvT.rearrange("(dc p) o -> p dc o", p=128), sGD)
    dma("sync", identf[:, :], idin, sDM)
    dma("sync", c2, cs[0], sDM)
    dma("sync", s2, cs[1], sDM)
    P.op("gpsimd", lambda e: e.memset(ones1f[:, :], 1.0), inc=sGP)
    prolog_gd, prolog_dm, prolog_gp = sGD.n, sDM.n, sGP.n

    if stop_after == "P":
        P.wait("sync", (sGD, sGD.n), (sDM, sDM.n), (sGD2, sGD2.n), (sGP, sGP.n))
        return _finish(nc, P, stack, tap_out)

    xld = []
    xfree = []
    bank_free = {0: (0, 0), 1: (0, 0)}
    stage_dm = {"q": (sSTQ, 0), "k": (sSTK, 0), "v": (sSTV, 0)}
    swp_gate = 0
    ropB_gate = 0
    for sc in range(NSC):
        bi = sc % 2
        w = [(sGD, prolog_gd), (sDM, prolog_dm)]
        if sc >= 2:
            w.append((sPE, xfree[sc - 2]))
        dma("gpsimd", xbf[bi].rearrange("p (dc c) -> p dc c", dc=NDC),
            xT.rearrange("(dc p) s -> p dc s", p=128)[:, :, sc * 512:(sc + 1) * 512],
            sXB[bi], waits=w)
        xld.append((sXB[bi], sXB[bi].n))
        col = sc * 512
        gi = 0
        for (stage, w_s, name, dst) in (() if "noqk" in ABLATE else ((qstage, wq_s, "q", qb), (kstage, wk_s, "k", kb))):
            fin = []
            for ot in range(NH):
                psum = p0 if (gi % 2 == 0) else p1
                bfree = bank_free[gi % 2]
                for dc in range(NDC):
                    ww = [xld[sc]]
                    if dc == 0:
                        ww += [(sAC, bfree[0]), (sDV, bfree[1])]

                    def mm(e, psum=psum, w_s=w_s, dc=dc, ot=ot, bi=bi):
                        return e.matmul(
                            psum, w_s[:, dc * 512 + ot * 128:dc * 512 + (ot + 1) * 128],
                            xbf[bi][:, dc * 512:(dc + 1) * 512],
                            start=(dc == 0), stop=(dc == NDC - 1))
                    P.op("tensor", mm, waits=ww, inc=sPE if dc == NDC - 1 else None)
                pe_v = sPE.n
                if "norope" in ABLATE:
                    sl = stage[:, ot * 512:(ot + 1) * 512]
                    P.op("vector", lambda e, sl=sl, psum=psum: e.tensor_copy(sl, psum),
                         waits=[(sPE, pe_v), stage_dm[name], (sDV, ropB_gate)], inc=sDV)
                    ropB_gate = sDV.n
                    fin.append(sDV.n)
                    bank_free[gi % 2] = (sAC.n, sDV.n)
                    gi += 1
                    continue
                P.op("scalar", lambda e, psum=psum: e.copy(swp[0:64, :], psum[64:128, :]),
                     waits=[(sPE, pe_v), (sGP, swp_gate)], inc=sAC)
                P.op("scalar", lambda e, psum=psum: e.copy(swp[64:128, :], psum[0:64, :]),
                     inc=sAC)
                ac_v = sAC.n
                P.op("vector", lambda e, psum=psum, col=col: e.tensor_mul(
                    ropA, psum, c2[:, col:col + 512]),
                    waits=[(sPE, pe_v), (sDM, prolog_dm), (sDV, ropB_gate),
                           (sAC, ac_v)], inc=sDV)
                dv_a = sDV.n
                if "r3" in ABLATE:
                    P.op("gpsimd", lambda e, col=col: e.tensor_mul(
                        ropB, c2[:, col:col + 512], s2[:, col:col + 512]),
                        waits=[(sAC, ac_v), (sDV, ropB_gate)], inc=sGP)
                else:
                    P.op("gpsimd", lambda e, col=col: e.tensor_mul(
                        ropB, swp, s2[:, col:col + 512]),
                        waits=[(sAC, ac_v), (sDV, ropB_gate)], inc=sGP)
                swp_gate = sGP.n
                sl = stage[:, ot * 512:(ot + 1) * 512]
                P.op("vector", lambda e, sl=sl: e.tensor_add(sl, ropA, ropB),
                     waits=[(sDV, dv_a), (sGP, sGP.n), stage_dm[name]], inc=sDV)
                ropB_gate = sDV.n
                fin.append(sDV.n)
                bank_free[gi % 2] = (sAC.n, sDV.n)
                gi += 1
            sch = sSTQ if name == "q" else sSTK
            if "nostage" not in ABLATE:
                dma("sync", dst.rearrange("h p s -> p h s")[:, :, sc * 512:(sc + 1) * 512],
                    stage.rearrange("p (h c) -> p h c", h=NH), sch, waits=[(sDV, fin[-1])])
                stage_dm[name] = (sch, sch.n)
        vfin = []
        for st in range(0 if "nov" in ABLATE else 4):
            psum = p0 if (gi % 2 == 0) else p1
            bfree = bank_free[gi % 2]
            for dc in range(NDC):
                ww = [xld[sc]]
                if dc == 0:
                    ww += [(sAC, bfree[0]), (sDV, bfree[1])]

                def mmv(e, psum=psum, dc=dc, st=st, bi=bi):
                    return e.matmul(
                        psum, xbf[bi][:, dc * 512 + st * 128:dc * 512 + (st + 1) * 128],
                        wv_s[:, dc * 512:(dc + 1) * 512],
                        start=(dc == 0), stop=(dc == NDC - 1))
                P.op("tensor", mmv, waits=ww, inc=sPE if dc == NDC - 1 else None)
            pe_v = sPE.n
            P.op("vector", lambda e, psum=psum, st=st: e.tensor_copy(
                vstage[:, st * 512:(st + 1) * 512], psum),
                waits=[(sPE, pe_v), stage_dm["v"]], inc=sDV)
            vfin.append(sDV.n)
            bank_free[gi % 2] = (sAC.n, sDV.n)
            gi += 1
        for hh in range(0 if ("nov" in ABLATE or "nostage" in ABLATE) else NH):
            dma("sync",
                vb[hh].rearrange("(st p) d -> p st d", p=128)[:, sc * 4:(sc + 1) * 4, :],
                vstage.rearrange("p (st h d) -> p st h d", st=4, h=NH)[:, :, hh, :],
                sSTV, waits=[(sDV, vfin[-1])])
        stage_dm["v"] = (sSTV, sSTV.n)
        xfree.append(sPE.n)
    phA_bar = [(sSTQ, sSTQ.n), (sSTK, sSTK.n), (sSTV, sSTV.n)]
    phA_pe = sPE.n

    if "qkv" in taps:
        dma("sync", tap_out["tap_q"].rearrange("h p s -> (h p) s"),
            qb.rearrange("h p s -> (h p) s"), sOUT, waits=phA_bar)
        dma("sync", tap_out["tap_k"].rearrange("h p s -> (h p) s"),
            kb.rearrange("h p s -> (h p) s"), sOUT, waits=phA_bar)
        dma("sync", tap_out["tap_v"].rearrange("h s d -> (h s) d"),
            vb.rearrange("h s d -> (h s) d"), sOUT, waits=phA_bar)

    if stop_after == "A":
        P.wait("sync", (sOUT, sOUT.n))
        return _finish(nc, P, stack, tap_out)

    dma("sync", msk_s.rearrange("p (q c) -> p q c", q=NQT),
        msk.rearrange("q p c -> p q c"), sDM,
        waits=[(sDV, sDV.n), (sGP, sGP.n)])
    mask_dm = sDM.n
    dma("sync", acc.rearrange("p (st d) -> p st d", st=8),
        xo.rearrange("(st p) d -> p st d", p=128), sDM)
    xo_dm = sDM.n

    head_pe_end = phA_pe
    ctx_evac_dv = 0
    exp_attn_gate = 0
    rs_gate = 0
    recb_v = 0
    for hl in range(NH):
        dma("sync", qh, qb[hl], sHLD,
            waits=phA_bar + [(sPE, head_pe_end)])
        dma("sync", kh, kb[hl], sHLD)
        dma("sync", vh.rearrange("p (st d) -> p st d", st=32),
            vb[hl].rearrange("(st p) d -> p st d", p=128), sHLD)
        hload = sHLD.n
        sc_gate = {0: (0, 0), 1: (0, 0)}
        for qc in range(NQC):
            nkc = qc + 1
            exp_done = {}
            for qt in range(NQT):
                for kc in range(nkc):
                    bk = kc % 2
                    psum = p0 if bk == 0 else p1
                    ww = [(sHLD, hload), (sAC, sc_gate[bk][0]), (sDV, sc_gate[bk][1])]

                    def mms(e, psum=psum, qc=qc, qt=qt, kc=kc):
                        return e.matmul(
                            psum, qh[:, qc * 512 + qt * 128:qc * 512 + (qt + 1) * 128],
                            kh[:, kc * 512:(kc + 1) * 512], start=True, stop=True)
                    P.op("tensor", mms, waits=ww, inc=sPE)
                    pe_v = sPE.n
                    acol = qt * 4096 + kc * 512
                    slot = rs[:, (qt * 16 + kc):(qt * 16 + kc) + 1]
                    if kc == qc:
                        P.op("vector", lambda e, psum=psum, qt=qt: e.tensor_add(
                            scd, psum, msk_s[:, qt * 512:(qt + 1) * 512]),
                            waits=[(sPE, pe_v), (sDM, xo_dm), (sAC, sAC.n)], inc=sDV)
                        dv_v = sDV.n
                        P.op("scalar", lambda e, acol=acol, slot=slot: e.activation(
                            attn[:, acol:acol + 512], scd, AF.Exp, scale=ISQ,
                            accum_out=slot),
                            waits=[(sDV, dv_v), (sPE2, exp_attn_gate), (sDV2, rs_gate)],
                            inc=sAC)
                        sc_gate[bk] = (sc_gate[bk][0], dv_v)
                    else:
                        P.op("scalar", lambda e, psum=psum, acol=acol, slot=slot: e.activation(
                            attn[:, acol:acol + 512], psum, AF.Exp, scale=ISQ,
                            accum_out=slot),
                            waits=[(sPE, pe_v), (sPE2, exp_attn_gate), (sDV2, rs_gate)],
                            inc=sAC)
                        sc_gate[bk] = (sAC.n, sc_gate[bk][1])
                    exp_done[(qt, kc)] = sAC.n
            for qt in range(NQT):
                P.op("vector", lambda e, qt=qt, nkc=nkc: e.reduce_sum(
                    rrec_s[:, qt:qt + 1], rs[:, qt * 16:qt * 16 + nkc], axis=AX.X),
                    waits=[(sAC, exp_done[(qt, nkc - 1)]), (sDV2, sDV2.n)], inc=sDV2)
            rs_gate = sDV2.n
            P.op("vector", lambda e: e.reciprocal(rrec_s[:, 4:8], rrec_s[:, 0:4]),
                 waits=[(sDV2, sDV2.n)], inc=sDV2)
            rec_v = sDV2.n
            for qt in range(NQT):
                P.op("tensor", lambda e, qt=qt: e.transpose(
                    p7[0:1, qt * 128:(qt + 1) * 128], rrec_s[:, 4 + qt:5 + qt],
                    identf[:, :]),
                    waits=[(sDV2, rec_v), (sDV3, sDV3.n)], inc=sPE2)
            P.op("vector", lambda e: e.tensor_copy(rT_s[:, :], p7[0:1, 0:512]),
                 waits=[(sPE2, sPE2.n)], inc=sDV3)
            P.op("tensor", lambda e: e.matmul(p7, ones1f[:, :], rT_s[:, :],
                                              start=True, stop=True),
                 waits=[(sDV3, sDV3.n)], inc=sPE2)
            P.op("vector", lambda e: e.tensor_copy(recb, p7),
                 waits=[(sPE2, sPE2.n)], inc=sDV3)
            recb_v = sDV3.n
            tp_gate = {0: sDV.n, 1: sDV.n}
            atb_gate = [sPE.n] * 4
            for kc in range(nkc):
                for skb in range(4):
                    tb = skb % 2
                    tp = t0 if tb == 0 else t1
                    for qt in range(NQT):
                        acol = qt * 4096 + kc * 512 + skb * 128
                        ww = [(sAC, exp_done[(qt, kc)])]
                        if qt == 0:
                            ww.append((sDV, tp_gate[tb]))

                        def mtr(e, tp=tp, qt=qt, acol=acol):
                            return e.transpose(tp[:, qt * 128:(qt + 1) * 128],
                                               attn[:, acol:acol + 128], identb[:, :])
                        P.op("tensor", mtr, waits=ww,
                             inc=sPE2 if qt == NQT - 1 else None)
                    tr_v = sPE2.n
                    P.op("vector", lambda e, tp=tp, skb=skb: e.tensor_copy(
                        atb[:, skb * 512:(skb + 1) * 512], tp),
                        waits=[(sPE2, tr_v), (sPE, atb_gate[skb])], inc=sDV)
                    ev_v = sDV.n
                    tp_gate[tb] = ev_v

                    def mc(e, kc=kc, skb=skb,
                           first=(kc == 0 and skb == 0),
                           last=(kc == nkc - 1 and skb == 3)):
                        return e.matmul(p2,
                                        vh[:, (kc * 4 + skb) * 128:(kc * 4 + skb + 1) * 128],
                                        atb[:, skb * 512:(skb + 1) * 512],
                                        start=first, stop=last)
                    ww = [(sDV, ev_v)]
                    if kc == 0 and skb == 0:
                        ww.append((sDV, ctx_evac_dv))
                    P.op("tensor", mc, waits=ww, inc=sPE)
                    atb_gate[skb] = sPE.n
            exp_attn_gate = sPE2.n
            P.op("vector", lambda e, hl=hl, qc=qc: e.tensor_mul(
                cst[:, hl * 4096 + qc * 512:hl * 4096 + (qc + 1) * 512], p2, recb),
                waits=[(sPE, sPE.n), (sDV3, recb_v)], inc=sDV)
            ctx_evac_dv = sDV.n
        head_pe_end = sPE.n

    if "ctx" in taps:
        dma("sync", tap_out["tap_ctx"], cst, sOUT, waits=[(sDV, ctx_evac_dv)])

    if stop_after == "B":
        P.wait("sync", (sOUT, sOUT.n))
        return _finish(nc, P, stack, tap_out)

    dma("gpsimd", woP_s.rearrange("p (dc o) -> p dc o", dc=NH),
        woP.rearrange("(dc p) o -> p dc o", p=128), sGD,
        waits=[(sAC, sAC.n), (sPE2, sPE2.n), (sPE, head_pe_end)])
    wop_gd = sGD.n
    cst_ready = ctx_evac_dv
    wo_gate = {0: sc_gate[0][0], 1: sc_gate[1][0]}
    wo_dv_gate = {0: 0, 1: 0}
    wostg_dm = [0, 0]
    rs_cc = []
    for oc in range(4):
        for st in range(32):
            bk = st % 2
            psum = p0 if bk == 0 else p1
            for hl in range(NH):
                ww = [(sGD, wop_gd), (sDV, cst_ready)]
                if hl == 0:
                    ww.append((sDV, wo_dv_gate[bk]))

                def mw(e, psum=psum, hl=hl, st=st, oc=oc):
                    return e.matmul(
                        psum, cst[:, hl * 4096 + st * 128:hl * 4096 + (st + 1) * 128],
                        woP_s[:, hl * 2048 + oc * 512:hl * 2048 + (oc + 1) * 512],
                        start=(hl == 0), stop=(hl == NH - 1))
                P.op("tensor", mw, waits=ww, inc=sPE if hl == NH - 1 else None)
            pe_v = sPE.n
            sg = st % 2
            P.op("vector", lambda e, psum=psum, sg=sg: e.tensor_copy(wostg[sg], psum),
                 waits=[(sPE, pe_v), (sWS[sg], wostg_dm[sg])], inc=sDV)
            wo_dv_gate[bk] = sDV.n
            dma("sync", rsin[oc][st * 128:(st + 1) * 128, :], wostg[sg], sWS[sg],
                waits=[(sDV, sDV.n)])
            wostg_dm[sg] = sWS[sg].n
        P.op("gpsimd", lambda e, oc=oc: e.collective_compute(
            "ReduceScatter", ALU.add, replica_groups=[[0, 1, 2, 3], [4, 5, 6, 7]],
            ins=[rsin[oc]], outs=[rsout[oc]]),
            waits=[(sWS[0], sWS[0].n), (sWS[1], sWS[1].n)], inc=sCC)
        rs_cc.append(sCC.n)
    wo_pe_end = sPE.n

    if "rs" in taps:
        for oc in range(4):
            dma("sync", tap_out["tap_rs"][oc], rsout[oc], sOUT, waits=[(sCC, rs_cc[oc])])

    for oc in range(4):
        dma("gpsimd", acc.rearrange("p (st d) -> p st d", st=8)[:, :, oc * 512:(oc + 1) * 512],
            rsout[oc].rearrange("(st p) d -> p st d", p=128), sGD,
            waits=[(sCC, rs_cc[oc]), (sDM, xo_dm)], accum_op=ALU.add)
    acc_gd = sGD.n

    if stop_after == "wo":
        P.wait("sync", (sOUT, sOUT.n))
        return _finish(nc, P, stack, tap_out)

    tap_gate = [0]
    stg_pe_gate = [0]
    tp_c_gate = {0: sDV3.n, 1: sDV3.n}
    stg_gp_gate = [0]

    def emit_ln(src_gates, write_hbf, out_to=None):
        for st in range(8):
            a_sl = acc[:, st * D:(st + 1) * D]
            mu = tiny[:, st * 8 + 0:st * 8 + 1]
            ng = tiny[:, st * 8 + 1:st * 8 + 2]
            ms = tiny[:, st * 8 + 2:st * 8 + 3]
            vv = tiny[:, st * 8 + 3:st * 8 + 4]
            sq = tiny[:, st * 8 + 4:st * 8 + 5]
            sd = tiny[:, st * 8 + 5:st * 8 + 6]
            rho = tiny[:, st * 8 + 6:st * 8 + 7]
            P.op("scalar", lambda e, a_sl=a_sl, sq=sq: e.activation(
                sqs, a_sl, AF.Square, accum_out=sq),
                waits=list(src_gates) + [(sAC2, sAC2.n)], inc=sAC2)
            sq_v = sAC2.n
            P.op("vector", lambda e, a_sl=a_sl, mu=mu: e.reduce_sum(mu, a_sl, axis=AX.X),
                 waits=list(src_gates) + [(sDV2, sDV2.n)], inc=sDV2)
            P.op("vector", lambda e, ng=ng, mu=mu: e.tensor_scalar_mul(ng, mu, -1.0 / D),
                 waits=[(sDV2, sDV2.n)], inc=sDV2)
            ng_v = sDV2.n
            P.op("gpsimd", lambda e, ms=ms, ng=ng: e.tensor_mul(ms, ng, ng),
                 waits=[(sDV2, ng_v), (sGP, sGP.n)], inc=sGP)
            ms_v = sGP.n
            P.op("vector", lambda e, vv=vv, sq=sq: e.tensor_scalar_mul(vv, sq, 1.0 / D),
                 waits=[(sAC2, sq_v), (sDV2, sDV2.n)], inc=sDV2)
            P.op("vector", lambda e, vv=vv, ms=ms: e.scalar_tensor_tensor(
                vv, vv, EPS, ms, op0=ALU.add, op1=ALU.subtract),
                 waits=[(sGP, ms_v), (sDV2, sDV2.n)], inc=sDV2)
            vv_v = sDV2.n
            P.op("scalar", lambda e, sd=sd, vv=vv: e.activation(sd, vv, AF.Sqrt),
                 waits=[(sDV2, vv_v), (sAC2, sAC2.n)], inc=sAC2)
            sd_v = sAC2.n
            P.op("vector", lambda e, rho=rho, sd=sd: e.reciprocal(rho, sd),
                 waits=[(sAC2, sd_v), (sDV2, sDV2.n)], inc=sDV2)
            P.op("vector", lambda e, a_sl=a_sl, ng=ng, rho=rho: e.tensor_scalar(
                stg, a_sl, ng, rho, op0=ALU.add, op1=ALU.mult),
                waits=[(sDV2, sDV2.n), (sOUT, tap_gate[0]),
                       (sPE2, stg_pe_gate[0]), (sGP, stg_gp_gate[0])], inc=sDV2)
            nrm_v = sDV2.n
            if write_hbf:
                for dc in range(NDC):
                    bk = dc % 2
                    pp = p2 if bk == 0 else p7
                    ww = [(sDV2, nrm_v), (sDV3, tp_c_gate[bk]), (sDV, ctx_evac_dv)]

                    def mt(e, pp=pp, dc=dc):
                        return e.transpose(pp[:, 0:128], stg[:, dc * 128:(dc + 1) * 128],
                                           identf[:, :])
                    P.op("tensor", mt, waits=ww, inc=sPE2)
                    tr_v = sPE2.n
                    P.op("vector", lambda e, pp=pp, dc=dc, st=st: e.tensor_copy(
                        h_bf[:, dc * 1024 + st * 128:dc * 1024 + (st + 1) * 128],
                        pp[:, 0:128]),
                        waits=[(sPE2, tr_v), (sDV3, sDV3.n)], inc=sDV3)
                    tp_c_gate[bk] = sDV3.n
                stg_pe_gate[0] = sPE2.n
                P.op("gpsimd", lambda e, a_sl=a_sl: e.tensor_copy(a_sl, stg),
                     waits=[(sDV2, nrm_v), (sGP, sGP.n), (sDV3, sDV3.n)], inc=sGP)
                stg_gp_gate[0] = sGP.n
            else:
                dma("sync", out_to[st * 128:(st + 1) * 128, :], stg, sOUT,
                    waits=[(sDV2, nrm_v)])
                tap_gate[0] = sOUT.n
                stg_gp_gate[0] = 0
                stg_pe_gate[0] = 0
                P.wait("vector", (sOUT, sOUT.n))

    emit_ln([(sGD, acc_gd)], write_hbf=True)
    h_ready_gp = sGP.n
    h_bf_ready = sDV3.n

    if "h" in taps:
        dma("sync", tap_out["tap_h"].rearrange("(st p) d -> p st d", p=128),
            acc.rearrange("p (st d) -> p st d", st=8), sOUT,
            waits=[(sGP, h_ready_gp)])
        tap_gate[0] = sOUT.n

    if stop_after == "ln1":
        P.wait("sync", (sOUT, sOUT.n))
        return _finish(nc, P, stack, tap_out)

    rt_gate = 0
    for st in range(8):
        for dc in range(NDC):
            ww = [(sDV3, h_bf_ready), (sGD, prolog_gd)]
            if dc == 0:
                ww.append((sDV2, rt_gate))

            def ml(e, dc=dc, st=st):
                return e.matmul(p2[:, 0:8],
                                h_bf[:, dc * 1024 + st * 128:dc * 1024 + (st + 1) * 128],
                                rw_s[:, dc * 8:(dc + 1) * 8],
                                start=(dc == 0), stop=(dc == NDC - 1))
            P.op("tensor", ml, waits=ww, inc=sPE if dc == NDC - 1 else None)
        pe_v = sPE.n
        seq = [
            ("vector", lambda e: e.tensor_copy(lg, p2[:, 0:8]), [(sPE, pe_v)]),
            ("vector", lambda e: e.reduce_max(m1, lg, axis=AX.X), None),
            ("vector", lambda e: e.tensor_scalar(mk1, lg, m1, None, op0=ALU.is_equal), None),
            ("vector", lambda e: e.scalar_tensor_tensor(l2, mk1, -1e30, lg,
                                                        op0=ALU.mult, op1=ALU.add), None),
            ("vector", lambda e: e.reduce_max(m2, l2, axis=AX.X), None),
            ("vector", lambda e: e.tensor_sub(dd, m1, m2), None),
            ("scalar", lambda e: e.activation(wA, dd, AF.Sigmoid), None),
            ("vector", lambda e: e.tensor_scalar(wB, wA, -1.0, 1.0,
                                                 op0=ALU.mult, op1=ALU.add), None),
            ("vector", lambda e: e.tensor_scalar(mk2, l2, m2, None, op0=ALU.is_equal), None),
            ("vector", lambda e, st=st: e.tensor_scalar(
                rta[:, st * 8:(st + 1) * 8], mk1, wA, None, op0=ALU.mult), None),
            ("vector", lambda e, st=st: e.scalar_tensor_tensor(
                rta[:, st * 8:(st + 1) * 8], mk2, wB, rta[:, st * 8:(st + 1) * 8],
                op0=ALU.mult, op1=ALU.add), None),
        ]
        for (eng, fn, extra) in seq:
            base = [(sDV2, sDV2.n), (sAC2, sAC2.n)]
            if extra:
                base += extra
            P.op(eng, fn, waits=base, inc=sDV2 if eng == "vector" else sAC2)
        rt_gate = sDV2.n
    rta_v = sDV2.n

    if "r" in taps:
        dma("sync", tap_out["tap_r"], rta, sOUT, waits=[(sDV2, rta_v)])
        tap_gate[0] = sOUT.n

    if stop_after == "router":
        P.wait("sync", (sOUT, sOUT.n))
        return _finish(nc, P, stack, tap_out)

    w12_pe_gate = {0: 0, 1: 0}
    w12_dm_gate = {0: 0, 1: 0}
    w3_pe_gate = {0: wo_pe_end, 1: wo_pe_end}
    w3_dm_gate = {0: 0, 1: 0}
    w3_acc_gate = {0: wo_dv_gate[0], 1: wo_dv_gate[1]}
    sil_gate = 0
    zmul_gate = 0
    z_pe_gate = [wo_pe_end] * 16
    ffn_pe_end = wo_pe_end
    gate_p0_dv = wo_dv_gate[0]
    val_p1_dv = wo_dv_gate[1]
    for (gidx, grp) in enumerate(EGRPS):
        z_slot_dv = {}
        for (le, e_) in enumerate(grp):
            for half in range(2):
                for j2 in range(4):
                    st = half * 4 + j2

                    def mtr(e, st=st, e_=e_, j2=j2):
                        return e.transpose(p2[0:1, j2 * 128:(j2 + 1) * 128],
                                           rta[:, st * 8 + e_:st * 8 + e_ + 1],
                                           identf[:, :])
                    P.op("tensor", mtr,
                         waits=[(sDV2, rta_v), (sDV3, sDV3.n)], inc=sPE2)
                P.op("vector", lambda e, half=half: e.tensor_copy(
                    rT2[:, half * 512:(half + 1) * 512], p2[0:1, 0:512]),
                    waits=[(sPE2, sPE2.n)], inc=sDV3)
            for half in range(2):
                P.op("tensor", lambda e, half=half: e.matmul(
                    p2, ones1f[:, :], rT2[:, half * 512:(half + 1) * 512],
                    start=True, stop=True),
                    waits=[(sDV3, sDV3.n), (sGP, sGP.n)], inc=sPE2)
                P.op("vector", lambda e, half=half: e.tensor_copy(
                    rb_f[:, half * 512:(half + 1) * 512], p2),
                    waits=[(sPE2, sPE2.n)], inc=sDV3)
            rb_v = sDV3.n
            for fb in range(NFB):
                bidx = fb % 2
                ww = [(sPE, w12_pe_gate[bidx]), (sPE, wo_pe_end)]
                dma("gpsimd", w1buf[bidx],
                    w1t[e_, fb].rearrange("p dc f -> p (dc f)"), sW1[bidx], waits=ww)
                dma("gpsimd", w2buf[bidx],
                    w2t[e_, fb].rearrange("p dc f -> p (dc f)"), sW1[bidx])
                wld = sW1[bidx].n
                for sh in range(2):
                    for dc in range(NDC):
                        ww = [(sW1[bidx], wld), (sDV3, h_bf_ready)]
                        if dc == 0:
                            ww += [(sAC, sil_gate), (sDV, gate_p0_dv)]

                        def mg(e, dc=dc, sh=sh, bidx=bidx):
                            return e.matmul(
                                p0, w1buf[bidx][:, dc * 128:(dc + 1) * 128],
                                h_bf[:, dc * 1024 + sh * 512:dc * 1024 + (sh + 1) * 512],
                                start=(dc == 0), stop=(dc == NDC - 1))
                        P.op("tensor", mg, waits=ww, inc=sPE if dc == NDC - 1 else None)
                    g_pe = sPE.n
                    for dc in range(NDC):
                        ww = [(sW1[bidx], wld)]
                        if dc == 0:
                            ww.append((sDV, val_p1_dv))

                        def mv(e, dc=dc, sh=sh, bidx=bidx):
                            return e.matmul(
                                p1, w2buf[bidx][:, dc * 128:(dc + 1) * 128],
                                h_bf[:, dc * 1024 + sh * 512:dc * 1024 + (sh + 1) * 512],
                                start=(dc == 0), stop=(dc == NDC - 1))
                        P.op("tensor", mv, waits=ww, inc=sPE if dc == NDC - 1 else None)
                    v_pe = sPE.n
                    P.op("scalar", lambda e: e.activation(silA, p0, AF.Silu),
                         waits=[(sPE, g_pe), (sGP, sGP.n)], inc=sAC)
                    sil_gate = sAC.n
                    gate_p0_dv = 0
                    P.op("gpsimd", lambda e, sh=sh: e.tensor_mul(
                        silB, silA, rb_f[:, sh * 512:(sh + 1) * 512]),
                        waits=[(sAC, sil_gate), (sDV3, rb_v), (sDV, zmul_gate)], inc=sGP)
                    gp_v = sGP.n
                    zi = le * 8 + fb
                    zsl = zbuf[:, zi * 1024 + sh * 512:zi * 1024 + (sh + 1) * 512]
                    P.op("vector", lambda e, zsl=zsl: e.tensor_mul(zsl, p1, silB),
                         waits=[(sGP, gp_v), (sPE, v_pe), (sPE, z_pe_gate[zi])],
                         inc=sDV)
                    zmul_gate = sDV.n
                    val_p1_dv = sDV.n
                    z_slot_dv[zi] = sDV.n
                w12_pe_gate[bidx] = sPE.n
        z_all = sDV.n
        for dchunk in range(4):
            for (le, e_) in enumerate(grp):
                bidx = (dchunk * 2 + le) % 2
                dma("gpsimd", w3buf[bidx].rearrange("p (fc d) -> p fc d", fc=NFC),
                    w3n[e_].rearrange("(fc p) d -> p fc d", p=128)[:, :, dchunk * 512:(dchunk + 1) * 512],
                    sW3[bidx], waits=[(sPE, w3_pe_gate[bidx]), (sPE, wo_pe_end)])
            w3ld = [(sW3[0], sW3[0].n), (sW3[1], sW3[1].n)]
            for st in range(8):
                bk = st % 2
                psum = p0 if bk == 0 else p1
                nmm = 0
                for (le, e_) in enumerate(grp):
                    bidx = (dchunk * 2 + le) % 2
                    for fc in range(NFC):
                        ww = list(w3ld) + [(sDV, z_all)]
                        if nmm == 0:
                            ww.append((sDV, w3_acc_gate[bk]))
                        zi = le * 8 + fc

                        def m3(e, psum=psum, zi=zi, fc=fc, st=st, bidx=bidx, nmm=nmm):
                            return e.matmul(
                                psum,
                                zbuf[:, zi * 1024 + st * 128:zi * 1024 + (st + 1) * 128],
                                w3buf[bidx][:, fc * 512:(fc + 1) * 512],
                                start=(nmm == 0), stop=(nmm == 15))
                        P.op("tensor", m3, waits=ww, inc=sPE if nmm == 15 else None)
                        nmm += 1
                pe_v = sPE.n
                a_sl = acc[:, st * D + dchunk * 512:st * D + (dchunk + 1) * 512]
                P.op("vector", lambda e, a_sl=a_sl, psum=psum: e.tensor_add(
                    a_sl, psum, a_sl),
                    waits=[(sPE, pe_v), (sGP, h_ready_gp), (sOUT, tap_gate[0]),
                           (sDV, sDV.n)], inc=sDV)
                w3_acc_gate[bk] = sDV.n
            w3_pe_gate[0] = sPE.n
            w3_pe_gate[1] = sPE.n
        for zi in range(16):
            z_pe_gate[zi] = sPE.n
        gate_p0_dv = w3_acc_gate[0]
        val_p1_dv = w3_acc_gate[1]
        ffn_pe_end = sPE.n
    ffn_done_dv = sDV.n

    if "ffn" in taps:
        dma("sync", tap_out["tap_ffn"].rearrange("(st p) d -> p st d", p=128),
            acc.rearrange("p (st d) -> p st d", st=8), sOUT, waits=[(sDV, ffn_done_dv)])
        tap_gate[0] = sOUT.n

    if stop_after == "ffn":
        P.wait("sync", (sOUT, sOUT.n))
        return _finish(nc, P, stack, tap_out)

    stg_pe_gate[0] = 0
    stg_gp_gate[0] = 0
    emit_ln([(sDV, ffn_done_dv)], write_hbf=False, out_to=outT)

    P.wait("sync", (sOUT, sOUT.n))

    return _finish(nc, P, stack, tap_out)


def _finish(nc, P, stack, tap_out):
    with nc.Block() as block:
        P.replay(block)
    stack.close()
    return nc, list(tap_out.keys())


_HPERM = np.concatenate([np.arange(0, 128, 2), np.arange(1, 128, 2)])


def _host_inputs(L0_attn_x, Wq, Wk, Wv, Wo, router_w, w1, w2, w3):
    x = np.asarray(L0_attn_x, dtype=np.float32)
    Wq = np.asarray(Wq, np.float32); Wk = np.asarray(Wk, np.float32)
    Wv = np.asarray(Wv, np.float32); Wo = np.asarray(Wo, np.float32)
    router_w = np.asarray(router_w, np.float32)
    w1 = np.asarray(w1, np.float32); w2 = np.asarray(w2, np.float32)
    w3 = np.asarray(w3, np.float32)

    xT = [np.ascontiguousarray(x[b].T) for b in range(B)]
    j = np.arange(64, dtype=np.float64)
    inv = np.power(10000.0, -2.0 * j / DK)
    pos = np.arange(S, dtype=np.float64)
    ang = pos[None, :] * inv[:, None]
    cosv = np.cos(ang); sinv = np.sin(ang)
    c2 = np.concatenate([cosv, cosv], 0).astype(np.float32)
    s2 = np.concatenate([-sinv, sinv], 0).astype(np.float32)
    csh = np.ascontiguousarray(np.stack([c2, s2], 0))
    ii = np.arange(128)[:, None]
    jj = np.arange(512)[None, :]
    mskh = np.ascontiguousarray(np.stack(
        [np.where(jj <= qt * 128 + ii, 0.0, -1e30).astype(np.float32)
         for qt in range(NQT)], 0))
    idin = np.eye(128, dtype=np.float32)
    w1th = np.ascontiguousarray(
        w1.reshape(NE, NDC, 128, NFB, 128).transpose(0, 3, 2, 1, 4))
    w2th = np.ascontiguousarray(
        w2.reshape(NE, NDC, 128, NFB, 128).transpose(0, 3, 2, 1, 4))

    in_maps = []
    for c in range(8):
        b, r = c // 4, c % 4
        heads = [4 * r + i for i in range(NH)]
        qrows = np.concatenate([h * 128 + _HPERM for h in heads])
        vrows = np.concatenate([h * 128 + np.arange(128) for h in heads])
        in_maps.append({
            "xT": xT[b],
            "xo": np.ascontiguousarray(x[b][r * NT:(r + 1) * NT]),
            "wqT": np.ascontiguousarray(Wq[qrows].T),
            "wkT": np.ascontiguousarray(Wk[qrows].T),
            "wvT": np.ascontiguousarray(Wv[vrows].T),
            "woP": np.ascontiguousarray(Wo[:, vrows].T),
            "rwT": np.ascontiguousarray(router_w.T),
            "w1t": w1th, "w2t": w2th, "w3n": w3,
            "cs": csh, "msk": mskh, "idin": idin,
        })
    return in_maps


def kernel(L0_attn_x, L0_attn_Wq, L0_attn_Wk, L0_attn_Wv, L0_attn_Wo,
           L0_ffn_router_w, L0_ffn_w1, L0_ffn_w2, L0_ffn_w3):
    in_maps = _host_inputs(L0_attn_x, L0_attn_Wq, L0_attn_Wk, L0_attn_Wv,
                           L0_attn_Wo, L0_ffn_router_w, L0_ffn_w1, L0_ffn_w2,
                           L0_ffn_w3)
    nc, _ = build_program()
    res = run_bass_kernel_spmd(nc, in_maps, core_ids=list(range(8)))
    out = np.empty((B, S, D), np.float32)
    for c in range(8):
        b, r = c // 4, c % 4
        out[b, r * NT:(r + 1) * NT, :] = res.results[c]["outT"]
    return out


# version 03
# speedup vs baseline: 1.0797x; 1.0133x over previous
import math
import numpy as np
import ml_dtypes
import concourse.bass as bass
import concourse.mybir as mybir
from concourse.bass_utils import run_bass_kernel_spmd

F32 = mybir.dt.float32
BF16 = mybir.dt.bfloat16
AF = mybir.ActivationFunctionType
ALU = mybir.AluOpType
AX = mybir.AxisListType

B, S, D, H, DK = 2, 4096, 2048, 16, 128
NH = 4
NT = 1024
NE, DF = 8, 1024
NDC = D // 128
NSC = S // 512
NQC = S // 512
NQT = 4
NFC = DF // 128
NFB = 8
EGRPS = [(0, 1), (2, 3), (4, 5), (6, 7)]
ISQ = 1.0 / math.sqrt(DK)
EPS = 1e-5
ABLATE = set()
ENGS = ("sync", "tensor", "vector", "scalar", "gpsimd")


class Sem:
    def __init__(self, h):
        self.h = h
        self.n = 0


class Prog:
    def __init__(self):
        self.items = {e: [] for e in ENGS}
        self.last_wait = {}

    def op(self, eng_name, build_fn, waits=(), inc=None, inc_by=1):
        wl = []
        for (s, v) in waits:
            if v <= 0:
                continue
            key = (eng_name, id(s))
            if self.last_wait.get(key, -1) < v:
                self.last_wait[key] = v
                wl.append((s, v))
        if inc is not None:
            inc.n += inc_by
        wl = tuple(wl)

        def f(eng, build_fn=build_fn, wl=wl, inc=inc, inc_by=inc_by):
            for (s, v) in wl:
                eng.wait_ge(s.h, v)
            ins = build_fn(eng)
            if inc is not None and ins is not None:
                ins.then_inc(inc.h, inc_by)
        self.items[eng_name].append(f)
        return inc.n if inc is not None else 0

    def wait(self, eng_name, *waits):
        self.op(eng_name, lambda e: None, waits=waits)

    def replay(self, block):
        for name in ENGS:
            ops = self.items[name]
            if not ops:
                continue

            def body(eng, ops=ops):
                for f in ops:
                    f(eng)
            getattr(block, name)(body)


def build_program(taps=(), stop_after=None):
    taps = set(taps)
    nc = bass.Bass("TRN2", target_bir_lowering=False, debug=False, num_devices=8)

    xT = nc.dram_tensor("xT", [D, S], F32, kind="ExternalInput").ap()
    xo = nc.dram_tensor("xo", [NT, D], F32, kind="ExternalInput").ap()
    wqT = nc.dram_tensor("wqT", [D, NH * DK], F32, kind="ExternalInput").ap()
    wkT = nc.dram_tensor("wkT", [D, NH * DK], F32, kind="ExternalInput").ap()
    wvT = nc.dram_tensor("wvT", [D, NH * DK], F32, kind="ExternalInput").ap()
    woP = nc.dram_tensor("woP", [NH * DK, D], F32, kind="ExternalInput").ap()
    rwT = nc.dram_tensor("rwT", [D, NE], F32, kind="ExternalInput").ap()
    w1t = nc.dram_tensor("w1t", [NE, NFB, 128, NDC, 128], F32, kind="ExternalInput").ap()
    w2t = nc.dram_tensor("w2t", [NE, NFB, 128, NDC, 128], F32, kind="ExternalInput").ap()
    w3n = nc.dram_tensor("w3n", [NE, DF, D], F32, kind="ExternalInput").ap()
    cs = nc.dram_tensor("cs", [2, 128, S], F32, kind="ExternalInput").ap()
    msk = nc.dram_tensor("msk", [NQT, 128, 512], F32, kind="ExternalInput").ap()
    idin = nc.dram_tensor("idin", [128, 128], F32, kind="ExternalInput").ap()
    outT = nc.dram_tensor("outT", [NT, D], F32, kind="ExternalOutput").ap()

    qb = nc.dram_tensor("qb", [NH, 128, S], BF16, kind="Internal").ap()
    kb = nc.dram_tensor("kb", [NH, 128, S], BF16, kind="Internal").ap()
    vb = nc.dram_tensor("vb", [NH, S, 128], BF16, kind="Internal").ap()
    rsin = [nc.dram_tensor(f"rsin{i}", [S, 512], F32, kind="Internal").ap() for i in range(4)]
    rsout = [nc.dram_tensor(f"rsout{i}", [NT, 512], F32, kind="Internal").ap() for i in range(4)]

    tap_out = {}
    if "qkv" in taps:
        tap_out["tap_q"] = nc.dram_tensor("tap_q", [NH, 128, S], BF16, kind="ExternalOutput").ap()
        tap_out["tap_k"] = nc.dram_tensor("tap_k", [NH, 128, S], BF16, kind="ExternalOutput").ap()
        tap_out["tap_v"] = nc.dram_tensor("tap_v", [NH, S, 128], BF16, kind="ExternalOutput").ap()
    if "ctx" in taps:
        tap_out["tap_ctx"] = nc.dram_tensor("tap_ctx", [128, NH * S], BF16, kind="ExternalOutput").ap()
    if "rs" in taps:
        tap_out["tap_rs"] = nc.dram_tensor("tap_rs", [4, NT, 512], F32, kind="ExternalOutput").ap()
    if "h" in taps:
        tap_out["tap_h"] = nc.dram_tensor("tap_h", [NT, D], F32, kind="ExternalOutput").ap()
    if "r" in taps:
        tap_out["tap_r"] = nc.dram_tensor("tap_r", [128, 64], F32, kind="ExternalOutput").ap()
    if "ffn" in taps:
        tap_out["tap_ffn"] = nc.dram_tensor("tap_ffn", [NT, D], F32, kind="ExternalOutput").ap()

    ab = nc.alloc_sbuf_tensor("arena_bf", [128, 49152], BF16).ap()
    af = nc.alloc_sbuf_tensor("arena_f32", [128, 11264], F32).ap()
    acc = nc.alloc_sbuf_tensor("acc", [128, 8 * D], F32).ap()
    identb = nc.alloc_sbuf_tensor("identb", [128, 128], BF16).ap()
    identf = nc.alloc_sbuf_tensor("identf", [128, 128], F32).ap()
    ones1f = nc.alloc_sbuf_tensor("ones1f", [1, 128], F32).ap()
    rw_s = nc.alloc_sbuf_tensor("rw_s", [128, NDC * NE], BF16).ap()
    rrec_s = nc.alloc_sbuf_tensor("rrec_s", [128, 8], F32).ap()
    rT_s = nc.alloc_sbuf_tensor("rT_s", [1, 512], F32).ap()

    wq_s = ab[:, 0:8192]
    wk_s = ab[:, 8192:16384]
    wv_s = ab[:, 16384:24576]
    xbf = [ab[:, 24576 + i * 8192:24576 + (i + 1) * 8192] for i in (0, 1)]
    qstage = ab[:, 40960:43008]
    kstage = ab[:, 43008:45056]
    vstage = ab[:, 45056:47104]
    cst = ab[:, 0:16384]
    qh = ab[:, 16384:20480]
    kh = ab[:, 20480:24576]
    vh = ab[:, 24576:28672]
    attn = ab[:, 28672:45056]
    atb = ab[:, 45056:47104]
    woP_s = ab[:, 28672:36864]
    zbuf = ab[:, 0:16384]
    h_bf = ab[:, 16384:32768]
    w1buf = [ab[:, 32768 + i * 2048:32768 + (i + 1) * 2048] for i in (0, 1)]
    w2buf = [ab[:, 36864 + i * 2048:36864 + (i + 1) * 2048] for i in (0, 1)]
    w3buf = [ab[:, 40960 + i * 4096:40960 + (i + 1) * 4096] for i in (0, 1)]

    ropB = af[:, 0:512]
    c2 = af[:, 2048:6144]
    s2 = af[:, 6144:10240]
    swp = af[:, 10240:10752]
    ropA = af[:, 10752:11264]
    msk_s = af[:, 0:2048]
    scd = af[:, 10240:10752]
    rs = af[:, 10752:10816]
    recb = af[:, 2048:2560]
    wostg = [af[:, 2048 + i * 2048:2048 + (i + 1) * 2048] for i in (0, 1)]
    rT2 = af[0:1, 0:1024]
    tiny = af[:, 1024:1088]
    rta = af[:, 1088:1152]
    lg = af[:, 1152:1160]
    m1 = af[:, 1160:1161]
    l2 = af[:, 1168:1176]
    m2 = af[:, 1176:1177]
    dd = af[:, 1177:1178]
    wA = af[:, 1178:1179]
    wB = af[:, 1179:1180]
    mk1 = af[:, 1184:1192]
    mk2 = af[:, 1192:1200]
    sqs = af[:, 2048:4096]
    stg = af[:, 4096:6144]
    silA = af[:, 6144:6656]
    silB = af[:, 6656:7168]
    rb_f = af[:, 7168:8192]

    p0 = nc.alloc_psum_tensor("p0", [128, 512], F32).ap()
    p1 = nc.alloc_psum_tensor("p1", [128, 512], F32).ap()
    p2 = nc.alloc_psum_tensor("p2", [128, 512], F32).ap()
    p7 = nc.alloc_psum_tensor("p7", [128, 512], F32).ap()
    t0 = nc.alloc_psum_tensor("t0", [128, 512], BF16).ap()
    t1 = nc.alloc_psum_tensor("t1", [128, 512], BF16).ap()

    import contextlib
    stack = contextlib.ExitStack()
    SN = ["sDM", "sGD", "sGD2", "sPE", "sPE2", "sDV", "sDV2", "sDV3",
          "sAC", "sAC2", "sGP", "sCC", "sOUT",
          "sXB0", "sXB1", "sSTQ", "sSTK", "sSTV", "sHLD",
          "sWS0", "sWS1", "sW10", "sW11", "sW30", "sW31"]
    sem = {n: Sem(stack.enter_context(nc.semaphore(n))) for n in SN}
    sDM, sGD, sGD2 = sem["sDM"], sem["sGD"], sem["sGD2"]
    sPE, sPE2 = sem["sPE"], sem["sPE2"]
    sDV, sDV2, sDV3 = sem["sDV"], sem["sDV2"], sem["sDV3"]
    sAC, sAC2, sGP = sem["sAC"], sem["sAC2"], sem["sGP"]
    sCC, sOUT = sem["sCC"], sem["sOUT"]
    sXB = [sem["sXB0"], sem["sXB1"]]
    sSTQ, sSTK, sSTV = sem["sSTQ"], sem["sSTK"], sem["sSTV"]
    sHLD = sem["sHLD"]
    sWS = [sem["sWS0"], sem["sWS1"]]
    sW1 = [sem["sW10"], sem["sW11"]]
    sW3 = [sem["sW30"], sem["sW31"]]

    P = Prog()

    def dma(eng, out, in_, s, waits=(), **kw):
        return P.op(eng, lambda e: e.dma_start(out=out, in_=in_, **kw),
                    waits=waits, inc=s, inc_by=16)

    dma("gpsimd", identb[:, :], idin, sGD)
    dma("gpsimd", rw_s.rearrange("p (dc e) -> p dc e", dc=NDC),
        rwT.rearrange("(dc p) e -> p dc e", p=128), sGD)
    dma("gpsimd", wq_s.rearrange("p (dc o) -> p dc o", dc=NDC),
        wqT.rearrange("(dc p) o -> p dc o", p=128), sGD)
    dma("gpsimd", wk_s.rearrange("p (dc o) -> p dc o", dc=NDC),
        wkT.rearrange("(dc p) o -> p dc o", p=128), sGD)
    dma("gpsimd", wv_s.rearrange("p (dc o) -> p dc o", dc=NDC),
        wvT.rearrange("(dc p) o -> p dc o", p=128), sGD)
    dma("sync", identf[:, :], idin, sDM)
    dma("sync", c2, cs[0], sDM)
    dma("sync", s2, cs[1], sDM)
    P.op("gpsimd", lambda e: e.memset(ones1f[:, :], 1.0), inc=sGP)
    prolog_gd, prolog_dm, prolog_gp = sGD.n, sDM.n, sGP.n

    if stop_after == "P":
        P.wait("sync", (sGD, sGD.n), (sDM, sDM.n), (sGD2, sGD2.n), (sGP, sGP.n))
        return _finish(nc, P, stack, tap_out)

    xld = []
    xfree = []
    bank_free = {0: (0, 0), 1: (0, 0)}
    stage_dm = {"q": (sSTQ, 0), "k": (sSTK, 0), "v": (sSTV, 0)}
    swp_gate = 0
    ropB_gate = 0
    for sc in range(NSC):
        bi = sc % 2
        w = [(sGD, prolog_gd), (sDM, prolog_dm)]
        if sc >= 2:
            w.append((sPE, xfree[sc - 2]))
        dma("gpsimd", xbf[bi].rearrange("p (dc c) -> p dc c", dc=NDC),
            xT.rearrange("(dc p) s -> p dc s", p=128)[:, :, sc * 512:(sc + 1) * 512],
            sXB[bi], waits=w)
        xld.append((sXB[bi], sXB[bi].n))
        col = sc * 512
        gi = 0
        for (stage, w_s, name, dst) in (() if "noqk" in ABLATE else ((qstage, wq_s, "q", qb), (kstage, wk_s, "k", kb))):
            fin = []
            for ot in range(NH):
                psum = p0 if (gi % 2 == 0) else p1
                bfree = bank_free[gi % 2]
                for dc in range(NDC):
                    ww = [xld[sc]]
                    if dc == 0:
                        ww += [(sAC, bfree[0]), (sDV, bfree[1])]

                    def mm(e, psum=psum, w_s=w_s, dc=dc, ot=ot, bi=bi):
                        return e.matmul(
                            psum, w_s[:, dc * 512 + ot * 128:dc * 512 + (ot + 1) * 128],
                            xbf[bi][:, dc * 512:(dc + 1) * 512],
                            start=(dc == 0), stop=(dc == NDC - 1))
                    P.op("tensor", mm, waits=ww, inc=sPE if dc == NDC - 1 else None)
                pe_v = sPE.n
                if "norope" in ABLATE:
                    sl = stage[:, ot * 512:(ot + 1) * 512]
                    P.op("vector", lambda e, sl=sl, psum=psum: e.tensor_copy(sl, psum),
                         waits=[(sPE, pe_v), stage_dm[name], (sDV, ropB_gate)], inc=sDV)
                    ropB_gate = sDV.n
                    fin.append(sDV.n)
                    bank_free[gi % 2] = (sAC.n, sDV.n)
                    gi += 1
                    continue
                P.op("scalar", lambda e, psum=psum: e.copy(swp[0:64, :], psum[64:128, :]),
                     waits=[(sPE, pe_v), (sGP, swp_gate)], inc=sAC)
                P.op("scalar", lambda e, psum=psum: e.copy(swp[64:128, :], psum[0:64, :]),
                     inc=sAC)
                ac_v = sAC.n
                P.op("vector", lambda e, psum=psum, col=col: e.tensor_mul(
                    ropA, psum, c2[:, col:col + 512]),
                    waits=[(sPE, pe_v), (sDM, prolog_dm), (sDV, ropB_gate),
                           (sAC, ac_v)], inc=sDV)
                dv_a = sDV.n
                if "r3" in ABLATE:
                    P.op("gpsimd", lambda e, col=col: e.tensor_mul(
                        ropB, c2[:, col:col + 512], s2[:, col:col + 512]),
                        waits=[(sAC, ac_v), (sDV, ropB_gate)], inc=sGP)
                else:
                    P.op("gpsimd", lambda e, col=col: e.tensor_mul(
                        ropB, swp, s2[:, col:col + 512]),
                        waits=[(sAC, ac_v), (sDV, ropB_gate)], inc=sGP)
                swp_gate = sGP.n
                sl = stage[:, ot * 512:(ot + 1) * 512]
                P.op("vector", lambda e, sl=sl: e.tensor_add(sl, ropA, ropB),
                     waits=[(sDV, dv_a), (sGP, sGP.n), stage_dm[name]], inc=sDV)
                ropB_gate = sDV.n
                fin.append(sDV.n)
                bank_free[gi % 2] = (sAC.n, sDV.n)
                gi += 1
            sch = sSTQ if name == "q" else sSTK
            if "nostage" not in ABLATE:
                dma("sync", dst.rearrange("h p s -> p h s")[:, :, sc * 512:(sc + 1) * 512],
                    stage.rearrange("p (h c) -> p h c", h=NH), sch, waits=[(sDV, fin[-1])])
                stage_dm[name] = (sch, sch.n)
        vfin = []
        for st in range(0 if "nov" in ABLATE else 4):
            psum = p0 if (gi % 2 == 0) else p1
            bfree = bank_free[gi % 2]
            for dc in range(NDC):
                ww = [xld[sc]]
                if dc == 0:
                    ww += [(sAC, bfree[0]), (sDV, bfree[1])]

                def mmv(e, psum=psum, dc=dc, st=st, bi=bi):
                    return e.matmul(
                        psum, xbf[bi][:, dc * 512 + st * 128:dc * 512 + (st + 1) * 128],
                        wv_s[:, dc * 512:(dc + 1) * 512],
                        start=(dc == 0), stop=(dc == NDC - 1))
                P.op("tensor", mmv, waits=ww, inc=sPE if dc == NDC - 1 else None)
            pe_v = sPE.n
            P.op("vector", lambda e, psum=psum, st=st: e.tensor_copy(
                vstage[:, st * 512:(st + 1) * 512], psum),
                waits=[(sPE, pe_v), stage_dm["v"]], inc=sDV)
            vfin.append(sDV.n)
            bank_free[gi % 2] = (sAC.n, sDV.n)
            gi += 1
        for hh in range(0 if ("nov" in ABLATE or "nostage" in ABLATE) else NH):
            dma("sync",
                vb[hh].rearrange("(st p) d -> p st d", p=128)[:, sc * 4:(sc + 1) * 4, :],
                vstage.rearrange("p (st h d) -> p st h d", st=4, h=NH)[:, :, hh, :],
                sSTV, waits=[(sDV, vfin[-1])])
        stage_dm["v"] = (sSTV, sSTV.n)
        xfree.append(sPE.n)
    phA_bar = [(sSTQ, sSTQ.n), (sSTK, sSTK.n), (sSTV, sSTV.n)]
    phA_pe = sPE.n

    if "qkv" in taps:
        dma("sync", tap_out["tap_q"].rearrange("h p s -> (h p) s"),
            qb.rearrange("h p s -> (h p) s"), sOUT, waits=phA_bar)
        dma("sync", tap_out["tap_k"].rearrange("h p s -> (h p) s"),
            kb.rearrange("h p s -> (h p) s"), sOUT, waits=phA_bar)
        dma("sync", tap_out["tap_v"].rearrange("h s d -> (h s) d"),
            vb.rearrange("h s d -> (h s) d"), sOUT, waits=phA_bar)

    if stop_after == "A":
        P.wait("sync", (sOUT, sOUT.n))
        return _finish(nc, P, stack, tap_out)

    dma("sync", msk_s.rearrange("p (q c) -> p q c", q=NQT),
        msk.rearrange("q p c -> p q c"), sDM,
        waits=[(sDV, sDV.n), (sGP, sGP.n)])
    mask_dm = sDM.n
    dma("sync", acc.rearrange("p (st d) -> p st d", st=8),
        xo.rearrange("(st p) d -> p st d", p=128), sDM)
    xo_dm = sDM.n

    head_pe_end = phA_pe
    ctx_evac_dv = 0
    exp_attn_gate = 0
    rs_gate = 0
    recb_v = 0
    for hl in range(NH):
        dma("sync", qh, qb[hl], sHLD,
            waits=phA_bar + [(sPE, head_pe_end)])
        dma("sync", kh, kb[hl], sHLD)
        dma("sync", vh.rearrange("p (st d) -> p st d", st=32),
            vb[hl].rearrange("(st p) d -> p st d", p=128), sHLD)
        hload = sHLD.n
        sc_gate = {0: (0, 0), 1: (0, 0)}
        for qc in range(NQC):
            nkc = qc + 1
            exp_done = {}
            for qt in range(NQT):
                for kc in range(nkc):
                    bk = kc % 2
                    psum = p0 if bk == 0 else p1
                    ww = [(sHLD, hload), (sAC, sc_gate[bk][0]), (sDV, sc_gate[bk][1])]

                    def mms(e, psum=psum, qc=qc, qt=qt, kc=kc):
                        return e.matmul(
                            psum, qh[:, qc * 512 + qt * 128:qc * 512 + (qt + 1) * 128],
                            kh[:, kc * 512:(kc + 1) * 512], start=True, stop=True)
                    P.op("tensor", mms, waits=ww, inc=sPE)
                    pe_v = sPE.n
                    acol = qt * 4096 + kc * 512
                    slot = rs[:, (qt * 16 + kc):(qt * 16 + kc) + 1]
                    if kc == qc:
                        P.op("vector", lambda e, psum=psum, qt=qt: e.tensor_add(
                            scd, psum, msk_s[:, qt * 512:(qt + 1) * 512]),
                            waits=[(sPE, pe_v), (sDM, xo_dm), (sAC, sAC.n)], inc=sDV)
                        dv_v = sDV.n
                        P.op("scalar", lambda e, acol=acol, slot=slot: e.activation(
                            attn[:, acol:acol + 512], scd, AF.Exp, scale=ISQ,
                            accum_out=slot),
                            waits=[(sDV, dv_v), (sPE2, exp_attn_gate), (sDV2, rs_gate)],
                            inc=sAC)
                        sc_gate[bk] = (sc_gate[bk][0], dv_v)
                    else:
                        P.op("scalar", lambda e, psum=psum, acol=acol, slot=slot: e.activation(
                            attn[:, acol:acol + 512], psum, AF.Exp, scale=ISQ,
                            accum_out=slot),
                            waits=[(sPE, pe_v), (sPE2, exp_attn_gate), (sDV2, rs_gate)],
                            inc=sAC)
                        sc_gate[bk] = (sAC.n, sc_gate[bk][1])
                    exp_done[(qt, kc)] = sAC.n
            for qt in range(NQT):
                P.op("vector", lambda e, qt=qt, nkc=nkc: e.reduce_sum(
                    rrec_s[:, qt:qt + 1], rs[:, qt * 16:qt * 16 + nkc], axis=AX.X),
                    waits=[(sAC, exp_done[(qt, nkc - 1)]), (sDV2, sDV2.n)], inc=sDV2)
            rs_gate = sDV2.n
            P.op("vector", lambda e: e.reciprocal(rrec_s[:, 4:8], rrec_s[:, 0:4]),
                 waits=[(sDV2, sDV2.n)], inc=sDV2)
            rec_v = sDV2.n
            tp_gate = {0: sDV.n, 1: sDV.n}
            atb_gate = [sPE.n] * 4
            for kc in range(nkc):
                for skb in range(4):
                    tb = skb % 2
                    tp = t0 if tb == 0 else t1
                    for qt in range(NQT):
                        acol = qt * 4096 + kc * 512 + skb * 128
                        ww = [(sAC, exp_done[(qt, kc)])]
                        if qt == 0:
                            ww.append((sDV, tp_gate[tb]))

                        def mtr(e, tp=tp, qt=qt, acol=acol):
                            return e.transpose(tp[:, qt * 128:(qt + 1) * 128],
                                               attn[:, acol:acol + 128], identb[:, :])
                        P.op("tensor", mtr, waits=ww,
                             inc=sPE2 if qt == NQT - 1 else None)
                    tr_v = sPE2.n
                    P.op("vector", lambda e, tp=tp, skb=skb: e.tensor_copy(
                        atb[:, skb * 512:(skb + 1) * 512], tp),
                        waits=[(sPE2, tr_v), (sPE, atb_gate[skb])], inc=sDV)
                    ev_v = sDV.n
                    tp_gate[tb] = ev_v

                    def mc(e, kc=kc, skb=skb,
                           first=(kc == 0 and skb == 0),
                           last=(kc == nkc - 1 and skb == 3)):
                        return e.matmul(p2,
                                        vh[:, (kc * 4 + skb) * 128:(kc * 4 + skb + 1) * 128],
                                        atb[:, skb * 512:(skb + 1) * 512],
                                        start=first, stop=last)
                    ww = [(sDV, ev_v)]
                    if kc == 0 and skb == 0:
                        ww.append((sDV, ctx_evac_dv))
                    P.op("tensor", mc, waits=ww, inc=sPE)
                    atb_gate[skb] = sPE.n
            for qt in range(NQT):
                P.op("tensor", lambda e, qt=qt: e.transpose(
                    p7[0:1, qt * 128:(qt + 1) * 128], rrec_s[:, 4 + qt:5 + qt],
                    identf[:, :]),
                    waits=[(sDV2, rec_v), (sDV3, sDV3.n)], inc=sPE2)
            P.op("vector", lambda e: e.tensor_copy(rT_s[:, :], p7[0:1, 0:512]),
                 waits=[(sPE2, sPE2.n)], inc=sDV3)
            P.op("tensor", lambda e: e.matmul(p7, ones1f[:, :], rT_s[:, :],
                                              start=True, stop=True),
                 waits=[(sDV3, sDV3.n)], inc=sPE2)
            P.op("vector", lambda e: e.tensor_copy(recb, p7),
                 waits=[(sPE2, sPE2.n)], inc=sDV3)
            recb_v = sDV3.n
            exp_attn_gate = sPE2.n
            P.op("vector", lambda e, hl=hl, qc=qc: e.tensor_mul(
                cst[:, hl * 4096 + qc * 512:hl * 4096 + (qc + 1) * 512], p2, recb),
                waits=[(sPE, sPE.n), (sDV3, recb_v)], inc=sDV)
            ctx_evac_dv = sDV.n
        head_pe_end = sPE.n

    if "ctx" in taps:
        dma("sync", tap_out["tap_ctx"], cst, sOUT, waits=[(sDV, ctx_evac_dv)])

    if stop_after == "B":
        P.wait("sync", (sOUT, sOUT.n))
        return _finish(nc, P, stack, tap_out)

    dma("gpsimd", woP_s.rearrange("p (dc o) -> p dc o", dc=NH),
        woP.rearrange("(dc p) o -> p dc o", p=128), sGD,
        waits=[(sAC, sAC.n), (sPE2, sPE2.n), (sPE, head_pe_end)])
    wop_gd = sGD.n
    cst_ready = ctx_evac_dv
    wo_gate = {0: sc_gate[0][0], 1: sc_gate[1][0]}
    wo_dv_gate = {0: 0, 1: 0}
    wostg_dm = [0, 0]
    rs_cc = []
    for oc in range(4):
        for sq in range(8):
            sg = sq % 2
            for j in range(4):
                st = sq * 4 + j
                bk = st % 2
                psum = p0 if bk == 0 else p1
                for hl in range(NH):
                    ww = [(sGD, wop_gd), (sDV, cst_ready)]
                    if hl == 0:
                        ww += [(sDV, wo_dv_gate[bk]), (sWS[sg], wostg_dm[sg])]

                    def mw(e, psum=psum, hl=hl, st=st, oc=oc):
                        return e.matmul(
                            psum, cst[:, hl * 4096 + st * 128:hl * 4096 + (st + 1) * 128],
                            woP_s[:, hl * 2048 + oc * 512:hl * 2048 + (oc + 1) * 512],
                            start=(hl == 0), stop=(hl == NH - 1))
                    P.op("tensor", mw, waits=ww, inc=sPE if hl == NH - 1 else None)
                pe_v = sPE.n
                P.op("vector", lambda e, psum=psum, sg=sg, j=j: e.tensor_copy(
                    wostg[sg][:, j * 512:(j + 1) * 512], psum),
                    waits=[(sPE, pe_v), (sWS[sg], wostg_dm[sg])], inc=sDV)
                wo_dv_gate[bk] = sDV.n
            dma("sync",
                rsin[oc].rearrange("(q p) c -> p q c", p=128)[:, sq * 4:(sq + 1) * 4, :],
                wostg[sg].rearrange("p (j c) -> p j c", j=4), sWS[sg],
                waits=[(sDV, sDV.n)])
            wostg_dm[sg] = sWS[sg].n
        P.op("gpsimd", lambda e, oc=oc: e.collective_compute(
            "ReduceScatter", ALU.add, replica_groups=[[0, 1, 2, 3], [4, 5, 6, 7]],
            ins=[rsin[oc]], outs=[rsout[oc]]),
            waits=[(sWS[0], sWS[0].n), (sWS[1], sWS[1].n)], inc=sCC)
        rs_cc.append(sCC.n)
    wo_pe_end = sPE.n

    if "rs" in taps:
        for oc in range(4):
            dma("sync", tap_out["tap_rs"][oc], rsout[oc], sOUT, waits=[(sCC, rs_cc[oc])])

    for oc in range(4):
        dma("gpsimd", acc.rearrange("p (st d) -> p st d", st=8)[:, :, oc * 512:(oc + 1) * 512],
            rsout[oc].rearrange("(st p) d -> p st d", p=128), sGD,
            waits=[(sCC, rs_cc[oc]), (sDM, xo_dm)], accum_op=ALU.add)
    acc_gd = sGD.n

    if stop_after == "wo":
        P.wait("sync", (sOUT, sOUT.n))
        return _finish(nc, P, stack, tap_out)

    tap_gate = [0]
    stg_pe_gate = [0]
    tp_c_gate = {0: sDV3.n, 1: sDV3.n}
    stg_gp_gate = [0]

    def emit_ln(src_gates, write_hbf, out_to=None):
        for st in range(8):
            a_sl = acc[:, st * D:(st + 1) * D]
            mu = tiny[:, st * 8 + 0:st * 8 + 1]
            ng = tiny[:, st * 8 + 1:st * 8 + 2]
            ms = tiny[:, st * 8 + 2:st * 8 + 3]
            vv = tiny[:, st * 8 + 3:st * 8 + 4]
            sq = tiny[:, st * 8 + 4:st * 8 + 5]
            sd = tiny[:, st * 8 + 5:st * 8 + 6]
            rho = tiny[:, st * 8 + 6:st * 8 + 7]
            P.op("scalar", lambda e, a_sl=a_sl, sq=sq: e.activation(
                sqs, a_sl, AF.Square, accum_out=sq),
                waits=list(src_gates) + [(sAC2, sAC2.n)], inc=sAC2)
            sq_v = sAC2.n
            P.op("vector", lambda e, a_sl=a_sl, mu=mu: e.reduce_sum(mu, a_sl, axis=AX.X),
                 waits=list(src_gates) + [(sDV2, sDV2.n)], inc=sDV2)
            P.op("vector", lambda e, ng=ng, mu=mu: e.tensor_scalar_mul(ng, mu, -1.0 / D),
                 waits=[(sDV2, sDV2.n)], inc=sDV2)
            ng_v = sDV2.n
            P.op("gpsimd", lambda e, ms=ms, ng=ng: e.tensor_mul(ms, ng, ng),
                 waits=[(sDV2, ng_v), (sGP, sGP.n)], inc=sGP)
            ms_v = sGP.n
            P.op("vector", lambda e, vv=vv, sq=sq: e.tensor_scalar_mul(vv, sq, 1.0 / D),
                 waits=[(sAC2, sq_v), (sDV2, sDV2.n)], inc=sDV2)
            P.op("vector", lambda e, vv=vv, ms=ms: e.scalar_tensor_tensor(
                vv, vv, EPS, ms, op0=ALU.add, op1=ALU.subtract),
                 waits=[(sGP, ms_v), (sDV2, sDV2.n)], inc=sDV2)
            vv_v = sDV2.n
            P.op("scalar", lambda e, sd=sd, vv=vv: e.activation(sd, vv, AF.Sqrt),
                 waits=[(sDV2, vv_v), (sAC2, sAC2.n)], inc=sAC2)
            sd_v = sAC2.n
            P.op("vector", lambda e, rho=rho, sd=sd: e.reciprocal(rho, sd),
                 waits=[(sAC2, sd_v), (sDV2, sDV2.n)], inc=sDV2)
            P.op("vector", lambda e, a_sl=a_sl, ng=ng, rho=rho: e.tensor_scalar(
                stg, a_sl, ng, rho, op0=ALU.add, op1=ALU.mult),
                waits=[(sDV2, sDV2.n), (sOUT, tap_gate[0]),
                       (sPE2, stg_pe_gate[0]), (sGP, stg_gp_gate[0])], inc=sDV2)
            nrm_v = sDV2.n
            if write_hbf:
                for dc in range(NDC):
                    bk = dc % 2
                    pp = p2 if bk == 0 else p7
                    ww = [(sDV2, nrm_v), (sDV3, tp_c_gate[bk]), (sDV, ctx_evac_dv)]

                    def mt(e, pp=pp, dc=dc):
                        return e.transpose(pp[:, 0:128], stg[:, dc * 128:(dc + 1) * 128],
                                           identf[:, :])
                    P.op("tensor", mt, waits=ww, inc=sPE2)
                    tr_v = sPE2.n
                    P.op("vector", lambda e, pp=pp, dc=dc, st=st: e.tensor_copy(
                        h_bf[:, dc * 1024 + st * 128:dc * 1024 + (st + 1) * 128],
                        pp[:, 0:128]),
                        waits=[(sPE2, tr_v), (sDV3, sDV3.n)], inc=sDV3)
                    tp_c_gate[bk] = sDV3.n
                stg_pe_gate[0] = sPE2.n
                P.op("gpsimd", lambda e, a_sl=a_sl: e.tensor_copy(a_sl, stg),
                     waits=[(sDV2, nrm_v), (sGP, sGP.n), (sDV3, sDV3.n)], inc=sGP)
                stg_gp_gate[0] = sGP.n
            else:
                dma("sync", out_to[st * 128:(st + 1) * 128, :], stg, sOUT,
                    waits=[(sDV2, nrm_v)])
                tap_gate[0] = sOUT.n
                stg_gp_gate[0] = 0
                stg_pe_gate[0] = 0
                P.wait("vector", (sOUT, sOUT.n))

    emit_ln([(sGD, acc_gd)], write_hbf=True)
    h_ready_gp = sGP.n
    h_bf_ready = sDV3.n

    if "h" in taps:
        dma("sync", tap_out["tap_h"].rearrange("(st p) d -> p st d", p=128),
            acc.rearrange("p (st d) -> p st d", st=8), sOUT,
            waits=[(sGP, h_ready_gp)])
        tap_gate[0] = sOUT.n

    if stop_after == "ln1":
        P.wait("sync", (sOUT, sOUT.n))
        return _finish(nc, P, stack, tap_out)

    rt_gate = 0
    for st in range(8):
        for dc in range(NDC):
            ww = [(sDV3, h_bf_ready), (sGD, prolog_gd)]
            if dc == 0:
                ww.append((sDV2, rt_gate))

            def ml(e, dc=dc, st=st):
                return e.matmul(p2[:, 0:8],
                                h_bf[:, dc * 1024 + st * 128:dc * 1024 + (st + 1) * 128],
                                rw_s[:, dc * 8:(dc + 1) * 8],
                                start=(dc == 0), stop=(dc == NDC - 1))
            P.op("tensor", ml, waits=ww, inc=sPE if dc == NDC - 1 else None)
        pe_v = sPE.n
        seq = [
            ("vector", lambda e: e.tensor_copy(lg, p2[:, 0:8]), [(sPE, pe_v)]),
            ("vector", lambda e: e.reduce_max(m1, lg, axis=AX.X), None),
            ("vector", lambda e: e.tensor_scalar(mk1, lg, m1, None, op0=ALU.is_equal), None),
            ("vector", lambda e: e.scalar_tensor_tensor(l2, mk1, -1e30, lg,
                                                        op0=ALU.mult, op1=ALU.add), None),
            ("vector", lambda e: e.reduce_max(m2, l2, axis=AX.X), None),
            ("vector", lambda e: e.tensor_sub(dd, m1, m2), None),
            ("scalar", lambda e: e.activation(wA, dd, AF.Sigmoid), None),
            ("vector", lambda e: e.tensor_scalar(wB, wA, -1.0, 1.0,
                                                 op0=ALU.mult, op1=ALU.add), None),
            ("vector", lambda e: e.tensor_scalar(mk2, l2, m2, None, op0=ALU.is_equal), None),
            ("vector", lambda e, st=st: e.tensor_scalar(
                rta[:, st * 8:(st + 1) * 8], mk1, wA, None, op0=ALU.mult), None),
            ("vector", lambda e, st=st: e.scalar_tensor_tensor(
                rta[:, st * 8:(st + 1) * 8], mk2, wB, rta[:, st * 8:(st + 1) * 8],
                op0=ALU.mult, op1=ALU.add), None),
        ]
        for (i_, (eng, fn, extra)) in enumerate(seq):
            base = [(sDV2, sDV2.n), (sAC2, sAC2.n)]
            if extra:
                base += extra
            P.op(eng, fn, waits=base, inc=sDV2 if eng == "vector" else sAC2)
            if i_ == 0:
                rt_gate = sDV2.n
    rta_v = sDV2.n

    if "r" in taps:
        dma("sync", tap_out["tap_r"], rta, sOUT, waits=[(sDV2, rta_v)])
        tap_gate[0] = sOUT.n

    if stop_after == "router":
        P.wait("sync", (sOUT, sOUT.n))
        return _finish(nc, P, stack, tap_out)

    w12_pe_gate = {0: 0, 1: 0}
    w12_dm_gate = {0: 0, 1: 0}
    w3_pe_gate = {0: wo_pe_end, 1: wo_pe_end}
    w3_dm_gate = {0: 0, 1: 0}
    w3_acc_gate = {0: wo_dv_gate[0], 1: wo_dv_gate[1]}
    sil_gate = 0
    zmul_gate = 0
    z_pe_gate = [wo_pe_end] * 16
    ffn_pe_end = wo_pe_end
    gate_p0_dv = wo_dv_gate[0]
    val_p1_dv = wo_dv_gate[1]
    for (gidx, grp) in enumerate(EGRPS):
        z_slot_dv = {}
        for (le, e_) in enumerate(grp):
            def emit_rb(e_=e_):
                for half in range(2):
                    for j2 in range(4):
                        st = half * 4 + j2

                        def mtr(e, st=st, e_=e_, j2=j2):
                            return e.transpose(p2[0:1, j2 * 128:(j2 + 1) * 128],
                                               rta[:, st * 8 + e_:st * 8 + e_ + 1],
                                               identf[:, :])
                        P.op("tensor", mtr,
                             waits=[(sDV2, rta_v), (sDV3, sDV3.n)], inc=sPE2)
                    P.op("vector", lambda e, half=half: e.tensor_copy(
                        rT2[:, half * 512:(half + 1) * 512], p2[0:1, 0:512]),
                        waits=[(sPE2, sPE2.n)], inc=sDV3)
                for half in range(2):
                    P.op("tensor", lambda e, half=half: e.matmul(
                        p2, ones1f[:, :], rT2[:, half * 512:(half + 1) * 512],
                        start=True, stop=True),
                        waits=[(sDV3, sDV3.n), (sGP, sGP.n)], inc=sPE2)
                    P.op("vector", lambda e, half=half: e.tensor_copy(
                        rb_f[:, half * 512:(half + 1) * 512], p2),
                        waits=[(sPE2, sPE2.n)], inc=sDV3)
                return sDV3.n
            rb_v = None
            for fb in range(NFB):
                bidx = fb % 2
                ww = [(sPE, w12_pe_gate[bidx]), (sPE, wo_pe_end)]
                dma("gpsimd", w1buf[bidx],
                    w1t[e_, fb].rearrange("p dc f -> p (dc f)"), sW1[bidx], waits=ww)
                dma("gpsimd", w2buf[bidx],
                    w2t[e_, fb].rearrange("p dc f -> p (dc f)"), sW1[bidx])
                wld = sW1[bidx].n
                for sh in range(2):
                    for dc in range(NDC):
                        ww = [(sW1[bidx], wld), (sDV3, h_bf_ready)]
                        if dc == 0:
                            ww += [(sAC, sil_gate), (sDV, gate_p0_dv)]

                        def mg(e, dc=dc, sh=sh, bidx=bidx):
                            return e.matmul(
                                p0, w1buf[bidx][:, dc * 128:(dc + 1) * 128],
                                h_bf[:, dc * 1024 + sh * 512:dc * 1024 + (sh + 1) * 512],
                                start=(dc == 0), stop=(dc == NDC - 1))
                        P.op("tensor", mg, waits=ww, inc=sPE if dc == NDC - 1 else None)
                    g_pe = sPE.n
                    for dc in range(NDC):
                        ww = [(sW1[bidx], wld)]
                        if dc == 0:
                            ww.append((sDV, val_p1_dv))

                        def mv(e, dc=dc, sh=sh, bidx=bidx):
                            return e.matmul(
                                p1, w2buf[bidx][:, dc * 128:(dc + 1) * 128],
                                h_bf[:, dc * 1024 + sh * 512:dc * 1024 + (sh + 1) * 512],
                                start=(dc == 0), stop=(dc == NDC - 1))
                        P.op("tensor", mv, waits=ww, inc=sPE if dc == NDC - 1 else None)
                    v_pe = sPE.n
                    if rb_v is None:
                        rb_v = emit_rb()
                    P.op("scalar", lambda e: e.activation(silA, p0, AF.Silu),
                         waits=[(sPE, g_pe), (sGP, sGP.n)], inc=sAC)
                    sil_gate = sAC.n
                    gate_p0_dv = 0
                    P.op("gpsimd", lambda e, sh=sh: e.tensor_mul(
                        silB, silA, rb_f[:, sh * 512:(sh + 1) * 512]),
                        waits=[(sAC, sil_gate), (sDV3, rb_v), (sDV, zmul_gate)], inc=sGP)
                    gp_v = sGP.n
                    zi = le * 8 + fb
                    zsl = zbuf[:, zi * 1024 + sh * 512:zi * 1024 + (sh + 1) * 512]
                    P.op("vector", lambda e, zsl=zsl: e.tensor_mul(zsl, p1, silB),
                         waits=[(sGP, gp_v), (sPE, v_pe), (sPE, z_pe_gate[zi])],
                         inc=sDV)
                    zmul_gate = sDV.n
                    val_p1_dv = sDV.n
                    z_slot_dv[zi] = sDV.n
                w12_pe_gate[bidx] = sPE.n
        z_all = sDV.n
        for dchunk in range(4):
            for (le, e_) in enumerate(grp):
                bidx = (dchunk * 2 + le) % 2
                dma("gpsimd", w3buf[bidx].rearrange("p (fc d) -> p fc d", fc=NFC),
                    w3n[e_].rearrange("(fc p) d -> p fc d", p=128)[:, :, dchunk * 512:(dchunk + 1) * 512],
                    sW3[bidx], waits=[(sPE, w3_pe_gate[bidx]), (sPE, wo_pe_end)])
            w3ld = [(sW3[0], sW3[0].n), (sW3[1], sW3[1].n)]
            for st in range(8):
                bk = st % 2
                psum = p0 if bk == 0 else p1
                nmm = 0
                for (le, e_) in enumerate(grp):
                    bidx = (dchunk * 2 + le) % 2
                    for fc in range(NFC):
                        ww = list(w3ld) + [(sDV, z_all)]
                        if nmm == 0:
                            ww.append((sDV, w3_acc_gate[bk]))
                        zi = le * 8 + fc

                        def m3(e, psum=psum, zi=zi, fc=fc, st=st, bidx=bidx, nmm=nmm):
                            return e.matmul(
                                psum,
                                zbuf[:, zi * 1024 + st * 128:zi * 1024 + (st + 1) * 128],
                                w3buf[bidx][:, fc * 512:(fc + 1) * 512],
                                start=(nmm == 0), stop=(nmm == 15))
                        P.op("tensor", m3, waits=ww, inc=sPE if nmm == 15 else None)
                        nmm += 1
                pe_v = sPE.n
                a_sl = acc[:, st * D + dchunk * 512:st * D + (dchunk + 1) * 512]
                P.op("vector", lambda e, a_sl=a_sl, psum=psum: e.tensor_add(
                    a_sl, psum, a_sl),
                    waits=[(sPE, pe_v), (sGP, h_ready_gp), (sOUT, tap_gate[0]),
                           (sDV, sDV.n)], inc=sDV)
                w3_acc_gate[bk] = sDV.n
            w3_pe_gate[0] = sPE.n
            w3_pe_gate[1] = sPE.n
        for zi in range(16):
            z_pe_gate[zi] = sPE.n
        gate_p0_dv = w3_acc_gate[0]
        val_p1_dv = w3_acc_gate[1]
        ffn_pe_end = sPE.n
    ffn_done_dv = sDV.n

    if "ffn" in taps:
        dma("sync", tap_out["tap_ffn"].rearrange("(st p) d -> p st d", p=128),
            acc.rearrange("p (st d) -> p st d", st=8), sOUT, waits=[(sDV, ffn_done_dv)])
        tap_gate[0] = sOUT.n

    if stop_after == "ffn":
        P.wait("sync", (sOUT, sOUT.n))
        return _finish(nc, P, stack, tap_out)

    stg_pe_gate[0] = 0
    stg_gp_gate[0] = 0
    emit_ln([(sDV, ffn_done_dv)], write_hbf=False, out_to=outT)

    P.wait("sync", (sOUT, sOUT.n))

    return _finish(nc, P, stack, tap_out)


def _finish(nc, P, stack, tap_out):
    with nc.Block() as block:
        P.replay(block)
    stack.close()
    return nc, list(tap_out.keys())


_HPERM = np.concatenate([np.arange(0, 128, 2), np.arange(1, 128, 2)])


def _host_inputs(L0_attn_x, Wq, Wk, Wv, Wo, router_w, w1, w2, w3):
    x = np.asarray(L0_attn_x, dtype=np.float32)
    Wq = np.asarray(Wq, np.float32); Wk = np.asarray(Wk, np.float32)
    Wv = np.asarray(Wv, np.float32); Wo = np.asarray(Wo, np.float32)
    router_w = np.asarray(router_w, np.float32)
    w1 = np.asarray(w1, np.float32); w2 = np.asarray(w2, np.float32)
    w3 = np.asarray(w3, np.float32)

    xT = [np.ascontiguousarray(x[b].T) for b in range(B)]
    j = np.arange(64, dtype=np.float64)
    inv = np.power(10000.0, -2.0 * j / DK)
    pos = np.arange(S, dtype=np.float64)
    ang = pos[None, :] * inv[:, None]
    cosv = np.cos(ang); sinv = np.sin(ang)
    c2 = np.concatenate([cosv, cosv], 0).astype(np.float32)
    s2 = np.concatenate([-sinv, sinv], 0).astype(np.float32)
    csh = np.ascontiguousarray(np.stack([c2, s2], 0))
    ii = np.arange(128)[:, None]
    jj = np.arange(512)[None, :]
    mskh = np.ascontiguousarray(np.stack(
        [np.where(jj <= qt * 128 + ii, 0.0, -1e30).astype(np.float32)
         for qt in range(NQT)], 0))
    idin = np.eye(128, dtype=np.float32)
    w1th = np.ascontiguousarray(
        w1.reshape(NE, NDC, 128, NFB, 128).transpose(0, 3, 2, 1, 4))
    w2th = np.ascontiguousarray(
        w2.reshape(NE, NDC, 128, NFB, 128).transpose(0, 3, 2, 1, 4))

    in_maps = []
    for c in range(8):
        b, r = c // 4, c % 4
        heads = [4 * r + i for i in range(NH)]
        qrows = np.concatenate([h * 128 + _HPERM for h in heads])
        vrows = np.concatenate([h * 128 + np.arange(128) for h in heads])
        in_maps.append({
            "xT": xT[b],
            "xo": np.ascontiguousarray(x[b][r * NT:(r + 1) * NT]),
            "wqT": np.ascontiguousarray(Wq[qrows].T),
            "wkT": np.ascontiguousarray(Wk[qrows].T),
            "wvT": np.ascontiguousarray(Wv[vrows].T),
            "woP": np.ascontiguousarray(Wo[:, vrows].T),
            "rwT": np.ascontiguousarray(router_w.T),
            "w1t": w1th, "w2t": w2th, "w3n": w3,
            "cs": csh, "msk": mskh, "idin": idin,
        })
    return in_maps


def kernel(L0_attn_x, L0_attn_Wq, L0_attn_Wk, L0_attn_Wv, L0_attn_Wo,
           L0_ffn_router_w, L0_ffn_w1, L0_ffn_w2, L0_ffn_w3):
    in_maps = _host_inputs(L0_attn_x, L0_attn_Wq, L0_attn_Wk, L0_attn_Wv,
                           L0_attn_Wo, L0_ffn_router_w, L0_ffn_w1, L0_ffn_w2,
                           L0_ffn_w3)
    nc, _ = build_program()
    res = run_bass_kernel_spmd(nc, in_maps, core_ids=list(range(8)))
    out = np.empty((B, S, D), np.float32)
    for c in range(8):
        b, r = c // 4, c % 4
        out[b, r * NT:(r + 1) * NT, :] = res.results[c]["outT"]
    return out


# version 05
# speedup vs baseline: 1.1165x; 1.0341x over previous
import math
import numpy as np
import ml_dtypes
import concourse.bass as bass
import concourse.mybir as mybir
from concourse.bass_utils import run_bass_kernel_spmd

F32 = mybir.dt.float32
BF16 = mybir.dt.bfloat16
AF = mybir.ActivationFunctionType
ALU = mybir.AluOpType
AX = mybir.AxisListType

B, S, D, H, DK = 2, 4096, 2048, 16, 128
NH = 4
NT = 1024
NE, DF = 8, 1024
NDC = D // 128
NSC = S // 512
NQC = S // 512
NQT = 4
NFC = DF // 128
NFB = 8
EGRPS = [(0, 1), (2, 3), (4, 5), (6, 7)]
ISQ = 1.0 / math.sqrt(DK)
EPS = 1e-5
ABLATE = set()
ENGS = ("sync", "tensor", "vector", "scalar", "gpsimd")


class Sem:
    def __init__(self, h):
        self.h = h
        self.n = 0


class Prog:
    def __init__(self):
        self.items = {e: [] for e in ENGS}
        self.last_wait = {}

    def op(self, eng_name, build_fn, waits=(), inc=None, inc_by=1):
        wl = []
        for (s, v) in waits:
            if v <= 0:
                continue
            key = (eng_name, id(s))
            if self.last_wait.get(key, -1) < v:
                self.last_wait[key] = v
                wl.append((s, v))
        if inc is not None:
            inc.n += inc_by
        wl = tuple(wl)

        def f(eng, build_fn=build_fn, wl=wl, inc=inc, inc_by=inc_by):
            for (s, v) in wl:
                eng.wait_ge(s.h, v)
            ins = build_fn(eng)
            if inc is not None and ins is not None:
                ins.then_inc(inc.h, inc_by)
        self.items[eng_name].append(f)
        return inc.n if inc is not None else 0

    def wait(self, eng_name, *waits):
        self.op(eng_name, lambda e: None, waits=waits)

    def replay(self, block):
        for name in ENGS:
            ops = self.items[name]
            if not ops:
                continue

            def body(eng, ops=ops):
                for f in ops:
                    f(eng)
            getattr(block, name)(body)


def build_program(taps=(), stop_after=None):
    taps = set(taps)
    nc = bass.Bass("TRN2", target_bir_lowering=False, debug=False, num_devices=8)

    xT = nc.dram_tensor("xT", [D, S], F32, kind="ExternalInput").ap()
    xo = nc.dram_tensor("xo", [NT, D], F32, kind="ExternalInput").ap()
    wqT = nc.dram_tensor("wqT", [D, NH * DK], F32, kind="ExternalInput").ap()
    wkT = nc.dram_tensor("wkT", [D, NH * DK], F32, kind="ExternalInput").ap()
    wvT = nc.dram_tensor("wvT", [D, NH * DK], F32, kind="ExternalInput").ap()
    woP = nc.dram_tensor("woP", [NH * DK, D], F32, kind="ExternalInput").ap()
    rwT = nc.dram_tensor("rwT", [D, NE], F32, kind="ExternalInput").ap()
    w1t = nc.dram_tensor("w1t", [NE, NFB, 128, NDC, 128], F32, kind="ExternalInput").ap()
    w2t = nc.dram_tensor("w2t", [NE, NFB, 128, NDC, 128], F32, kind="ExternalInput").ap()
    w3n = nc.dram_tensor("w3n", [NE, DF, D], F32, kind="ExternalInput").ap()
    cs = nc.dram_tensor("cs", [2, 128, S], F32, kind="ExternalInput").ap()
    msk = nc.dram_tensor("msk", [NQT, 128, 512], F32, kind="ExternalInput").ap()
    idin = nc.dram_tensor("idin", [128, 128], F32, kind="ExternalInput").ap()
    outT = nc.dram_tensor("outT", [NT, D], F32, kind="ExternalOutput").ap()

    qb = nc.dram_tensor("qb", [NH, 128, S], BF16, kind="Internal").ap()
    kb = nc.dram_tensor("kb", [NH, 128, S], BF16, kind="Internal").ap()
    vb = nc.dram_tensor("vb", [NH, S, 128], BF16, kind="Internal").ap()
    rsin = [nc.dram_tensor(f"rsin{i}", [S, 512], F32, kind="Internal").ap() for i in range(4)]
    rsout = [nc.dram_tensor(f"rsout{i}", [NT, 512], F32, kind="Internal").ap() for i in range(4)]

    tap_out = {}
    if "qkv" in taps:
        tap_out["tap_q"] = nc.dram_tensor("tap_q", [NH, 128, S], BF16, kind="ExternalOutput").ap()
        tap_out["tap_k"] = nc.dram_tensor("tap_k", [NH, 128, S], BF16, kind="ExternalOutput").ap()
        tap_out["tap_v"] = nc.dram_tensor("tap_v", [NH, S, 128], BF16, kind="ExternalOutput").ap()
    if "ctx" in taps:
        tap_out["tap_ctx"] = nc.dram_tensor("tap_ctx", [128, NH * S], BF16, kind="ExternalOutput").ap()
    if "rs" in taps:
        tap_out["tap_rs"] = nc.dram_tensor("tap_rs", [4, NT, 512], F32, kind="ExternalOutput").ap()
    if "h" in taps:
        tap_out["tap_h"] = nc.dram_tensor("tap_h", [NT, D], F32, kind="ExternalOutput").ap()
    if "r" in taps:
        tap_out["tap_r"] = nc.dram_tensor("tap_r", [128, 64], F32, kind="ExternalOutput").ap()
    if "ffn" in taps:
        tap_out["tap_ffn"] = nc.dram_tensor("tap_ffn", [NT, D], F32, kind="ExternalOutput").ap()

    ab = nc.alloc_sbuf_tensor("arena_bf", [128, 49152], BF16).ap()
    af = nc.alloc_sbuf_tensor("arena_f32", [128, 11264], F32).ap()
    acc = nc.alloc_sbuf_tensor("acc", [128, 8 * D], F32).ap()
    identb = nc.alloc_sbuf_tensor("identb", [128, 128], BF16).ap()
    identf = nc.alloc_sbuf_tensor("identf", [128, 128], F32).ap()
    ones1f = nc.alloc_sbuf_tensor("ones1f", [1, 128], F32).ap()
    rw_s = nc.alloc_sbuf_tensor("rw_s", [128, NDC * NE], BF16).ap()
    rrec_s = nc.alloc_sbuf_tensor("rrec_s", [128, 8], F32).ap()
    rT_s = nc.alloc_sbuf_tensor("rT_s", [1, 512], F32).ap()

    wq_s = ab[:, 0:8192]
    wk_s = ab[:, 8192:16384]
    wv_s = ab[:, 16384:24576]
    xbf = [ab[:, 24576 + i * 8192:24576 + (i + 1) * 8192] for i in (0, 1)]
    qstage = ab[:, 40960:43008]
    kstage = ab[:, 43008:45056]
    vstage = ab[:, 45056:47104]
    cst = ab[:, 0:16384]
    qh = ab[:, 16384:20480]
    kh = ab[:, 20480:24576]
    vh = ab[:, 24576:28672]
    attn = ab[:, 28672:45056]
    atb = ab[:, 45056:47104]
    woP_s = ab[:, 28672:36864]
    zbuf = ab[:, 0:16384]
    h_bf = ab[:, 16384:32768]
    w1buf = [ab[:, 32768 + i * 2048:32768 + (i + 1) * 2048] for i in (0, 1)]
    w2buf = [ab[:, 36864 + i * 2048:36864 + (i + 1) * 2048] for i in (0, 1)]
    w3buf = [ab[:, 40960:45056], ab[:, 45056:49152],
             ab[:, 32768:36864], ab[:, 36864:40960]]

    ropB = af[:, 0:512]
    c2 = af[:, 2048:6144]
    s2 = af[:, 6144:10240]
    swp = af[:, 10240:10752]
    ropA = af[:, 10752:11264]
    msk_s = af[:, 0:2048]
    scd = af[:, 10240:10752]
    rs = af[:, 10752:10816]
    recb = af[:, 2048:2560]
    wostg = [af[:, 2048 + i * 2048:2048 + (i + 1) * 2048] for i in (0, 1)]
    rT2 = af[0:1, 0:1024]
    tiny = af[:, 1024:1088]
    rta = af[:, 1088:1152]
    lg = af[:, 1152:1160]
    m1 = af[:, 1160:1161]
    l2 = af[:, 1168:1176]
    m2 = af[:, 1176:1177]
    dd = af[:, 1177:1178]
    wA = af[:, 1178:1179]
    wB = af[:, 1179:1180]
    mk1 = af[:, 1184:1192]
    mk2 = af[:, 1192:1200]
    sqs = af[:, 2048:4096]
    stg = af[:, 4096:6144]
    silA = af[:, 6144:6656]
    silB = af[:, 6656:7168]
    rb_f = af[:, 7168:8192]

    p0 = nc.alloc_psum_tensor("p0", [128, 512], F32).ap()
    p1 = nc.alloc_psum_tensor("p1", [128, 512], F32).ap()
    p2 = nc.alloc_psum_tensor("p2", [128, 512], F32).ap()
    p7 = nc.alloc_psum_tensor("p7", [128, 512], F32).ap()
    p3 = nc.alloc_psum_tensor("p3", [128, 512], F32).ap()
    t0 = nc.alloc_psum_tensor("t0", [128, 512], BF16).ap()
    t1 = nc.alloc_psum_tensor("t1", [128, 512], BF16).ap()
    t2 = nc.alloc_psum_tensor("t2", [128, 512], BF16).ap()

    import contextlib
    stack = contextlib.ExitStack()
    SN = ["sDM", "sGD", "sGD2", "sPE", "sPE2", "sDV", "sDV2", "sDV3",
          "sAC", "sAC2", "sGP", "sCC", "sOUT",
          "sXB0", "sXB1", "sSTQ", "sSTK", "sSTV", "sHLD",
          "sWS0", "sWS1", "sW10", "sW11", "sW30", "sW31", "sW32", "sW33"]
    sem = {n: Sem(stack.enter_context(nc.semaphore(n))) for n in SN}
    sDM, sGD, sGD2 = sem["sDM"], sem["sGD"], sem["sGD2"]
    sPE, sPE2 = sem["sPE"], sem["sPE2"]
    sDV, sDV2, sDV3 = sem["sDV"], sem["sDV2"], sem["sDV3"]
    sAC, sAC2, sGP = sem["sAC"], sem["sAC2"], sem["sGP"]
    sCC, sOUT = sem["sCC"], sem["sOUT"]
    sXB = [sem["sXB0"], sem["sXB1"]]
    sSTQ, sSTK, sSTV = sem["sSTQ"], sem["sSTK"], sem["sSTV"]
    sHLD = sem["sHLD"]
    sWS = [sem["sWS0"], sem["sWS1"]]
    sW1 = [sem["sW10"], sem["sW11"]]
    sW3 = [sem["sW30"], sem["sW31"], sem["sW32"], sem["sW33"]]

    P = Prog()

    def dma(eng, out, in_, s, waits=(), **kw):
        return P.op(eng, lambda e: e.dma_start(out=out, in_=in_, **kw),
                    waits=waits, inc=s, inc_by=16)

    dma("gpsimd", identb[:, :], idin, sGD)
    dma("gpsimd", rw_s.rearrange("p (dc e) -> p dc e", dc=NDC),
        rwT.rearrange("(dc p) e -> p dc e", p=128), sGD)
    dma("gpsimd", wq_s.rearrange("p (dc o) -> p dc o", dc=NDC),
        wqT.rearrange("(dc p) o -> p dc o", p=128), sGD)
    dma("gpsimd", wk_s.rearrange("p (dc o) -> p dc o", dc=NDC),
        wkT.rearrange("(dc p) o -> p dc o", p=128), sGD)
    dma("gpsimd", wv_s.rearrange("p (dc o) -> p dc o", dc=NDC),
        wvT.rearrange("(dc p) o -> p dc o", p=128), sGD)
    dma("sync", identf[:, :], idin, sDM)
    dma("sync", c2, cs[0], sDM)
    dma("sync", s2, cs[1], sDM)
    P.op("gpsimd", lambda e: e.memset(ones1f[:, :], 1.0), inc=sGP)
    prolog_gd, prolog_dm, prolog_gp = sGD.n, sDM.n, sGP.n

    if stop_after == "P":
        P.wait("sync", (sGD, sGD.n), (sDM, sDM.n), (sGD2, sGD2.n), (sGP, sGP.n))
        return _finish(nc, P, stack, tap_out)

    xld = []
    xfree = []
    bank_free = {0: (0, 0), 1: (0, 0), 2: (0, 0)}
    stage_dm = {"q": (sSTQ, 0), "k": (sSTK, 0), "v": (sSTV, 0)}
    swp_gate = 0
    ropB_gate = 0
    for sc in range(NSC):
        bi = sc % 2
        w = [(sGD, prolog_gd), (sDM, prolog_dm)]
        if sc >= 2:
            w.append((sPE, xfree[sc - 2]))
        dma("gpsimd", xbf[bi].rearrange("p (dc c) -> p dc c", dc=NDC),
            xT.rearrange("(dc p) s -> p dc s", p=128)[:, :, sc * 512:(sc + 1) * 512],
            sXB[bi], waits=w)
        xld.append((sXB[bi], sXB[bi].n))
        col = sc * 512
        gi = 0
        for (stage, w_s, name, dst) in (() if "noqk" in ABLATE else ((qstage, wq_s, "q", qb), (kstage, wk_s, "k", kb))):
            fin = []
            for ot in range(NH):
                psum = (p0, p1, p3)[gi % 3]
                bfree = bank_free[gi % 3]
                for dc in range(NDC):
                    ww = [xld[sc]]
                    if dc == 0:
                        ww += [(sAC, bfree[0]), (sDV, bfree[1])]

                    def mm(e, psum=psum, w_s=w_s, dc=dc, ot=ot, bi=bi):
                        return e.matmul(
                            psum, w_s[:, dc * 512 + ot * 128:dc * 512 + (ot + 1) * 128],
                            xbf[bi][:, dc * 512:(dc + 1) * 512],
                            start=(dc == 0), stop=(dc == NDC - 1))
                    P.op("tensor", mm, waits=ww, inc=sPE if dc == NDC - 1 else None)
                pe_v = sPE.n
                if "norope" in ABLATE:
                    sl = stage[:, ot * 512:(ot + 1) * 512]
                    P.op("vector", lambda e, sl=sl, psum=psum: e.tensor_copy(sl, psum),
                         waits=[(sPE, pe_v), stage_dm[name], (sDV, ropB_gate)], inc=sDV)
                    ropB_gate = sDV.n
                    fin.append(sDV.n)
                    bank_free[gi % 3] = (sAC.n, sDV.n)
                    gi += 1
                    continue
                P.op("scalar", lambda e, psum=psum: e.copy(swp[0:64, :], psum[64:128, :]),
                     waits=[(sPE, pe_v), (sGP, swp_gate)], inc=sAC)
                P.op("scalar", lambda e, psum=psum: e.copy(swp[64:128, :], psum[0:64, :]),
                     inc=sAC)
                ac_v = sAC.n
                P.op("vector", lambda e, psum=psum, col=col: e.tensor_mul(
                    ropA, psum, c2[:, col:col + 512]),
                    waits=[(sPE, pe_v), (sDM, prolog_dm), (sDV, ropB_gate),
                           (sAC, ac_v)], inc=sDV)
                dv_a = sDV.n
                if "r3" in ABLATE:
                    P.op("gpsimd", lambda e, col=col: e.tensor_mul(
                        ropB, c2[:, col:col + 512], s2[:, col:col + 512]),
                        waits=[(sAC, ac_v), (sDV, ropB_gate)], inc=sGP)
                else:
                    P.op("gpsimd", lambda e, col=col: e.tensor_mul(
                        ropB, swp, s2[:, col:col + 512]),
                        waits=[(sAC, ac_v), (sDV, ropB_gate)], inc=sGP)
                swp_gate = sGP.n
                sl = stage[:, ot * 512:(ot + 1) * 512]
                P.op("vector", lambda e, sl=sl: e.tensor_add(sl, ropA, ropB),
                     waits=[(sDV, dv_a), (sGP, sGP.n), stage_dm[name]], inc=sDV)
                ropB_gate = sDV.n
                fin.append(sDV.n)
                bank_free[gi % 3] = (sAC.n, sDV.n)
                gi += 1
            sch = sSTQ if name == "q" else sSTK
            if "nostage" not in ABLATE:
                dma("sync", dst.rearrange("h p s -> p h s")[:, :, sc * 512:(sc + 1) * 512],
                    stage.rearrange("p (h c) -> p h c", h=NH), sch, waits=[(sDV, fin[-1])])
                stage_dm[name] = (sch, sch.n)
        vfin = []
        for st in range(0 if "nov" in ABLATE else 4):
            psum = (p0, p1, p3)[gi % 3]
            bfree = bank_free[gi % 3]
            for dc in range(NDC):
                ww = [xld[sc]]
                if dc == 0:
                    ww += [(sAC, bfree[0]), (sDV, bfree[1])]

                def mmv(e, psum=psum, dc=dc, st=st, bi=bi):
                    return e.matmul(
                        psum, xbf[bi][:, dc * 512 + st * 128:dc * 512 + (st + 1) * 128],
                        wv_s[:, dc * 512:(dc + 1) * 512],
                        start=(dc == 0), stop=(dc == NDC - 1))
                P.op("tensor", mmv, waits=ww, inc=sPE if dc == NDC - 1 else None)
            pe_v = sPE.n
            P.op("vector", lambda e, psum=psum, st=st: e.tensor_copy(
                vstage[:, st * 512:(st + 1) * 512], psum),
                waits=[(sPE, pe_v), stage_dm["v"]], inc=sDV)
            vfin.append(sDV.n)
            bank_free[gi % 3] = (sAC.n, sDV.n)
            gi += 1
        for hh in range(0 if ("nov" in ABLATE or "nostage" in ABLATE) else NH):
            dma("sync",
                vb[hh].rearrange("(st p) d -> p st d", p=128)[:, sc * 4:(sc + 1) * 4, :],
                vstage.rearrange("p (st h d) -> p st h d", st=4, h=NH)[:, :, hh, :],
                sSTV, waits=[(sDV, vfin[-1])])
        stage_dm["v"] = (sSTV, sSTV.n)
        xfree.append(sPE.n)
    phA_bar = [(sSTQ, sSTQ.n), (sSTK, sSTK.n), (sSTV, sSTV.n)]
    phA_pe = sPE.n

    if "qkv" in taps:
        dma("sync", tap_out["tap_q"].rearrange("h p s -> (h p) s"),
            qb.rearrange("h p s -> (h p) s"), sOUT, waits=phA_bar)
        dma("sync", tap_out["tap_k"].rearrange("h p s -> (h p) s"),
            kb.rearrange("h p s -> (h p) s"), sOUT, waits=phA_bar)
        dma("sync", tap_out["tap_v"].rearrange("h s d -> (h s) d"),
            vb.rearrange("h s d -> (h s) d"), sOUT, waits=phA_bar)

    if stop_after == "A":
        P.wait("sync", (sOUT, sOUT.n))
        return _finish(nc, P, stack, tap_out)

    dma("sync", msk_s.rearrange("p (q c) -> p q c", q=NQT),
        msk.rearrange("q p c -> p q c"), sDM,
        waits=[(sDV, sDV.n), (sGP, sGP.n)])
    mask_dm = sDM.n
    dma("sync", acc.rearrange("p (st d) -> p st d", st=8),
        xo.rearrange("(st p) d -> p st d", p=128), sDM)
    xo_dm = sDM.n

    head_pe_end = phA_pe
    ctx_evac_dv = 0
    exp_attn_gate = 0
    rs_gate = 0
    recb_v = 0
    for hl in range(NH):
        dma("sync", qh, qb[hl], sHLD,
            waits=phA_bar + [(sPE, head_pe_end)])
        dma("sync", kh, kb[hl], sHLD)
        dma("sync", vh.rearrange("p (st d) -> p st d", st=32),
            vb[hl].rearrange("(st p) d -> p st d", p=128), sHLD)
        hload = sHLD.n
        sc_gate = {0: (0, 0), 1: (0, 0), 2: (0, 0)}
        for qc in range(NQC):
            nkc = qc + 1
            exp_done = {}
            for qt in range(NQT):
                for kc in range(nkc):
                    bk = kc % 3
                    psum = (p0, p1, p3)[bk]
                    ww = [(sHLD, hload), (sAC, sc_gate[bk][0]), (sDV, sc_gate[bk][1])]

                    def mms(e, psum=psum, qc=qc, qt=qt, kc=kc):
                        return e.matmul(
                            psum, qh[:, qc * 512 + qt * 128:qc * 512 + (qt + 1) * 128],
                            kh[:, kc * 512:(kc + 1) * 512], start=True, stop=True)
                    P.op("tensor", mms, waits=ww, inc=sPE)
                    pe_v = sPE.n
                    acol = qt * 4096 + kc * 512
                    slot = rs[:, (qt * 16 + kc):(qt * 16 + kc) + 1]
                    if kc == qc:
                        P.op("vector", lambda e, psum=psum, qt=qt: e.tensor_add(
                            scd, psum, msk_s[:, qt * 512:(qt + 1) * 512]),
                            waits=[(sPE, pe_v), (sDM, xo_dm), (sAC, sAC.n)], inc=sDV)
                        dv_v = sDV.n
                        P.op("scalar", lambda e, acol=acol, slot=slot: e.activation(
                            attn[:, acol:acol + 512], scd, AF.Exp, scale=ISQ,
                            accum_out=slot),
                            waits=[(sDV, dv_v), (sPE2, exp_attn_gate), (sDV2, rs_gate)],
                            inc=sAC)
                        sc_gate[bk] = (sc_gate[bk][0], dv_v)
                    else:
                        P.op("scalar", lambda e, psum=psum, acol=acol, slot=slot: e.activation(
                            attn[:, acol:acol + 512], psum, AF.Exp, scale=ISQ,
                            accum_out=slot),
                            waits=[(sPE, pe_v), (sPE2, exp_attn_gate), (sDV2, rs_gate)],
                            inc=sAC)
                        sc_gate[bk] = (sAC.n, sc_gate[bk][1])
                    exp_done[(qt, kc)] = sAC.n
            for qt in range(NQT):
                P.op("vector", lambda e, qt=qt, nkc=nkc: e.reduce_sum(
                    rrec_s[:, qt:qt + 1], rs[:, qt * 16:qt * 16 + nkc], axis=AX.X),
                    waits=[(sAC, exp_done[(qt, nkc - 1)]), (sDV2, sDV2.n)], inc=sDV2)
            rs_gate = sDV2.n
            P.op("vector", lambda e: e.reciprocal(rrec_s[:, 4:8], rrec_s[:, 0:4]),
                 waits=[(sDV2, sDV2.n)], inc=sDV2)
            rec_v = sDV2.n
            tp_gate = {0: sDV.n, 1: sDV.n, 2: sDV.n}
            atb_gate = [sPE.n] * 4
            tbc = 0
            for kc in range(nkc):
                for skb in range(4):
                    tb = tbc % 3
                    tbc += 1
                    tp = (t0, t1, t2)[tb]
                    for qt in range(NQT):
                        acol = qt * 4096 + kc * 512 + skb * 128
                        ww = [(sAC, exp_done[(qt, kc)])]
                        if qt == 0:
                            ww.append((sDV, tp_gate[tb]))

                        def mtr(e, tp=tp, qt=qt, acol=acol):
                            return e.transpose(tp[:, qt * 128:(qt + 1) * 128],
                                               attn[:, acol:acol + 128], identb[:, :])
                        P.op("tensor", mtr, waits=ww,
                             inc=sPE2 if qt == NQT - 1 else None)
                    tr_v = sPE2.n
                    P.op("vector", lambda e, tp=tp, skb=skb: e.tensor_copy(
                        atb[:, skb * 512:(skb + 1) * 512], tp),
                        waits=[(sPE2, tr_v), (sPE, atb_gate[skb])], inc=sDV)
                    ev_v = sDV.n
                    tp_gate[tb] = ev_v

                    def mc(e, kc=kc, skb=skb,
                           first=(kc == 0 and skb == 0),
                           last=(kc == nkc - 1 and skb == 3)):
                        return e.matmul(p2,
                                        vh[:, (kc * 4 + skb) * 128:(kc * 4 + skb + 1) * 128],
                                        atb[:, skb * 512:(skb + 1) * 512],
                                        start=first, stop=last)
                    ww = [(sDV, ev_v)]
                    if kc == 0 and skb == 0:
                        ww.append((sDV, ctx_evac_dv))
                    P.op("tensor", mc, waits=ww, inc=sPE)
                    atb_gate[skb] = sPE.n
            for qt in range(NQT):
                P.op("tensor", lambda e, qt=qt: e.transpose(
                    p7[0:1, qt * 128:(qt + 1) * 128], rrec_s[:, 4 + qt:5 + qt],
                    identf[:, :]),
                    waits=[(sDV2, rec_v), (sDV3, sDV3.n)], inc=sPE2)
            P.op("vector", lambda e: e.tensor_copy(rT_s[:, :], p7[0:1, 0:512]),
                 waits=[(sPE2, sPE2.n)], inc=sDV3)
            P.op("tensor", lambda e: e.matmul(p7, ones1f[:, :], rT_s[:, :],
                                              start=True, stop=True),
                 waits=[(sDV3, sDV3.n)], inc=sPE2)
            P.op("vector", lambda e: e.tensor_copy(recb, p7),
                 waits=[(sPE2, sPE2.n)], inc=sDV3)
            recb_v = sDV3.n
            exp_attn_gate = sPE2.n
            P.op("vector", lambda e, hl=hl, qc=qc: e.tensor_mul(
                cst[:, hl * 4096 + qc * 512:hl * 4096 + (qc + 1) * 512], p2, recb),
                waits=[(sPE, sPE.n), (sDV3, recb_v)], inc=sDV)
            ctx_evac_dv = sDV.n
        head_pe_end = sPE.n

    if "ctx" in taps:
        dma("sync", tap_out["tap_ctx"], cst, sOUT, waits=[(sDV, ctx_evac_dv)])

    if stop_after == "B":
        P.wait("sync", (sOUT, sOUT.n))
        return _finish(nc, P, stack, tap_out)

    dma("gpsimd", woP_s.rearrange("p (dc o) -> p dc o", dc=NH),
        woP.rearrange("(dc p) o -> p dc o", p=128), sGD,
        waits=[(sAC, sAC.n), (sPE2, sPE2.n), (sPE, head_pe_end)])
    wop_gd = sGD.n
    cst_ready = ctx_evac_dv
    wo_gate = {0: sc_gate[0][0], 1: sc_gate[1][0]}
    wo_dv_gate = {0: 0, 1: 0}
    wostg_dm = [0, 0]
    rs_cc = []
    for oc in range(4):
        for sq in range(8):
            sg = sq % 2
            for j in range(4):
                st = sq * 4 + j
                bk = st % 2
                psum = p0 if bk == 0 else p1
                for hl in range(NH):
                    ww = [(sGD, wop_gd), (sDV, cst_ready)]
                    if hl == 0:
                        ww += [(sDV, wo_dv_gate[bk]), (sWS[sg], wostg_dm[sg])]

                    def mw(e, psum=psum, hl=hl, st=st, oc=oc):
                        return e.matmul(
                            psum, cst[:, hl * 4096 + st * 128:hl * 4096 + (st + 1) * 128],
                            woP_s[:, hl * 2048 + oc * 512:hl * 2048 + (oc + 1) * 512],
                            start=(hl == 0), stop=(hl == NH - 1))
                    P.op("tensor", mw, waits=ww, inc=sPE if hl == NH - 1 else None)
                pe_v = sPE.n
                P.op("vector", lambda e, psum=psum, sg=sg, j=j: e.tensor_copy(
                    wostg[sg][:, j * 512:(j + 1) * 512], psum),
                    waits=[(sPE, pe_v), (sWS[sg], wostg_dm[sg])], inc=sDV)
                wo_dv_gate[bk] = sDV.n
            dma("sync",
                rsin[oc].rearrange("(q p) c -> p q c", p=128)[:, sq * 4:(sq + 1) * 4, :],
                wostg[sg].rearrange("p (j c) -> p j c", j=4), sWS[sg],
                waits=[(sDV, sDV.n)])
            wostg_dm[sg] = sWS[sg].n
        P.op("gpsimd", lambda e, oc=oc: e.collective_compute(
            "ReduceScatter", ALU.add, replica_groups=[[0, 1, 2, 3], [4, 5, 6, 7]],
            ins=[rsin[oc]], outs=[rsout[oc]]),
            waits=[(sWS[0], sWS[0].n), (sWS[1], sWS[1].n)], inc=sCC)
        rs_cc.append(sCC.n)
    wo_pe_end = sPE.n

    if "rs" in taps:
        for oc in range(4):
            dma("sync", tap_out["tap_rs"][oc], rsout[oc], sOUT, waits=[(sCC, rs_cc[oc])])

    for oc in range(4):
        dma("gpsimd", acc.rearrange("p (st d) -> p st d", st=8)[:, :, oc * 512:(oc + 1) * 512],
            rsout[oc].rearrange("(st p) d -> p st d", p=128), sGD,
            waits=[(sCC, rs_cc[oc]), (sDM, xo_dm)], accum_op=ALU.add)
    acc_gd = sGD.n

    if stop_after == "wo":
        P.wait("sync", (sOUT, sOUT.n))
        return _finish(nc, P, stack, tap_out)

    tap_gate = [0]
    stg_pe_gate = [0]
    tp_c_gate = {0: sDV3.n, 1: sDV3.n}
    stg_gp_gate = [0]

    def emit_ln(src_gates, write_hbf, out_to=None):
        for st in range(8):
            a_sl = acc[:, st * D:(st + 1) * D]
            mu = tiny[:, st * 8 + 0:st * 8 + 1]
            ng = tiny[:, st * 8 + 1:st * 8 + 2]
            ms = tiny[:, st * 8 + 2:st * 8 + 3]
            vv = tiny[:, st * 8 + 3:st * 8 + 4]
            sq = tiny[:, st * 8 + 4:st * 8 + 5]
            sd = tiny[:, st * 8 + 5:st * 8 + 6]
            rho = tiny[:, st * 8 + 6:st * 8 + 7]
            P.op("scalar", lambda e, a_sl=a_sl, sq=sq: e.activation(
                sqs, a_sl, AF.Square, accum_out=sq),
                waits=list(src_gates) + [(sAC2, sAC2.n)], inc=sAC2)
            sq_v = sAC2.n
            P.op("vector", lambda e, a_sl=a_sl, mu=mu: e.reduce_sum(mu, a_sl, axis=AX.X),
                 waits=list(src_gates) + [(sDV2, sDV2.n)], inc=sDV2)
            P.op("vector", lambda e, ng=ng, mu=mu: e.tensor_scalar_mul(ng, mu, -1.0 / D),
                 waits=[(sDV2, sDV2.n)], inc=sDV2)
            ng_v = sDV2.n
            P.op("gpsimd", lambda e, ms=ms, ng=ng: e.tensor_mul(ms, ng, ng),
                 waits=[(sDV2, ng_v), (sGP, sGP.n)], inc=sGP)
            ms_v = sGP.n
            P.op("vector", lambda e, vv=vv, sq=sq: e.tensor_scalar_mul(vv, sq, 1.0 / D),
                 waits=[(sAC2, sq_v), (sDV2, sDV2.n)], inc=sDV2)
            P.op("vector", lambda e, vv=vv, ms=ms: e.scalar_tensor_tensor(
                vv, vv, EPS, ms, op0=ALU.add, op1=ALU.subtract),
                 waits=[(sGP, ms_v), (sDV2, sDV2.n)], inc=sDV2)
            vv_v = sDV2.n
            P.op("scalar", lambda e, sd=sd, vv=vv: e.activation(sd, vv, AF.Sqrt),
                 waits=[(sDV2, vv_v), (sAC2, sAC2.n)], inc=sAC2)
            sd_v = sAC2.n
            P.op("vector", lambda e, rho=rho, sd=sd: e.reciprocal(rho, sd),
                 waits=[(sAC2, sd_v), (sDV2, sDV2.n)], inc=sDV2)
            P.op("vector", lambda e, a_sl=a_sl, ng=ng, rho=rho: e.tensor_scalar(
                stg, a_sl, ng, rho, op0=ALU.add, op1=ALU.mult),
                waits=[(sDV2, sDV2.n), (sOUT, tap_gate[0]),
                       (sPE2, stg_pe_gate[0]), (sGP, stg_gp_gate[0])], inc=sDV2)
            nrm_v = sDV2.n
            if write_hbf:
                for dc in range(NDC):
                    bk = dc % 2
                    pp = p2 if bk == 0 else p7
                    ww = [(sDV2, nrm_v), (sDV3, tp_c_gate[bk]), (sDV, ctx_evac_dv)]

                    def mt(e, pp=pp, dc=dc):
                        return e.transpose(pp[:, 0:128], stg[:, dc * 128:(dc + 1) * 128],
                                           identf[:, :])
                    P.op("tensor", mt, waits=ww, inc=sPE2)
                    tr_v = sPE2.n
                    P.op("vector", lambda e, pp=pp, dc=dc, st=st: e.tensor_copy(
                        h_bf[:, dc * 1024 + st * 128:dc * 1024 + (st + 1) * 128],
                        pp[:, 0:128]),
                        waits=[(sPE2, tr_v), (sDV3, sDV3.n)], inc=sDV3)
                    tp_c_gate[bk] = sDV3.n
                stg_pe_gate[0] = sPE2.n
                P.op("gpsimd", lambda e, a_sl=a_sl: e.tensor_copy(a_sl, stg),
                     waits=[(sDV2, nrm_v), (sGP, sGP.n), (sDV3, sDV3.n)], inc=sGP)
                stg_gp_gate[0] = sGP.n
            else:
                dma("sync", out_to[st * 128:(st + 1) * 128, :], stg, sOUT,
                    waits=[(sDV2, nrm_v)])
                tap_gate[0] = sOUT.n
                stg_gp_gate[0] = 0
                stg_pe_gate[0] = 0
                P.wait("vector", (sOUT, sOUT.n))

    emit_ln([(sGD, acc_gd)], write_hbf=True)
    h_ready_gp = sGP.n
    h_bf_ready = sDV3.n

    if "h" in taps:
        dma("sync", tap_out["tap_h"].rearrange("(st p) d -> p st d", p=128),
            acc.rearrange("p (st d) -> p st d", st=8), sOUT,
            waits=[(sGP, h_ready_gp)])
        tap_gate[0] = sOUT.n

    if stop_after == "ln1":
        P.wait("sync", (sOUT, sOUT.n))
        return _finish(nc, P, stack, tap_out)

    rt_gate = 0
    for st in range(8):
        for dc in range(NDC):
            ww = [(sDV3, h_bf_ready), (sGD, prolog_gd)]
            if dc == 0:
                ww.append((sDV2, rt_gate))

            def ml(e, dc=dc, st=st):
                return e.matmul(p2[:, 0:8],
                                h_bf[:, dc * 1024 + st * 128:dc * 1024 + (st + 1) * 128],
                                rw_s[:, dc * 8:(dc + 1) * 8],
                                start=(dc == 0), stop=(dc == NDC - 1))
            P.op("tensor", ml, waits=ww, inc=sPE if dc == NDC - 1 else None)
        pe_v = sPE.n
        seq = [
            ("vector", lambda e: e.tensor_copy(lg, p2[:, 0:8]), [(sPE, pe_v)]),
            ("vector", lambda e: e.reduce_max(m1, lg, axis=AX.X), None),
            ("vector", lambda e: e.tensor_scalar(mk1, lg, m1, None, op0=ALU.is_equal), None),
            ("vector", lambda e: e.scalar_tensor_tensor(l2, mk1, -1e30, lg,
                                                        op0=ALU.mult, op1=ALU.add), None),
            ("vector", lambda e: e.reduce_max(m2, l2, axis=AX.X), None),
            ("vector", lambda e: e.tensor_sub(dd, m1, m2), None),
            ("scalar", lambda e: e.activation(wA, dd, AF.Sigmoid), None),
            ("vector", lambda e: e.tensor_scalar(wB, wA, -1.0, 1.0,
                                                 op0=ALU.mult, op1=ALU.add), None),
            ("vector", lambda e: e.tensor_scalar(mk2, l2, m2, None, op0=ALU.is_equal), None),
            ("vector", lambda e, st=st: e.tensor_scalar(
                rta[:, st * 8:(st + 1) * 8], mk1, wA, None, op0=ALU.mult), None),
            ("vector", lambda e, st=st: e.scalar_tensor_tensor(
                rta[:, st * 8:(st + 1) * 8], mk2, wB, rta[:, st * 8:(st + 1) * 8],
                op0=ALU.mult, op1=ALU.add), None),
        ]
        for (i_, (eng, fn, extra)) in enumerate(seq):
            base = [(sDV2, sDV2.n), (sAC2, sAC2.n)]
            if extra:
                base += extra
            P.op(eng, fn, waits=base, inc=sDV2 if eng == "vector" else sAC2)
            if i_ == 0:
                rt_gate = sDV2.n
    rta_v = sDV2.n

    if "r" in taps:
        dma("sync", tap_out["tap_r"], rta, sOUT, waits=[(sDV2, rta_v)])
        tap_gate[0] = sOUT.n

    if stop_after == "router":
        P.wait("sync", (sOUT, sOUT.n))
        return _finish(nc, P, stack, tap_out)

    w12_pe_gate = {0: 0, 1: 0}
    w12_dm_gate = {0: 0, 1: 0}
    w3_pe_gate = {0: wo_pe_end, 1: wo_pe_end}
    w3_dm_gate = {0: 0, 1: 0}
    w3_acc_gate = {0: wo_dv_gate[0], 1: wo_dv_gate[1]}
    sil_gate = 0
    zmul_gate = 0
    z_pe_gate = [wo_pe_end] * 16
    ffn_pe_end = wo_pe_end
    gate_p0_dv = wo_dv_gate[0]
    val_p1_dv = wo_dv_gate[1]
    for (gidx, grp) in enumerate(EGRPS):
        z_slot_dv = {}
        for (le, e_) in enumerate(grp):
            def emit_rb(e_=e_):
                for half in range(2):
                    for j2 in range(4):
                        st = half * 4 + j2

                        def mtr(e, st=st, e_=e_, j2=j2):
                            return e.transpose(p2[0:1, j2 * 128:(j2 + 1) * 128],
                                               rta[:, st * 8 + e_:st * 8 + e_ + 1],
                                               identf[:, :])
                        P.op("tensor", mtr,
                             waits=[(sDV2, rta_v), (sDV3, sDV3.n)], inc=sPE2)
                    P.op("vector", lambda e, half=half: e.tensor_copy(
                        rT2[:, half * 512:(half + 1) * 512], p2[0:1, 0:512]),
                        waits=[(sPE2, sPE2.n)], inc=sDV3)
                for half in range(2):
                    P.op("tensor", lambda e, half=half: e.matmul(
                        p2, ones1f[:, :], rT2[:, half * 512:(half + 1) * 512],
                        start=True, stop=True),
                        waits=[(sDV3, sDV3.n), (sGP, sGP.n)], inc=sPE2)
                    P.op("vector", lambda e, half=half: e.tensor_copy(
                        rb_f[:, half * 512:(half + 1) * 512], p2),
                        waits=[(sPE2, sPE2.n)], inc=sDV3)
                return sDV3.n
            rb_v = None
            for fb in range(NFB):
                bidx = fb % 2
                ww = [(sPE, w12_pe_gate[bidx]), (sPE, wo_pe_end)]
                dma("gpsimd", w1buf[bidx],
                    w1t[e_, fb].rearrange("p dc f -> p (dc f)"), sW1[bidx], waits=ww)
                dma("gpsimd", w2buf[bidx],
                    w2t[e_, fb].rearrange("p dc f -> p (dc f)"), sW1[bidx])
                wld = sW1[bidx].n
                for sh in range(2):
                    for dc in range(NDC):
                        ww = [(sW1[bidx], wld), (sDV3, h_bf_ready)]
                        if dc == 0:
                            ww += [(sAC, sil_gate), (sDV, gate_p0_dv)]

                        def mg(e, dc=dc, sh=sh, bidx=bidx):
                            return e.matmul(
                                p0, w1buf[bidx][:, dc * 128:(dc + 1) * 128],
                                h_bf[:, dc * 1024 + sh * 512:dc * 1024 + (sh + 1) * 512],
                                start=(dc == 0), stop=(dc == NDC - 1))
                        P.op("tensor", mg, waits=ww, inc=sPE if dc == NDC - 1 else None)
                    g_pe = sPE.n
                    for dc in range(NDC):
                        ww = [(sW1[bidx], wld)]
                        if dc == 0:
                            ww.append((sDV, val_p1_dv))

                        def mv(e, dc=dc, sh=sh, bidx=bidx):
                            return e.matmul(
                                p1, w2buf[bidx][:, dc * 128:(dc + 1) * 128],
                                h_bf[:, dc * 1024 + sh * 512:dc * 1024 + (sh + 1) * 512],
                                start=(dc == 0), stop=(dc == NDC - 1))
                        P.op("tensor", mv, waits=ww, inc=sPE if dc == NDC - 1 else None)
                    v_pe = sPE.n
                    if rb_v is None:
                        rb_v = emit_rb()
                    P.op("scalar", lambda e: e.activation(silA, p0, AF.Silu),
                         waits=[(sPE, g_pe), (sGP, sGP.n)], inc=sAC)
                    sil_gate = sAC.n
                    gate_p0_dv = 0
                    P.op("gpsimd", lambda e, sh=sh: e.tensor_mul(
                        silB, silA, rb_f[:, sh * 512:(sh + 1) * 512]),
                        waits=[(sAC, sil_gate), (sDV3, rb_v), (sDV, zmul_gate)], inc=sGP)
                    gp_v = sGP.n
                    zi = le * 8 + fb
                    zsl = zbuf[:, zi * 1024 + sh * 512:zi * 1024 + (sh + 1) * 512]
                    P.op("vector", lambda e, zsl=zsl: e.tensor_mul(zsl, p1, silB),
                         waits=[(sGP, gp_v), (sPE, v_pe), (sPE, z_pe_gate[zi])],
                         inc=sDV)
                    zmul_gate = sDV.n
                    val_p1_dv = sDV.n
                    z_slot_dv[zi] = sDV.n
                w12_pe_gate[bidx] = sPE.n
        z_all = sDV.n
        w3_pe_gate4 = {0: w12_pe_gate[0], 1: w12_pe_gate[1],
                       2: sPE.n, 3: sPE.n}
        for dchunk in range(4):
            for (le, e_) in enumerate(grp):
                bidx = (dchunk * 2 + le) % 4
                dma("gpsimd", w3buf[bidx].rearrange("p (fc d) -> p fc d", fc=NFC),
                    w3n[e_].rearrange("(fc p) d -> p fc d", p=128)[:, :, dchunk * 512:(dchunk + 1) * 512],
                    sW3[bidx], waits=[(sPE, w3_pe_gate4[bidx]), (sPE, wo_pe_end)])
            w3ld = [(sW3[i], sW3[i].n) for i in range(4)]
            for st in range(8):
                bk = st % 2
                psum = p0 if bk == 0 else p1
                nmm = 0
                for (le, e_) in enumerate(grp):
                    bidx = (dchunk * 2 + le) % 4
                    for fc in range(NFC):
                        ww = list(w3ld) + [(sDV, z_all)]
                        if nmm == 0:
                            ww.append((sDV, w3_acc_gate[bk]))
                        zi = le * 8 + fc

                        def m3(e, psum=psum, zi=zi, fc=fc, st=st, bidx=bidx, nmm=nmm):
                            return e.matmul(
                                psum,
                                zbuf[:, zi * 1024 + st * 128:zi * 1024 + (st + 1) * 128],
                                w3buf[bidx][:, fc * 512:(fc + 1) * 512],
                                start=(nmm == 0), stop=(nmm == 15))
                        P.op("tensor", m3, waits=ww, inc=sPE if nmm == 15 else None)
                        nmm += 1
                pe_v = sPE.n
                a_sl = acc[:, st * D + dchunk * 512:st * D + (dchunk + 1) * 512]
                P.op("vector", lambda e, a_sl=a_sl, psum=psum: e.tensor_add(
                    a_sl, psum, a_sl),
                    waits=[(sPE, pe_v), (sGP, h_ready_gp), (sOUT, tap_gate[0]),
                           (sDV, sDV.n)], inc=sDV)
                w3_acc_gate[bk] = sDV.n
            w3_pe_gate4[(dchunk * 2) % 4] = sPE.n
            w3_pe_gate4[(dchunk * 2 + 1) % 4] = sPE.n
        for zi in range(16):
            z_pe_gate[zi] = sPE.n
        gate_p0_dv = w3_acc_gate[0]
        val_p1_dv = w3_acc_gate[1]
        ffn_pe_end = sPE.n
        w12_pe_gate[0] = sPE.n
        w12_pe_gate[1] = sPE.n
    ffn_done_dv = sDV.n

    if "ffn" in taps:
        dma("sync", tap_out["tap_ffn"].rearrange("(st p) d -> p st d", p=128),
            acc.rearrange("p (st d) -> p st d", st=8), sOUT, waits=[(sDV, ffn_done_dv)])
        tap_gate[0] = sOUT.n

    if stop_after == "ffn":
        P.wait("sync", (sOUT, sOUT.n))
        return _finish(nc, P, stack, tap_out)

    stg_pe_gate[0] = 0
    stg_gp_gate[0] = 0
    emit_ln([(sDV, ffn_done_dv)], write_hbf=False, out_to=outT)

    P.wait("sync", (sOUT, sOUT.n))

    return _finish(nc, P, stack, tap_out)


def _finish(nc, P, stack, tap_out):
    with nc.Block() as block:
        P.replay(block)
    stack.close()
    return nc, list(tap_out.keys())


_HPERM = np.concatenate([np.arange(0, 128, 2), np.arange(1, 128, 2)])


def _host_inputs(L0_attn_x, Wq, Wk, Wv, Wo, router_w, w1, w2, w3):
    x = np.asarray(L0_attn_x, dtype=np.float32)
    Wq = np.asarray(Wq, np.float32); Wk = np.asarray(Wk, np.float32)
    Wv = np.asarray(Wv, np.float32); Wo = np.asarray(Wo, np.float32)
    router_w = np.asarray(router_w, np.float32)
    w1 = np.asarray(w1, np.float32); w2 = np.asarray(w2, np.float32)
    w3 = np.asarray(w3, np.float32)

    xT = [np.ascontiguousarray(x[b].T) for b in range(B)]
    j = np.arange(64, dtype=np.float64)
    inv = np.power(10000.0, -2.0 * j / DK)
    pos = np.arange(S, dtype=np.float64)
    ang = pos[None, :] * inv[:, None]
    cosv = np.cos(ang); sinv = np.sin(ang)
    c2 = np.concatenate([cosv, cosv], 0).astype(np.float32)
    s2 = np.concatenate([-sinv, sinv], 0).astype(np.float32)
    csh = np.ascontiguousarray(np.stack([c2, s2], 0))
    ii = np.arange(128)[:, None]
    jj = np.arange(512)[None, :]
    mskh = np.ascontiguousarray(np.stack(
        [np.where(jj <= qt * 128 + ii, 0.0, -1e30).astype(np.float32)
         for qt in range(NQT)], 0))
    idin = np.eye(128, dtype=np.float32)
    w1th = np.ascontiguousarray(
        w1.reshape(NE, NDC, 128, NFB, 128).transpose(0, 3, 2, 1, 4))
    w2th = np.ascontiguousarray(
        w2.reshape(NE, NDC, 128, NFB, 128).transpose(0, 3, 2, 1, 4))

    in_maps = []
    for c in range(8):
        b, r = c // 4, c % 4
        heads = [4 * r + i for i in range(NH)]
        qrows = np.concatenate([h * 128 + _HPERM for h in heads])
        vrows = np.concatenate([h * 128 + np.arange(128) for h in heads])
        in_maps.append({
            "xT": xT[b],
            "xo": np.ascontiguousarray(x[b][r * NT:(r + 1) * NT]),
            "wqT": np.ascontiguousarray(Wq[qrows].T),
            "wkT": np.ascontiguousarray(Wk[qrows].T),
            "wvT": np.ascontiguousarray(Wv[vrows].T),
            "woP": np.ascontiguousarray(Wo[:, vrows].T),
            "rwT": np.ascontiguousarray(router_w.T),
            "w1t": w1th, "w2t": w2th, "w3n": w3,
            "cs": csh, "msk": mskh, "idin": idin,
        })
    return in_maps


def kernel(L0_attn_x, L0_attn_Wq, L0_attn_Wk, L0_attn_Wv, L0_attn_Wo,
           L0_ffn_router_w, L0_ffn_w1, L0_ffn_w2, L0_ffn_w3):
    in_maps = _host_inputs(L0_attn_x, L0_attn_Wq, L0_attn_Wk, L0_attn_Wv,
                           L0_attn_Wo, L0_ffn_router_w, L0_ffn_w1, L0_ffn_w2,
                           L0_ffn_w3)
    nc, _ = build_program()
    res = run_bass_kernel_spmd(nc, in_maps, core_ids=list(range(8)))
    out = np.empty((B, S, D), np.float32)
    for c in range(8):
        b, r = c // 4, c % 4
        out[b, r * NT:(r + 1) * NT, :] = res.results[c]["outT"]
    return out


# version 06
# speedup vs baseline: 1.1413x; 1.0222x over previous
import math
import numpy as np
import ml_dtypes
import concourse.bass as bass
import concourse.mybir as mybir
from concourse.bass_utils import run_bass_kernel_spmd

F32 = mybir.dt.float32
BF16 = mybir.dt.bfloat16
AF = mybir.ActivationFunctionType
ALU = mybir.AluOpType
AX = mybir.AxisListType

B, S, D, H, DK = 2, 4096, 2048, 16, 128
NH = 4
NT = 1024
NE, DF = 8, 1024
NDC = D // 128
NSC = S // 512
NQC = S // 512
NQT = 4
NFC = DF // 128
NFB = 8
EGRPS = [(0, 1), (2, 3), (4, 5), (6, 7)]
ISQ = 1.0 / math.sqrt(DK)
EPS = 1e-5
ABLATE = set()
ENGS = ("sync", "tensor", "vector", "scalar", "gpsimd")


class Sem:
    def __init__(self, h):
        self.h = h
        self.n = 0


class Prog:
    def __init__(self):
        self.items = {e: [] for e in ENGS}
        self.last_wait = {}

    def op(self, eng_name, build_fn, waits=(), inc=None, inc_by=1):
        wl = []
        for (s, v) in waits:
            if v <= 0:
                continue
            key = (eng_name, id(s))
            if self.last_wait.get(key, -1) < v:
                self.last_wait[key] = v
                wl.append((s, v))
        if inc is not None:
            inc.n += inc_by
        wl = tuple(wl)

        def f(eng, build_fn=build_fn, wl=wl, inc=inc, inc_by=inc_by):
            for (s, v) in wl:
                eng.wait_ge(s.h, v)
            ins = build_fn(eng)
            if inc is not None and ins is not None:
                ins.then_inc(inc.h, inc_by)
        self.items[eng_name].append(f)
        return inc.n if inc is not None else 0

    def wait(self, eng_name, *waits):
        self.op(eng_name, lambda e: None, waits=waits)

    def replay(self, block):
        for name in ENGS:
            ops = self.items[name]
            if not ops:
                continue

            def body(eng, ops=ops):
                for f in ops:
                    f(eng)
            getattr(block, name)(body)


def build_program(taps=(), stop_after=None):
    taps = set(taps)
    nc = bass.Bass("TRN2", target_bir_lowering=False, debug=False, num_devices=8)

    xT = nc.dram_tensor("xT", [D, S], F32, kind="ExternalInput").ap()
    xo = nc.dram_tensor("xo", [NT, D], F32, kind="ExternalInput").ap()
    wqT = nc.dram_tensor("wqT", [D, NH * DK], F32, kind="ExternalInput").ap()
    wkT = nc.dram_tensor("wkT", [D, NH * DK], F32, kind="ExternalInput").ap()
    wvT = nc.dram_tensor("wvT", [D, NH * DK], F32, kind="ExternalInput").ap()
    woP = nc.dram_tensor("woP", [NH * DK, D], F32, kind="ExternalInput").ap()
    rwT = nc.dram_tensor("rwT", [D, NE], F32, kind="ExternalInput").ap()
    w1t = nc.dram_tensor("w1t", [NE, NFB, 128, NDC, 128], F32, kind="ExternalInput").ap()
    w2t = nc.dram_tensor("w2t", [NE, NFB, 128, NDC, 128], F32, kind="ExternalInput").ap()
    w3n = nc.dram_tensor("w3n", [NE, DF, D], F32, kind="ExternalInput").ap()
    cs = nc.dram_tensor("cs", [2, 128, S], F32, kind="ExternalInput").ap()
    msk = nc.dram_tensor("msk", [NQT, 128, 512], F32, kind="ExternalInput").ap()
    idin = nc.dram_tensor("idin", [128, 128], F32, kind="ExternalInput").ap()
    outT = nc.dram_tensor("outT", [NT, D], F32, kind="ExternalOutput").ap()

    qb = nc.dram_tensor("qb", [NH, 128, S], BF16, kind="Internal").ap()
    kb = nc.dram_tensor("kb", [NH, 128, S], BF16, kind="Internal").ap()
    vb = nc.dram_tensor("vb", [NH, S, 128], BF16, kind="Internal").ap()
    rsin = [nc.dram_tensor(f"rsin{i}", [S, 512], F32, kind="Internal").ap() for i in range(4)]
    rsout = [nc.dram_tensor(f"rsout{i}", [NT, 512], F32, kind="Internal").ap() for i in range(4)]

    tap_out = {}
    if "qkv" in taps:
        tap_out["tap_q"] = nc.dram_tensor("tap_q", [NH, 128, S], BF16, kind="ExternalOutput").ap()
        tap_out["tap_k"] = nc.dram_tensor("tap_k", [NH, 128, S], BF16, kind="ExternalOutput").ap()
        tap_out["tap_v"] = nc.dram_tensor("tap_v", [NH, S, 128], BF16, kind="ExternalOutput").ap()
    if "ctx" in taps:
        tap_out["tap_ctx"] = nc.dram_tensor("tap_ctx", [128, NH * S], BF16, kind="ExternalOutput").ap()
    if "rs" in taps:
        tap_out["tap_rs"] = nc.dram_tensor("tap_rs", [4, NT, 512], F32, kind="ExternalOutput").ap()
    if "h" in taps:
        tap_out["tap_h"] = nc.dram_tensor("tap_h", [NT, D], F32, kind="ExternalOutput").ap()
    if "r" in taps:
        tap_out["tap_r"] = nc.dram_tensor("tap_r", [128, 64], F32, kind="ExternalOutput").ap()
    if "ffn" in taps:
        tap_out["tap_ffn"] = nc.dram_tensor("tap_ffn", [NT, D], F32, kind="ExternalOutput").ap()

    ab = nc.alloc_sbuf_tensor("arena_bf", [128, 49152], BF16).ap()
    af = nc.alloc_sbuf_tensor("arena_f32", [128, 11264], F32).ap()
    acc = nc.alloc_sbuf_tensor("acc", [128, 8 * D], F32).ap()
    identb = nc.alloc_sbuf_tensor("identb", [128, 128], BF16).ap()
    identf = nc.alloc_sbuf_tensor("identf", [128, 128], F32).ap()
    ones1f = nc.alloc_sbuf_tensor("ones1f", [1, 128], F32).ap()
    rw_s = nc.alloc_sbuf_tensor("rw_s", [128, NDC * NE], BF16).ap()
    rrec_s = nc.alloc_sbuf_tensor("rrec_s", [128, 8], F32).ap()
    rT_s = nc.alloc_sbuf_tensor("rT_s", [1, 512], F32).ap()

    wq_s = ab[:, 0:8192]
    wk_s = ab[:, 8192:16384]
    wv_s = ab[:, 16384:24576]
    xbf = [ab[:, 24576 + i * 8192:24576 + (i + 1) * 8192] for i in (0, 1)]
    qstage = ab[:, 40960:43008]
    kstage = ab[:, 43008:45056]
    vstage = ab[:, 45056:47104]
    cst = ab[:, 0:16384]
    qh = ab[:, 16384:20480]
    kh = ab[:, 20480:24576]
    vh = ab[:, 24576:28672]
    attn = ab[:, 28672:45056]
    atb = ab[:, 45056:47104]
    woP_s = ab[:, 28672:36864]
    zbuf = ab[:, 0:16384]
    h_bf = ab[:, 16384:32768]
    w1buf = [ab[:, 32768 + i * 2048:32768 + (i + 1) * 2048] for i in (0, 1)]
    w2buf = [ab[:, 36864 + i * 2048:36864 + (i + 1) * 2048] for i in (0, 1)]
    w3buf = [ab[:, 40960:45056], ab[:, 45056:49152],
             ab[:, 32768:36864], ab[:, 36864:40960]]

    ropB = af[:, 0:512]
    c2 = af[:, 2048:6144]
    s2 = af[:, 6144:10240]
    swp = af[:, 10240:10752]
    ropA = af[:, 10752:11264]
    msk_s = af[:, 0:2048]
    scd = af[:, 10240:10752]
    rs = af[:, 10752:10816]
    recb = af[:, 2048:2560]
    wostg = [af[:, 2048 + i * 2048:2048 + (i + 1) * 2048] for i in (0, 1)]
    rT2 = af[0:1, 0:1024]
    tiny = af[:, 1024:1088]
    rta = af[:, 1088:1152]
    sqs = af[:, 2048:4096]
    stg = af[:, 4096:6144]
    stg2 = af[:, 8192:10240]
    silA = af[:, 6144:6656]
    silB = af[:, 6656:7168]
    rb_f = af[:, 7168:8192]

    p0 = nc.alloc_psum_tensor("p0", [128, 512], F32).ap()
    p1 = nc.alloc_psum_tensor("p1", [128, 512], F32).ap()
    p2 = nc.alloc_psum_tensor("p2", [128, 512], F32).ap()
    p7 = nc.alloc_psum_tensor("p7", [128, 512], F32).ap()
    p3 = nc.alloc_psum_tensor("p3", [128, 512], F32).ap()
    t0 = nc.alloc_psum_tensor("t0", [128, 512], BF16).ap()
    t1 = nc.alloc_psum_tensor("t1", [128, 512], BF16).ap()
    t2 = nc.alloc_psum_tensor("t2", [128, 512], BF16).ap()

    import contextlib
    stack = contextlib.ExitStack()
    SN = ["sDM", "sGD", "sGD2", "sPE", "sPE2", "sDV", "sDV2", "sDV3",
          "sAC", "sAC2", "sGP", "sCC", "sOUT",
          "sXB0", "sXB1", "sSTQ", "sSTK", "sSTV", "sHLD",
          "sWS0", "sWS1", "sW10", "sW11", "sW30", "sW31", "sW32", "sW33"]
    sem = {n: Sem(stack.enter_context(nc.semaphore(n))) for n in SN}
    sDM, sGD, sGD2 = sem["sDM"], sem["sGD"], sem["sGD2"]
    sPE, sPE2 = sem["sPE"], sem["sPE2"]
    sDV, sDV2, sDV3 = sem["sDV"], sem["sDV2"], sem["sDV3"]
    sAC, sAC2, sGP = sem["sAC"], sem["sAC2"], sem["sGP"]
    sCC, sOUT = sem["sCC"], sem["sOUT"]
    sXB = [sem["sXB0"], sem["sXB1"]]
    sSTQ, sSTK, sSTV = sem["sSTQ"], sem["sSTK"], sem["sSTV"]
    sHLD = sem["sHLD"]
    sWS = [sem["sWS0"], sem["sWS1"]]
    sW1 = [sem["sW10"], sem["sW11"]]
    sW3 = [sem["sW30"], sem["sW31"], sem["sW32"], sem["sW33"]]

    P = Prog()

    def dma(eng, out, in_, s, waits=(), **kw):
        return P.op(eng, lambda e: e.dma_start(out=out, in_=in_, **kw),
                    waits=waits, inc=s, inc_by=16)

    dma("gpsimd", identb[:, :], idin, sGD)
    dma("gpsimd", rw_s.rearrange("p (dc e) -> p dc e", dc=NDC),
        rwT.rearrange("(dc p) e -> p dc e", p=128), sGD)
    dma("gpsimd", wq_s.rearrange("p (dc o) -> p dc o", dc=NDC),
        wqT.rearrange("(dc p) o -> p dc o", p=128), sGD)
    dma("gpsimd", wk_s.rearrange("p (dc o) -> p dc o", dc=NDC),
        wkT.rearrange("(dc p) o -> p dc o", p=128), sGD)
    dma("gpsimd", wv_s.rearrange("p (dc o) -> p dc o", dc=NDC),
        wvT.rearrange("(dc p) o -> p dc o", p=128), sGD)
    dma("sync", identf[:, :], idin, sDM)
    dma("sync", c2, cs[0], sDM)
    dma("sync", s2, cs[1], sDM)
    P.op("gpsimd", lambda e: e.memset(ones1f[:, :], 1.0), inc=sGP)
    prolog_gd, prolog_dm, prolog_gp = sGD.n, sDM.n, sGP.n

    if stop_after == "P":
        P.wait("sync", (sGD, sGD.n), (sDM, sDM.n), (sGD2, sGD2.n), (sGP, sGP.n))
        return _finish(nc, P, stack, tap_out)

    xld = []
    xfree = []
    bank_free = {0: (0, 0), 1: (0, 0), 2: (0, 0)}
    stage_dm = {"q": (sSTQ, 0), "k": (sSTK, 0), "v": (sSTV, 0)}
    swp_gate = 0
    ropB_gate = 0
    for sc in range(NSC):
        bi = sc % 2
        w = [(sGD, prolog_gd), (sDM, prolog_dm)]
        if sc >= 2:
            w.append((sPE, xfree[sc - 2]))
        dma("gpsimd", xbf[bi].rearrange("p (dc c) -> p dc c", dc=NDC),
            xT.rearrange("(dc p) s -> p dc s", p=128)[:, :, sc * 512:(sc + 1) * 512],
            sXB[bi], waits=w)
        xld.append((sXB[bi], sXB[bi].n))
        col = sc * 512
        gi = 0
        for (stage, w_s, name, dst) in (() if "noqk" in ABLATE else ((qstage, wq_s, "q", qb), (kstage, wk_s, "k", kb))):
            fin = []
            for ot in range(NH):
                psum = (p0, p1, p3)[gi % 3]
                bfree = bank_free[gi % 3]
                for dc in range(NDC):
                    ww = [xld[sc]]
                    if dc == 0:
                        ww += [(sAC, bfree[0]), (sDV, bfree[1])]

                    def mm(e, psum=psum, w_s=w_s, dc=dc, ot=ot, bi=bi):
                        return e.matmul(
                            psum, w_s[:, dc * 512 + ot * 128:dc * 512 + (ot + 1) * 128],
                            xbf[bi][:, dc * 512:(dc + 1) * 512],
                            start=(dc == 0), stop=(dc == NDC - 1))
                    P.op("tensor", mm, waits=ww, inc=sPE if dc == NDC - 1 else None)
                pe_v = sPE.n
                if "norope" in ABLATE:
                    sl = stage[:, ot * 512:(ot + 1) * 512]
                    P.op("vector", lambda e, sl=sl, psum=psum: e.tensor_copy(sl, psum),
                         waits=[(sPE, pe_v), stage_dm[name], (sDV, ropB_gate)], inc=sDV)
                    ropB_gate = sDV.n
                    fin.append(sDV.n)
                    bank_free[gi % 3] = (sAC.n, sDV.n)
                    gi += 1
                    continue
                P.op("scalar", lambda e, psum=psum: e.copy(swp[0:64, :], psum[64:128, :]),
                     waits=[(sPE, pe_v), (sGP, swp_gate)], inc=sAC)
                P.op("scalar", lambda e, psum=psum: e.copy(swp[64:128, :], psum[0:64, :]),
                     inc=sAC)
                ac_v = sAC.n
                P.op("vector", lambda e, psum=psum, col=col: e.tensor_mul(
                    ropA, psum, c2[:, col:col + 512]),
                    waits=[(sPE, pe_v), (sDM, prolog_dm), (sDV, ropB_gate),
                           (sAC, ac_v)], inc=sDV)
                dv_a = sDV.n
                if "r3" in ABLATE:
                    P.op("gpsimd", lambda e, col=col: e.tensor_mul(
                        ropB, c2[:, col:col + 512], s2[:, col:col + 512]),
                        waits=[(sAC, ac_v), (sDV, ropB_gate)], inc=sGP)
                else:
                    P.op("gpsimd", lambda e, col=col: e.tensor_mul(
                        ropB, swp, s2[:, col:col + 512]),
                        waits=[(sAC, ac_v), (sDV, ropB_gate)], inc=sGP)
                swp_gate = sGP.n
                sl = stage[:, ot * 512:(ot + 1) * 512]
                P.op("vector", lambda e, sl=sl: e.tensor_add(sl, ropA, ropB),
                     waits=[(sDV, dv_a), (sGP, sGP.n), stage_dm[name]], inc=sDV)
                ropB_gate = sDV.n
                fin.append(sDV.n)
                bank_free[gi % 3] = (sAC.n, sDV.n)
                gi += 1
            sch = sSTQ if name == "q" else sSTK
            if "nostage" not in ABLATE:
                dma("sync", dst.rearrange("h p s -> p h s")[:, :, sc * 512:(sc + 1) * 512],
                    stage.rearrange("p (h c) -> p h c", h=NH), sch, waits=[(sDV, fin[-1])])
                stage_dm[name] = (sch, sch.n)
        vfin = []
        for st in range(0 if "nov" in ABLATE else 4):
            psum = (p0, p1, p3)[gi % 3]
            bfree = bank_free[gi % 3]
            for dc in range(NDC):
                ww = [xld[sc]]
                if dc == 0:
                    ww += [(sAC, bfree[0]), (sDV, bfree[1])]

                def mmv(e, psum=psum, dc=dc, st=st, bi=bi):
                    return e.matmul(
                        psum, xbf[bi][:, dc * 512 + st * 128:dc * 512 + (st + 1) * 128],
                        wv_s[:, dc * 512:(dc + 1) * 512],
                        start=(dc == 0), stop=(dc == NDC - 1))
                P.op("tensor", mmv, waits=ww, inc=sPE if dc == NDC - 1 else None)
            pe_v = sPE.n
            P.op("vector", lambda e, psum=psum, st=st: e.tensor_copy(
                vstage[:, st * 512:(st + 1) * 512], psum),
                waits=[(sPE, pe_v), stage_dm["v"]], inc=sDV)
            vfin.append(sDV.n)
            bank_free[gi % 3] = (sAC.n, sDV.n)
            gi += 1
        for hh in range(0 if ("nov" in ABLATE or "nostage" in ABLATE) else NH):
            dma("sync",
                vb[hh].rearrange("(st p) d -> p st d", p=128)[:, sc * 4:(sc + 1) * 4, :],
                vstage.rearrange("p (st h d) -> p st h d", st=4, h=NH)[:, :, hh, :],
                sSTV, waits=[(sDV, vfin[-1])])
        stage_dm["v"] = (sSTV, sSTV.n)
        xfree.append(sPE.n)
    phA_bar = [(sSTQ, sSTQ.n), (sSTK, sSTK.n), (sSTV, sSTV.n)]
    phA_pe = sPE.n

    if "qkv" in taps:
        dma("sync", tap_out["tap_q"].rearrange("h p s -> (h p) s"),
            qb.rearrange("h p s -> (h p) s"), sOUT, waits=phA_bar)
        dma("sync", tap_out["tap_k"].rearrange("h p s -> (h p) s"),
            kb.rearrange("h p s -> (h p) s"), sOUT, waits=phA_bar)
        dma("sync", tap_out["tap_v"].rearrange("h s d -> (h s) d"),
            vb.rearrange("h s d -> (h s) d"), sOUT, waits=phA_bar)

    if stop_after == "A":
        P.wait("sync", (sOUT, sOUT.n))
        return _finish(nc, P, stack, tap_out)

    dma("sync", msk_s.rearrange("p (q c) -> p q c", q=NQT),
        msk.rearrange("q p c -> p q c"), sDM,
        waits=[(sDV, sDV.n), (sGP, sGP.n)])
    mask_dm = sDM.n
    dma("sync", acc.rearrange("p (st d) -> p st d", st=8),
        xo.rearrange("(st p) d -> p st d", p=128), sDM)
    xo_dm = sDM.n

    head_pe_end = phA_pe
    ctx_evac_dv = 0
    exp_attn_gate = 0
    rs_gate = 0
    recb_v = 0
    for hl in range(NH):
        dma("sync", qh, qb[hl], sHLD,
            waits=phA_bar + [(sPE, head_pe_end)])
        dma("sync", kh, kb[hl], sHLD)
        dma("sync", vh.rearrange("p (st d) -> p st d", st=32),
            vb[hl].rearrange("(st p) d -> p st d", p=128), sHLD)
        hload = sHLD.n
        sc_gate = {0: (0, 0), 1: (0, 0), 2: (0, 0)}
        for qc in range(NQC):
            nkc = qc + 1
            exp_done = {}
            for qt in range(NQT):
                for kc in range(nkc):
                    bk = kc % 3
                    psum = (p0, p1, p3)[bk]
                    ww = [(sHLD, hload), (sAC, sc_gate[bk][0]), (sDV, sc_gate[bk][1])]

                    def mms(e, psum=psum, qc=qc, qt=qt, kc=kc):
                        return e.matmul(
                            psum, qh[:, qc * 512 + qt * 128:qc * 512 + (qt + 1) * 128],
                            kh[:, kc * 512:(kc + 1) * 512], start=True, stop=True)
                    P.op("tensor", mms, waits=ww, inc=sPE)
                    pe_v = sPE.n
                    acol = qt * 4096 + kc * 512
                    slot = rs[:, (qt * 16 + kc):(qt * 16 + kc) + 1]
                    if kc == qc:
                        P.op("vector", lambda e, psum=psum, qt=qt: e.tensor_add(
                            scd, psum, msk_s[:, qt * 512:(qt + 1) * 512]),
                            waits=[(sPE, pe_v), (sDM, xo_dm), (sAC, sAC.n)], inc=sDV)
                        dv_v = sDV.n
                        P.op("scalar", lambda e, acol=acol, slot=slot: e.activation(
                            attn[:, acol:acol + 512], scd, AF.Exp, scale=ISQ,
                            accum_out=slot),
                            waits=[(sDV, dv_v), (sPE2, exp_attn_gate), (sDV2, rs_gate)],
                            inc=sAC)
                        sc_gate[bk] = (sc_gate[bk][0], dv_v)
                    else:
                        P.op("scalar", lambda e, psum=psum, acol=acol, slot=slot: e.activation(
                            attn[:, acol:acol + 512], psum, AF.Exp, scale=ISQ,
                            accum_out=slot),
                            waits=[(sPE, pe_v), (sPE2, exp_attn_gate), (sDV2, rs_gate)],
                            inc=sAC)
                        sc_gate[bk] = (sAC.n, sc_gate[bk][1])
                    exp_done[(qt, kc)] = sAC.n
            for qt in range(NQT):
                P.op("vector", lambda e, qt=qt, nkc=nkc: e.reduce_sum(
                    rrec_s[:, qt:qt + 1], rs[:, qt * 16:qt * 16 + nkc], axis=AX.X),
                    waits=[(sAC, exp_done[(qt, nkc - 1)]), (sDV2, sDV2.n)], inc=sDV2)
            rs_gate = sDV2.n
            P.op("vector", lambda e: e.reciprocal(rrec_s[:, 4:8], rrec_s[:, 0:4]),
                 waits=[(sDV2, sDV2.n)], inc=sDV2)
            rec_v = sDV2.n
            tp_gate = {0: sDV.n, 1: sDV.n, 2: sDV.n}
            atb_gate = [sPE.n] * 4
            tbc = 0
            for kc in range(nkc):
                for skb in range(4):
                    tb = tbc % 3
                    tbc += 1
                    tp = (t0, t1, t2)[tb]
                    for qt in range(NQT):
                        acol = qt * 4096 + kc * 512 + skb * 128
                        ww = [(sAC, exp_done[(qt, kc)])]
                        if qt == 0:
                            ww.append((sDV, tp_gate[tb]))

                        def mtr(e, tp=tp, qt=qt, acol=acol):
                            return e.transpose(tp[:, qt * 128:(qt + 1) * 128],
                                               attn[:, acol:acol + 128], identb[:, :])
                        P.op("tensor", mtr, waits=ww,
                             inc=sPE2 if qt == NQT - 1 else None)
                    tr_v = sPE2.n
                    P.op("vector", lambda e, tp=tp, skb=skb: e.tensor_copy(
                        atb[:, skb * 512:(skb + 1) * 512], tp),
                        waits=[(sPE2, tr_v), (sPE, atb_gate[skb])], inc=sDV)
                    ev_v = sDV.n
                    tp_gate[tb] = ev_v

                    def mc(e, kc=kc, skb=skb,
                           first=(kc == 0 and skb == 0),
                           last=(kc == nkc - 1 and skb == 3)):
                        return e.matmul(p2,
                                        vh[:, (kc * 4 + skb) * 128:(kc * 4 + skb + 1) * 128],
                                        atb[:, skb * 512:(skb + 1) * 512],
                                        start=first, stop=last)
                    ww = [(sDV, ev_v)]
                    if kc == 0 and skb == 0:
                        ww.append((sDV, ctx_evac_dv))
                    P.op("tensor", mc, waits=ww, inc=sPE)
                    atb_gate[skb] = sPE.n
            for qt in range(NQT):
                P.op("tensor", lambda e, qt=qt: e.transpose(
                    p7[0:1, qt * 128:(qt + 1) * 128], rrec_s[:, 4 + qt:5 + qt],
                    identf[:, :]),
                    waits=[(sDV2, rec_v), (sDV3, sDV3.n)], inc=sPE2)
            P.op("vector", lambda e: e.tensor_copy(rT_s[:, :], p7[0:1, 0:512]),
                 waits=[(sPE2, sPE2.n)], inc=sDV3)
            P.op("tensor", lambda e: e.matmul(p7, ones1f[:, :], rT_s[:, :],
                                              start=True, stop=True),
                 waits=[(sDV3, sDV3.n)], inc=sPE2)
            P.op("vector", lambda e: e.tensor_copy(recb, p7),
                 waits=[(sPE2, sPE2.n)], inc=sDV3)
            recb_v = sDV3.n
            exp_attn_gate = sPE2.n
            P.op("vector", lambda e, hl=hl, qc=qc: e.tensor_mul(
                cst[:, hl * 4096 + qc * 512:hl * 4096 + (qc + 1) * 512], p2, recb),
                waits=[(sPE, sPE.n), (sDV3, recb_v)], inc=sDV)
            ctx_evac_dv = sDV.n
        head_pe_end = sPE.n

    if "ctx" in taps:
        dma("sync", tap_out["tap_ctx"], cst, sOUT, waits=[(sDV, ctx_evac_dv)])

    if stop_after == "B":
        P.wait("sync", (sOUT, sOUT.n))
        return _finish(nc, P, stack, tap_out)

    dma("gpsimd", woP_s.rearrange("p (dc o) -> p dc o", dc=NH),
        woP.rearrange("(dc p) o -> p dc o", p=128), sGD,
        waits=[(sAC, sAC.n), (sPE2, sPE2.n), (sPE, head_pe_end)])
    wop_gd = sGD.n
    cst_ready = ctx_evac_dv
    wo_gate = {0: sc_gate[0][0], 1: sc_gate[1][0]}
    wo_dv_gate = {0: 0, 1: 0}
    wostg_dm = [0, 0]
    rs_cc = []
    for oc in range(4):
        for sq in range(8):
            sg = sq % 2
            for j in range(4):
                st = sq * 4 + j
                bk = st % 2
                psum = p0 if bk == 0 else p1
                for hl in range(NH):
                    ww = [(sGD, wop_gd), (sDV, cst_ready)]
                    if hl == 0:
                        ww += [(sDV, wo_dv_gate[bk]), (sWS[sg], wostg_dm[sg])]

                    def mw(e, psum=psum, hl=hl, st=st, oc=oc):
                        return e.matmul(
                            psum, cst[:, hl * 4096 + st * 128:hl * 4096 + (st + 1) * 128],
                            woP_s[:, hl * 2048 + oc * 512:hl * 2048 + (oc + 1) * 512],
                            start=(hl == 0), stop=(hl == NH - 1))
                    P.op("tensor", mw, waits=ww, inc=sPE if hl == NH - 1 else None)
                pe_v = sPE.n
                P.op("vector", lambda e, psum=psum, sg=sg, j=j: e.tensor_copy(
                    wostg[sg][:, j * 512:(j + 1) * 512], psum),
                    waits=[(sPE, pe_v), (sWS[sg], wostg_dm[sg])], inc=sDV)
                wo_dv_gate[bk] = sDV.n
            dma("sync",
                rsin[oc].rearrange("(q p) c -> p q c", p=128)[:, sq * 4:(sq + 1) * 4, :],
                wostg[sg].rearrange("p (j c) -> p j c", j=4), sWS[sg],
                waits=[(sDV, sDV.n)])
            wostg_dm[sg] = sWS[sg].n
        P.op("gpsimd", lambda e, oc=oc: e.collective_compute(
            "ReduceScatter", ALU.add, replica_groups=[[0, 1, 2, 3], [4, 5, 6, 7]],
            ins=[rsin[oc]], outs=[rsout[oc]]),
            waits=[(sWS[0], sWS[0].n), (sWS[1], sWS[1].n)], inc=sCC)
        rs_cc.append(sCC.n)
    wo_pe_end = sPE.n

    if "rs" in taps:
        for oc in range(4):
            dma("sync", tap_out["tap_rs"][oc], rsout[oc], sOUT, waits=[(sCC, rs_cc[oc])])

    for oc in range(4):
        dma("gpsimd", acc.rearrange("p (st d) -> p st d", st=8)[:, :, oc * 512:(oc + 1) * 512],
            rsout[oc].rearrange("(st p) d -> p st d", p=128), sGD,
            waits=[(sCC, rs_cc[oc]), (sDM, xo_dm)], accum_op=ALU.add)
    acc_gd = sGD.n

    if stop_after == "wo":
        P.wait("sync", (sOUT, sOUT.n))
        return _finish(nc, P, stack, tap_out)

    tap_gate = [0]
    outslot_gate = {0: 0, 1: 0}
    stg_pe_gate = [0]
    tp_c_gate = {0: sDV3.n, 1: sDV3.n}
    stg_gp_gate = [0]

    def emit_ln(src_gates, write_hbf, out_to=None):
        for st in range(8):
            a_sl = acc[:, st * D:(st + 1) * D]
            mu = tiny[:, st * 8 + 0:st * 8 + 1]
            ng = tiny[:, st * 8 + 1:st * 8 + 2]
            ms = tiny[:, st * 8 + 2:st * 8 + 3]
            vv = tiny[:, st * 8 + 3:st * 8 + 4]
            sq = tiny[:, st * 8 + 4:st * 8 + 5]
            sd = tiny[:, st * 8 + 5:st * 8 + 6]
            rho = tiny[:, st * 8 + 6:st * 8 + 7]
            P.op("scalar", lambda e, a_sl=a_sl, sq=sq: e.activation(
                sqs, a_sl, AF.Square, accum_out=sq),
                waits=list(src_gates) + [(sAC2, sAC2.n)], inc=sAC2)
            sq_v = sAC2.n
            P.op("vector", lambda e, a_sl=a_sl, mu=mu: e.reduce_sum(mu, a_sl, axis=AX.X),
                 waits=list(src_gates) + [(sDV2, sDV2.n)], inc=sDV2)
            P.op("vector", lambda e, ng=ng, mu=mu: e.tensor_scalar_mul(ng, mu, -1.0 / D),
                 waits=[(sDV2, sDV2.n)], inc=sDV2)
            ng_v = sDV2.n
            P.op("gpsimd", lambda e, ms=ms, ng=ng: e.tensor_mul(ms, ng, ng),
                 waits=[(sDV2, ng_v), (sGP, sGP.n)], inc=sGP)
            ms_v = sGP.n
            P.op("vector", lambda e, vv=vv, sq=sq: e.tensor_scalar_mul(vv, sq, 1.0 / D),
                 waits=[(sAC2, sq_v), (sDV2, sDV2.n)], inc=sDV2)
            P.op("vector", lambda e, vv=vv, ms=ms: e.scalar_tensor_tensor(
                vv, vv, EPS, ms, op0=ALU.add, op1=ALU.subtract),
                 waits=[(sGP, ms_v), (sDV2, sDV2.n)], inc=sDV2)
            vv_v = sDV2.n
            P.op("scalar", lambda e, sd=sd, vv=vv: e.activation(sd, vv, AF.Sqrt),
                 waits=[(sDV2, vv_v), (sAC2, sAC2.n)], inc=sAC2)
            sd_v = sAC2.n
            P.op("vector", lambda e, rho=rho, sd=sd: e.reciprocal(rho, sd),
                 waits=[(sAC2, sd_v), (sDV2, sDV2.n)], inc=sDV2)
            ostg = stg if (write_hbf or st % 2 == 0) else stg2
            oslot = sWS[st % 2]
            ww = [(sDV2, sDV2.n), (sOUT, tap_gate[0]),
                  (sPE2, stg_pe_gate[0]), (sGP, stg_gp_gate[0])]
            if not write_hbf:
                ww.append((oslot, outslot_gate[st % 2]))
            P.op("vector", lambda e, a_sl=a_sl, ng=ng, rho=rho, ostg=ostg: e.tensor_scalar(
                ostg, a_sl, ng, rho, op0=ALU.add, op1=ALU.mult),
                waits=ww, inc=sDV2)
            nrm_v = sDV2.n
            if write_hbf:
                for dc in range(NDC):
                    bk = dc % 2
                    pp = p2 if bk == 0 else p7
                    ww = [(sDV2, nrm_v), (sDV3, tp_c_gate[bk]), (sDV, ctx_evac_dv)]

                    def mt(e, pp=pp, dc=dc):
                        return e.transpose(pp[:, 0:128], stg[:, dc * 128:(dc + 1) * 128],
                                           identf[:, :])
                    P.op("tensor", mt, waits=ww, inc=sPE2)
                    tr_v = sPE2.n
                    P.op("vector", lambda e, pp=pp, dc=dc, st=st: e.tensor_copy(
                        h_bf[:, dc * 1024 + st * 128:dc * 1024 + (st + 1) * 128],
                        pp[:, 0:128]),
                        waits=[(sPE2, tr_v), (sDV3, sDV3.n)], inc=sDV3)
                    tp_c_gate[bk] = sDV3.n
                stg_pe_gate[0] = sPE2.n
                P.op("gpsimd", lambda e, a_sl=a_sl: e.tensor_copy(a_sl, stg),
                     waits=[(sDV2, nrm_v), (sGP, sGP.n), (sDV3, sDV3.n)], inc=sGP)
                stg_gp_gate[0] = sGP.n
            else:
                dma("sync", out_to[st * 128:(st + 1) * 128, :], ostg, oslot,
                    waits=[(sDV2, nrm_v)])
                outslot_gate[st % 2] = oslot.n
                stg_gp_gate[0] = 0
                stg_pe_gate[0] = 0

    emit_ln([(sGD, acc_gd)], write_hbf=True)
    h_ready_gp = sGP.n
    h_bf_ready = sDV3.n

    if "h" in taps:
        dma("sync", tap_out["tap_h"].rearrange("(st p) d -> p st d", p=128),
            acc.rearrange("p (st d) -> p st d", st=8), sOUT,
            waits=[(sGP, h_ready_gp)])
        tap_gate[0] = sOUT.n

    if stop_after == "ln1":
        P.wait("sync", (sOUT, sOUT.n))
        return _finish(nc, P, stack, tap_out)

    rt_gate = 0
    lgv = {}
    for st in range(8):
        for dc in range(NDC):
            ww = [(sDV3, h_bf_ready), (sGD, prolog_gd)]
            if dc == 0:
                ww.append((sDV2, rt_gate))

            def ml(e, dc=dc, st=st):
                return e.matmul(p2[:, st * 8:(st + 1) * 8],
                                h_bf[:, dc * 1024 + st * 128:dc * 1024 + (st + 1) * 128],
                                rw_s[:, dc * 8:(dc + 1) * 8],
                                start=(dc == 0), stop=(dc == NDC - 1))
            P.op("tensor", ml, waits=ww, inc=sPE if dc == NDC - 1 else None)
        pe_v = sPE.n
        b0 = 1152 + st * 40
        lg = af[:, b0:b0 + 8]
        l2 = af[:, b0 + 8:b0 + 16]
        mk1 = af[:, b0 + 16:b0 + 24]
        mk2 = af[:, b0 + 24:b0 + 32]
        m1 = af[:, b0 + 32:b0 + 33]
        m2 = af[:, b0 + 33:b0 + 34]
        dd = af[:, b0 + 34:b0 + 35]
        wA = af[:, b0 + 35:b0 + 36]
        wB = af[:, b0 + 36:b0 + 37]
        P.op("vector", lambda e, lg=lg, st=st: e.tensor_copy(lg, p2[:, st * 8:(st + 1) * 8]),
             waits=[(sPE, pe_v), (sDV2, sDV2.n)], inc=sDV2)
        rt_gate = sDV2.n
        P.op("vector", lambda e, m1=m1, lg=lg: e.reduce_max(m1, lg, axis=AX.X),
             waits=[(sDV2, sDV2.n)], inc=sDV2)
        P.op("vector", lambda e, mk1=mk1, lg=lg, m1=m1: e.tensor_scalar(
            mk1, lg, m1, None, op0=ALU.is_equal), waits=[(sDV2, sDV2.n)], inc=sDV2)
        P.op("vector", lambda e, l2=l2, mk1=mk1, lg=lg: e.scalar_tensor_tensor(
            l2, mk1, -1e30, lg, op0=ALU.mult, op1=ALU.add),
            waits=[(sDV2, sDV2.n)], inc=sDV2)
        P.op("vector", lambda e, m2=m2, l2=l2: e.reduce_max(m2, l2, axis=AX.X),
             waits=[(sDV2, sDV2.n)], inc=sDV2)
        P.op("vector", lambda e, dd=dd, m1=m1, m2=m2: e.tensor_sub(dd, m1, m2),
             waits=[(sDV2, sDV2.n)], inc=sDV2)
        dd_v = sDV2.n
        P.op("scalar", lambda e, wA=wA, dd=dd: e.activation(wA, dd, AF.Sigmoid),
             waits=[(sDV2, dd_v), (sAC2, sAC2.n)], inc=sAC2)
        sg_v = sAC2.n
        P.op("vector", lambda e, wB=wB, wA=wA: e.tensor_scalar(
            wB, wA, -1.0, 1.0, op0=ALU.mult, op1=ALU.add),
            waits=[(sAC2, sg_v), (sDV2, sDV2.n)], inc=sDV2)
        P.op("vector", lambda e, mk2=mk2, l2=l2, m2=m2: e.tensor_scalar(
            mk2, l2, m2, None, op0=ALU.is_equal), waits=[(sDV2, sDV2.n)], inc=sDV2)
        P.op("vector", lambda e, st=st, mk1=mk1, wA=wA: e.tensor_scalar(
            rta[:, st * 8:(st + 1) * 8], mk1, wA, None, op0=ALU.mult),
            waits=[(sDV2, sDV2.n)], inc=sDV2)
        P.op("vector", lambda e, st=st, mk2=mk2, wB=wB: e.scalar_tensor_tensor(
            rta[:, st * 8:(st + 1) * 8], mk2, wB, rta[:, st * 8:(st + 1) * 8],
            op0=ALU.mult, op1=ALU.add), waits=[(sDV2, sDV2.n)], inc=sDV2)
    rta_v = sDV2.n

    if "r" in taps:
        dma("sync", tap_out["tap_r"], rta, sOUT, waits=[(sDV2, rta_v)])
        tap_gate[0] = sOUT.n

    if stop_after == "router":
        P.wait("sync", (sOUT, sOUT.n))
        return _finish(nc, P, stack, tap_out)

    w12_pe_gate = {0: 0, 1: 0}
    w12_dm_gate = {0: 0, 1: 0}
    w3_pe_gate = {0: wo_pe_end, 1: wo_pe_end}
    w3_dm_gate = {0: 0, 1: 0}
    w3_acc_gate = {0: wo_dv_gate[0], 1: wo_dv_gate[1]}
    sil_gate = 0
    zmul_gate = 0
    z_pe_gate = [wo_pe_end] * 16
    ffn_pe_end = wo_pe_end
    gate_p0_dv = wo_dv_gate[0]
    val_p1_dv = wo_dv_gate[1]
    for (gidx, grp) in enumerate(EGRPS):
        z_slot_dv = {}
        for (le, e_) in enumerate(grp):
            def emit_rb(e_=e_):
                for half in range(2):
                    for j2 in range(4):
                        st = half * 4 + j2

                        def mtr(e, st=st, e_=e_, j2=j2):
                            return e.transpose(p2[0:1, j2 * 128:(j2 + 1) * 128],
                                               rta[:, st * 8 + e_:st * 8 + e_ + 1],
                                               identf[:, :])
                        P.op("tensor", mtr,
                             waits=[(sDV2, rta_v), (sDV3, sDV3.n)], inc=sPE2)
                    P.op("vector", lambda e, half=half: e.tensor_copy(
                        rT2[:, half * 512:(half + 1) * 512], p2[0:1, 0:512]),
                        waits=[(sPE2, sPE2.n)], inc=sDV3)
                for half in range(2):
                    P.op("tensor", lambda e, half=half: e.matmul(
                        p2, ones1f[:, :], rT2[:, half * 512:(half + 1) * 512],
                        start=True, stop=True),
                        waits=[(sDV3, sDV3.n), (sGP, sGP.n)], inc=sPE2)
                    P.op("vector", lambda e, half=half: e.tensor_copy(
                        rb_f[:, half * 512:(half + 1) * 512], p2),
                        waits=[(sPE2, sPE2.n)], inc=sDV3)
                return sDV3.n
            rb_v = None
            for fb in range(NFB):
                bidx = fb % 2
                ww = [(sPE, w12_pe_gate[bidx]), (sPE, wo_pe_end)]
                dma("gpsimd", w1buf[bidx],
                    w1t[e_, fb].rearrange("p dc f -> p (dc f)"), sW1[bidx], waits=ww)
                dma("gpsimd", w2buf[bidx],
                    w2t[e_, fb].rearrange("p dc f -> p (dc f)"), sW1[bidx])
                wld = sW1[bidx].n
                for sh in range(2):
                    for dc in range(NDC):
                        ww = [(sW1[bidx], wld), (sDV3, h_bf_ready)]
                        if dc == 0:
                            ww += [(sAC, sil_gate), (sDV, gate_p0_dv)]

                        def mg(e, dc=dc, sh=sh, bidx=bidx):
                            return e.matmul(
                                p0, w1buf[bidx][:, dc * 128:(dc + 1) * 128],
                                h_bf[:, dc * 1024 + sh * 512:dc * 1024 + (sh + 1) * 512],
                                start=(dc == 0), stop=(dc == NDC - 1))
                        P.op("tensor", mg, waits=ww, inc=sPE if dc == NDC - 1 else None)
                    g_pe = sPE.n
                    for dc in range(NDC):
                        ww = [(sW1[bidx], wld)]
                        if dc == 0:
                            ww.append((sDV, val_p1_dv))

                        def mv(e, dc=dc, sh=sh, bidx=bidx):
                            return e.matmul(
                                p1, w2buf[bidx][:, dc * 128:(dc + 1) * 128],
                                h_bf[:, dc * 1024 + sh * 512:dc * 1024 + (sh + 1) * 512],
                                start=(dc == 0), stop=(dc == NDC - 1))
                        P.op("tensor", mv, waits=ww, inc=sPE if dc == NDC - 1 else None)
                    v_pe = sPE.n
                    if rb_v is None:
                        rb_v = emit_rb()
                    P.op("scalar", lambda e: e.activation(silA, p0, AF.Silu),
                         waits=[(sPE, g_pe), (sGP, sGP.n)], inc=sAC)
                    sil_gate = sAC.n
                    gate_p0_dv = 0
                    P.op("gpsimd", lambda e, sh=sh: e.tensor_mul(
                        silB, silA, rb_f[:, sh * 512:(sh + 1) * 512]),
                        waits=[(sAC, sil_gate), (sDV3, rb_v), (sDV, zmul_gate)], inc=sGP)
                    gp_v = sGP.n
                    zi = le * 8 + fb
                    zsl = zbuf[:, zi * 1024 + sh * 512:zi * 1024 + (sh + 1) * 512]
                    P.op("vector", lambda e, zsl=zsl: e.tensor_mul(zsl, p1, silB),
                         waits=[(sGP, gp_v), (sPE, v_pe), (sPE, z_pe_gate[zi])],
                         inc=sDV)
                    zmul_gate = sDV.n
                    val_p1_dv = sDV.n
                    z_slot_dv[zi] = sDV.n
                w12_pe_gate[bidx] = sPE.n
        z_all = sDV.n
        w3_pe_gate4 = {0: w12_pe_gate[0], 1: w12_pe_gate[1],
                       2: sPE.n, 3: sPE.n}
        for dchunk in range(4):
            for (le, e_) in enumerate(grp):
                bidx = (dchunk * 2 + le) % 4
                dma("gpsimd", w3buf[bidx].rearrange("p (fc d) -> p fc d", fc=NFC),
                    w3n[e_].rearrange("(fc p) d -> p fc d", p=128)[:, :, dchunk * 512:(dchunk + 1) * 512],
                    sW3[bidx], waits=[(sPE, w3_pe_gate4[bidx]), (sPE, wo_pe_end)])
            w3ld = [(sW3[i], sW3[i].n) for i in range(4)]
            for st in range(8):
                bk = st % 2
                psum = p0 if bk == 0 else p1
                nmm = 0
                for (le, e_) in enumerate(grp):
                    bidx = (dchunk * 2 + le) % 4
                    for fc in range(NFC):
                        ww = list(w3ld) + [(sDV, z_all)]
                        if nmm == 0:
                            ww.append((sDV, w3_acc_gate[bk]))
                        zi = le * 8 + fc

                        def m3(e, psum=psum, zi=zi, fc=fc, st=st, bidx=bidx, nmm=nmm):
                            return e.matmul(
                                psum,
                                zbuf[:, zi * 1024 + st * 128:zi * 1024 + (st + 1) * 128],
                                w3buf[bidx][:, fc * 512:(fc + 1) * 512],
                                start=(nmm == 0), stop=(nmm == 15))
                        P.op("tensor", m3, waits=ww, inc=sPE if nmm == 15 else None)
                        nmm += 1
                pe_v = sPE.n
                a_sl = acc[:, st * D + dchunk * 512:st * D + (dchunk + 1) * 512]
                P.op("vector", lambda e, a_sl=a_sl, psum=psum: e.tensor_add(
                    a_sl, psum, a_sl),
                    waits=[(sPE, pe_v), (sGP, h_ready_gp), (sOUT, tap_gate[0]),
                           (sDV, sDV.n)], inc=sDV)
                w3_acc_gate[bk] = sDV.n
            w3_pe_gate4[(dchunk * 2) % 4] = sPE.n
            w3_pe_gate4[(dchunk * 2 + 1) % 4] = sPE.n
        for zi in range(16):
            z_pe_gate[zi] = sPE.n
        gate_p0_dv = w3_acc_gate[0]
        val_p1_dv = w3_acc_gate[1]
        ffn_pe_end = sPE.n
        w12_pe_gate[0] = sPE.n
        w12_pe_gate[1] = sPE.n
    ffn_done_dv = sDV.n

    if "ffn" in taps:
        dma("sync", tap_out["tap_ffn"].rearrange("(st p) d -> p st d", p=128),
            acc.rearrange("p (st d) -> p st d", st=8), sOUT, waits=[(sDV, ffn_done_dv)])
        tap_gate[0] = sOUT.n

    if stop_after == "ffn":
        P.wait("sync", (sOUT, sOUT.n))
        return _finish(nc, P, stack, tap_out)

    stg_pe_gate[0] = 0
    stg_gp_gate[0] = 0
    emit_ln([(sDV, ffn_done_dv)], write_hbf=False, out_to=outT)

    P.wait("sync", (sOUT, sOUT.n))

    return _finish(nc, P, stack, tap_out)


def _finish(nc, P, stack, tap_out):
    with nc.Block() as block:
        P.replay(block)
    stack.close()
    return nc, list(tap_out.keys())


_HPERM = np.concatenate([np.arange(0, 128, 2), np.arange(1, 128, 2)])


def _host_inputs(L0_attn_x, Wq, Wk, Wv, Wo, router_w, w1, w2, w3):
    x = np.asarray(L0_attn_x, dtype=np.float32)
    Wq = np.asarray(Wq, np.float32); Wk = np.asarray(Wk, np.float32)
    Wv = np.asarray(Wv, np.float32); Wo = np.asarray(Wo, np.float32)
    router_w = np.asarray(router_w, np.float32)
    w1 = np.asarray(w1, np.float32); w2 = np.asarray(w2, np.float32)
    w3 = np.asarray(w3, np.float32)

    xT = [np.ascontiguousarray(x[b].T) for b in range(B)]
    j = np.arange(64, dtype=np.float64)
    inv = np.power(10000.0, -2.0 * j / DK)
    pos = np.arange(S, dtype=np.float64)
    ang = pos[None, :] * inv[:, None]
    cosv = np.cos(ang); sinv = np.sin(ang)
    c2 = np.concatenate([cosv, cosv], 0).astype(np.float32)
    s2 = np.concatenate([-sinv, sinv], 0).astype(np.float32)
    csh = np.ascontiguousarray(np.stack([c2, s2], 0))
    ii = np.arange(128)[:, None]
    jj = np.arange(512)[None, :]
    mskh = np.ascontiguousarray(np.stack(
        [np.where(jj <= qt * 128 + ii, 0.0, -1e30).astype(np.float32)
         for qt in range(NQT)], 0))
    idin = np.eye(128, dtype=np.float32)
    w1th = np.ascontiguousarray(
        w1.reshape(NE, NDC, 128, NFB, 128).transpose(0, 3, 2, 1, 4))
    w2th = np.ascontiguousarray(
        w2.reshape(NE, NDC, 128, NFB, 128).transpose(0, 3, 2, 1, 4))

    in_maps = []
    for c in range(8):
        b, r = c // 4, c % 4
        heads = [4 * r + i for i in range(NH)]
        qrows = np.concatenate([h * 128 + _HPERM for h in heads])
        vrows = np.concatenate([h * 128 + np.arange(128) for h in heads])
        in_maps.append({
            "xT": xT[b],
            "xo": np.ascontiguousarray(x[b][r * NT:(r + 1) * NT]),
            "wqT": np.ascontiguousarray(Wq[qrows].T),
            "wkT": np.ascontiguousarray(Wk[qrows].T),
            "wvT": np.ascontiguousarray(Wv[vrows].T),
            "woP": np.ascontiguousarray(Wo[:, vrows].T),
            "rwT": np.ascontiguousarray(router_w.T),
            "w1t": w1th, "w2t": w2th, "w3n": w3,
            "cs": csh, "msk": mskh, "idin": idin,
        })
    return in_maps


def kernel(L0_attn_x, L0_attn_Wq, L0_attn_Wk, L0_attn_Wv, L0_attn_Wo,
           L0_ffn_router_w, L0_ffn_w1, L0_ffn_w2, L0_ffn_w3):
    in_maps = _host_inputs(L0_attn_x, L0_attn_Wq, L0_attn_Wk, L0_attn_Wv,
                           L0_attn_Wo, L0_ffn_router_w, L0_ffn_w1, L0_ffn_w2,
                           L0_ffn_w3)
    nc, _ = build_program()
    res = run_bass_kernel_spmd(nc, in_maps, core_ids=list(range(8)))
    out = np.empty((B, S, D), np.float32)
    for c in range(8):
        b, r = c // 4, c % 4
        out[b, r * NT:(r + 1) * NT, :] = res.results[c]["outT"]
    return out


# version 07
# speedup vs baseline: 1.1611x; 1.0173x over previous
import math
import numpy as np
import ml_dtypes
import concourse.bass as bass
import concourse.mybir as mybir
from concourse.bass_utils import run_bass_kernel_spmd

F32 = mybir.dt.float32
BF16 = mybir.dt.bfloat16
AF = mybir.ActivationFunctionType
ALU = mybir.AluOpType
AX = mybir.AxisListType

B, S, D, H, DK = 2, 4096, 2048, 16, 128
NH = 4
NT = 1024
NE, DF = 8, 1024
NDC = D // 128
NSC = S // 512
NQC = S // 512
NQT = 4
NFC = DF // 128
NFB = 8
EGRPS = [(0, 1), (2, 3), (4, 5), (6, 7)]
ISQ = 1.0 / math.sqrt(DK)
EPS = 1e-5
ABLATE = set()
ENGS = ("sync", "tensor", "vector", "scalar", "gpsimd")


class Sem:
    def __init__(self, h):
        self.h = h
        self.n = 0


class Prog:
    def __init__(self):
        self.items = {e: [] for e in ENGS}
        self.last_wait = {}

    def op(self, eng_name, build_fn, waits=(), inc=None, inc_by=1):
        wl = []
        for (s, v) in waits:
            if v <= 0:
                continue
            key = (eng_name, id(s))
            if self.last_wait.get(key, -1) < v:
                self.last_wait[key] = v
                wl.append((s, v))
        if inc is not None:
            inc.n += inc_by
        wl = tuple(wl)

        def f(eng, build_fn=build_fn, wl=wl, inc=inc, inc_by=inc_by):
            for (s, v) in wl:
                eng.wait_ge(s.h, v)
            ins = build_fn(eng)
            if inc is not None and ins is not None:
                ins.then_inc(inc.h, inc_by)
        self.items[eng_name].append(f)
        return inc.n if inc is not None else 0

    def wait(self, eng_name, *waits):
        self.op(eng_name, lambda e: None, waits=waits)

    def replay(self, block):
        for name in ENGS:
            ops = self.items[name]
            if not ops:
                continue

            def body(eng, ops=ops):
                for f in ops:
                    f(eng)
            getattr(block, name)(body)


def build_program(taps=(), stop_after=None):
    taps = set(taps)
    nc = bass.Bass("TRN2", target_bir_lowering=False, debug=False, num_devices=8)

    xT = nc.dram_tensor("xT", [D, S], F32, kind="ExternalInput").ap()
    xo = nc.dram_tensor("xo", [NT, D], F32, kind="ExternalInput").ap()
    wqT = nc.dram_tensor("wqT", [D, NH * DK], F32, kind="ExternalInput").ap()
    wkT = nc.dram_tensor("wkT", [D, NH * DK], F32, kind="ExternalInput").ap()
    wvT = nc.dram_tensor("wvT", [D, NH * DK], F32, kind="ExternalInput").ap()
    woP = nc.dram_tensor("woP", [NH * DK, D], F32, kind="ExternalInput").ap()
    rwT = nc.dram_tensor("rwT", [D, NE], F32, kind="ExternalInput").ap()
    w1t = nc.dram_tensor("w1t", [NE, NFB, 128, NDC, 128], F32, kind="ExternalInput").ap()
    w2t = nc.dram_tensor("w2t", [NE, NFB, 128, NDC, 128], F32, kind="ExternalInput").ap()
    w3n = nc.dram_tensor("w3n", [NE, DF, D], F32, kind="ExternalInput").ap()
    cs = nc.dram_tensor("cs", [2, 128, S], F32, kind="ExternalInput").ap()
    msk = nc.dram_tensor("msk", [NQT, 128, 512], F32, kind="ExternalInput").ap()
    idin = nc.dram_tensor("idin", [128, 128], F32, kind="ExternalInput").ap()
    outT = nc.dram_tensor("outT", [NT, D], F32, kind="ExternalOutput").ap()

    qb = nc.dram_tensor("qb", [NH, 128, S], BF16, kind="Internal").ap()
    kb = nc.dram_tensor("kb", [NH, 128, S], BF16, kind="Internal").ap()
    vb = nc.dram_tensor("vb", [NH, S, 128], BF16, kind="Internal").ap()
    rsin = [nc.dram_tensor(f"rsin{i}", [S, 512], F32, kind="Internal").ap() for i in range(4)]
    rsout = [nc.dram_tensor(f"rsout{i}", [NT, 512], F32, kind="Internal").ap() for i in range(4)]

    tap_out = {}
    if "qkv" in taps:
        tap_out["tap_q"] = nc.dram_tensor("tap_q", [NH, 128, S], BF16, kind="ExternalOutput").ap()
        tap_out["tap_k"] = nc.dram_tensor("tap_k", [NH, 128, S], BF16, kind="ExternalOutput").ap()
        tap_out["tap_v"] = nc.dram_tensor("tap_v", [NH, S, 128], BF16, kind="ExternalOutput").ap()
    if "ctx" in taps:
        tap_out["tap_ctx"] = nc.dram_tensor("tap_ctx", [128, NH * S], BF16, kind="ExternalOutput").ap()
    if "rs" in taps:
        tap_out["tap_rs"] = nc.dram_tensor("tap_rs", [4, NT, 512], F32, kind="ExternalOutput").ap()
    if "h" in taps:
        tap_out["tap_h"] = nc.dram_tensor("tap_h", [NT, D], F32, kind="ExternalOutput").ap()
    if "r" in taps:
        tap_out["tap_r"] = nc.dram_tensor("tap_r", [128, 64], F32, kind="ExternalOutput").ap()
    if "ffn" in taps:
        tap_out["tap_ffn"] = nc.dram_tensor("tap_ffn", [NT, D], F32, kind="ExternalOutput").ap()

    ab = nc.alloc_sbuf_tensor("arena_bf", [128, 49152], BF16).ap()
    af = nc.alloc_sbuf_tensor("arena_f32", [128, 11264], F32).ap()
    acc = nc.alloc_sbuf_tensor("acc", [128, 8 * D], F32).ap()
    identb = nc.alloc_sbuf_tensor("identb", [128, 128], BF16).ap()
    identf = nc.alloc_sbuf_tensor("identf", [128, 128], F32).ap()
    ones1f = nc.alloc_sbuf_tensor("ones1f", [1, 128], F32).ap()
    rw_s = nc.alloc_sbuf_tensor("rw_s", [128, NDC * NE], BF16).ap()
    rrec_s = nc.alloc_sbuf_tensor("rrec_s", [128, 8], F32).ap()
    rT_s = nc.alloc_sbuf_tensor("rT_s", [1, 512], F32).ap()

    wq_s = ab[:, 0:8192]
    wk_s = ab[:, 8192:16384]
    wv_s = ab[:, 16384:24576]
    xbf = [ab[:, 24576 + i * 8192:24576 + (i + 1) * 8192] for i in (0, 1)]
    qstage = ab[:, 40960:43008]
    kstage = ab[:, 43008:45056]
    vstage = ab[:, 45056:47104]
    cst = ab[:, 0:16384]
    qh = ab[:, 16384:20480]
    kh = ab[:, 20480:24576]
    vh = ab[:, 24576:28672]
    attn = ab[:, 28672:45056]
    atb = ab[:, 45056:47104]
    woP_s = ab[:, 28672:36864]
    zbuf = ab[:, 0:16384]
    h_bf = ab[:, 16384:32768]
    w1buf = [ab[:, 32768 + i * 2048:32768 + (i + 1) * 2048] for i in (0, 1)]
    w2buf = [ab[:, 36864 + i * 2048:36864 + (i + 1) * 2048] for i in (0, 1)]
    w3buf = [ab[:, 40960:45056], ab[:, 45056:49152],
             ab[:, 32768:36864], ab[:, 36864:40960]]

    ropB = af[:, 0:512]
    c2 = af[:, 2048:6144]
    s2 = af[:, 6144:10240]
    swp = af[:, 10240:10752]
    ropA = af[:, 10752:11264]
    msk_s = af[:, 0:2048]
    scd = af[:, 10240:10752]
    rs = af[:, 10752:10816]
    recb = af[:, 2048:2560]
    wostg = [af[:, 2048 + i * 2048:2048 + (i + 1) * 2048] for i in (0, 1)]
    rT2 = af[0:1, 0:1024]
    tiny = af[:, 1024:1088]
    rta = af[:, 1088:1152]
    sqs = af[:, 2048:4096]
    stg = af[:, 4096:6144]
    stg2 = af[:, 8192:10240]
    silA = af[:, 6144:6656]
    silB = af[:, 6656:7168]
    rb_f = af[:, 7168:8192]

    p0 = nc.alloc_psum_tensor("p0", [128, 512], F32).ap()
    p1 = nc.alloc_psum_tensor("p1", [128, 512], F32).ap()
    p2 = nc.alloc_psum_tensor("p2", [128, 512], F32).ap()
    p7 = nc.alloc_psum_tensor("p7", [128, 512], F32).ap()
    p3 = nc.alloc_psum_tensor("p3", [128, 512], F32).ap()
    t0 = nc.alloc_psum_tensor("t0", [128, 512], BF16).ap()
    t1 = nc.alloc_psum_tensor("t1", [128, 512], BF16).ap()
    t2 = nc.alloc_psum_tensor("t2", [128, 512], BF16).ap()

    import contextlib
    stack = contextlib.ExitStack()
    SN = ["sDM", "sGD", "sGD2", "sPE", "sPE2", "sDV", "sDV2", "sDV3",
          "sAC", "sAC2", "sGP", "sCC", "sOUT",
          "sXB0", "sXB1", "sSTQ", "sSTK", "sSTV", "sHLD",
          "sWS0", "sWS1", "sW10", "sW11", "sW30", "sW31", "sW32", "sW33"]
    sem = {n: Sem(stack.enter_context(nc.semaphore(n))) for n in SN}
    sDM, sGD, sGD2 = sem["sDM"], sem["sGD"], sem["sGD2"]
    sPE, sPE2 = sem["sPE"], sem["sPE2"]
    sDV, sDV2, sDV3 = sem["sDV"], sem["sDV2"], sem["sDV3"]
    sAC, sAC2, sGP = sem["sAC"], sem["sAC2"], sem["sGP"]
    sCC, sOUT = sem["sCC"], sem["sOUT"]
    sXB = [sem["sXB0"], sem["sXB1"]]
    sSTQ, sSTK, sSTV = sem["sSTQ"], sem["sSTK"], sem["sSTV"]
    sHLD = sem["sHLD"]
    sWS = [sem["sWS0"], sem["sWS1"]]
    sW1 = [sem["sW10"], sem["sW11"]]
    sW3 = [sem["sW30"], sem["sW31"], sem["sW32"], sem["sW33"]]

    P = Prog()

    def dma(eng, out, in_, s, waits=(), **kw):
        return P.op(eng, lambda e: e.dma_start(out=out, in_=in_, **kw),
                    waits=waits, inc=s, inc_by=16)

    dma("gpsimd", identb[:, :], idin, sGD)
    dma("gpsimd", rw_s.rearrange("p (dc e) -> p dc e", dc=NDC),
        rwT.rearrange("(dc p) e -> p dc e", p=128), sGD)
    dma("gpsimd", wq_s.rearrange("p (dc o) -> p dc o", dc=NDC),
        wqT.rearrange("(dc p) o -> p dc o", p=128), sGD)
    dma("gpsimd", wk_s.rearrange("p (dc o) -> p dc o", dc=NDC),
        wkT.rearrange("(dc p) o -> p dc o", p=128), sGD)
    dma("gpsimd", wv_s.rearrange("p (dc o) -> p dc o", dc=NDC),
        wvT.rearrange("(dc p) o -> p dc o", p=128), sGD)
    dma("sync", identf[:, :], idin, sDM)
    dma("sync", c2, cs[0], sDM)
    dma("sync", s2, cs[1], sDM)
    P.op("gpsimd", lambda e: e.memset(ones1f[:, :], 1.0), inc=sGP)
    prolog_gd, prolog_dm, prolog_gp = sGD.n, sDM.n, sGP.n

    if stop_after == "P":
        P.wait("sync", (sGD, sGD.n), (sDM, sDM.n), (sGD2, sGD2.n), (sGP, sGP.n))
        return _finish(nc, P, stack, tap_out)

    xld = []
    xfree = []
    bank_free = {0: (0, 0), 1: (0, 0), 2: (0, 0)}
    stage_dm = {"q": (sSTQ, 0), "k": (sSTK, 0), "v": (sSTV, 0)}
    swp_gate = 0
    ropB_gate = 0
    for sc in range(NSC):
        bi = sc % 2
        w = [(sGD, prolog_gd), (sDM, prolog_dm)]
        if sc >= 2:
            w.append((sPE, xfree[sc - 2]))
        dma("gpsimd", xbf[bi].rearrange("p (dc c) -> p dc c", dc=NDC),
            xT.rearrange("(dc p) s -> p dc s", p=128)[:, :, sc * 512:(sc + 1) * 512],
            sXB[bi], waits=w)
        xld.append((sXB[bi], sXB[bi].n))
        col = sc * 512
        gi = 0
        for (stage, w_s, name, dst) in (() if "noqk" in ABLATE else ((qstage, wq_s, "q", qb), (kstage, wk_s, "k", kb))):
            fin = []
            for ot in range(NH):
                psum = (p0, p1, p3)[gi % 3]
                bfree = bank_free[gi % 3]
                for dc in range(NDC):
                    ww = [xld[sc]]
                    if dc == 0:
                        ww += [(sAC, bfree[0]), (sDV, bfree[1])]

                    def mm(e, psum=psum, w_s=w_s, dc=dc, ot=ot, bi=bi):
                        return e.matmul(
                            psum, w_s[:, dc * 512 + ot * 128:dc * 512 + (ot + 1) * 128],
                            xbf[bi][:, dc * 512:(dc + 1) * 512],
                            start=(dc == 0), stop=(dc == NDC - 1))
                    P.op("tensor", mm, waits=ww, inc=sPE if dc == NDC - 1 else None)
                pe_v = sPE.n
                if "norope" in ABLATE:
                    sl = stage[:, ot * 512:(ot + 1) * 512]
                    P.op("vector", lambda e, sl=sl, psum=psum: e.tensor_copy(sl, psum),
                         waits=[(sPE, pe_v), stage_dm[name], (sDV, ropB_gate)], inc=sDV)
                    ropB_gate = sDV.n
                    fin.append(sDV.n)
                    bank_free[gi % 3] = (sAC.n, sDV.n)
                    gi += 1
                    continue
                P.op("scalar", lambda e, psum=psum: e.copy(swp[0:64, :], psum[64:128, :]),
                     waits=[(sPE, pe_v), (sGP, swp_gate)], inc=sAC)
                P.op("scalar", lambda e, psum=psum: e.copy(swp[64:128, :], psum[0:64, :]),
                     inc=sAC)
                ac_v = sAC.n
                P.op("vector", lambda e, psum=psum, col=col: e.tensor_mul(
                    ropA, psum, c2[:, col:col + 512]),
                    waits=[(sPE, pe_v), (sDM, prolog_dm), (sDV, ropB_gate),
                           (sAC, ac_v)], inc=sDV)
                dv_a = sDV.n
                if "r3" in ABLATE:
                    P.op("gpsimd", lambda e, col=col: e.tensor_mul(
                        ropB, c2[:, col:col + 512], s2[:, col:col + 512]),
                        waits=[(sAC, ac_v), (sDV, ropB_gate)], inc=sGP)
                else:
                    P.op("gpsimd", lambda e, col=col: e.tensor_mul(
                        ropB, swp, s2[:, col:col + 512]),
                        waits=[(sAC, ac_v), (sDV, ropB_gate)], inc=sGP)
                swp_gate = sGP.n
                sl = stage[:, ot * 512:(ot + 1) * 512]
                P.op("vector", lambda e, sl=sl: e.tensor_add(sl, ropA, ropB),
                     waits=[(sDV, dv_a), (sGP, sGP.n), stage_dm[name]], inc=sDV)
                ropB_gate = sDV.n
                fin.append(sDV.n)
                bank_free[gi % 3] = (sAC.n, sDV.n)
                gi += 1
            sch = sSTQ if name == "q" else sSTK
            if "nostage" not in ABLATE:
                dma("sync", dst.rearrange("h p s -> p h s")[:, :, sc * 512:(sc + 1) * 512],
                    stage.rearrange("p (h c) -> p h c", h=NH), sch, waits=[(sDV, fin[-1])])
                stage_dm[name] = (sch, sch.n)
        vfin = []
        for st in range(0 if "nov" in ABLATE else 4):
            psum = (p0, p1, p3)[gi % 3]
            bfree = bank_free[gi % 3]
            for dc in range(NDC):
                ww = [xld[sc]]
                if dc == 0:
                    ww += [(sAC, bfree[0]), (sDV, bfree[1])]

                def mmv(e, psum=psum, dc=dc, st=st, bi=bi):
                    return e.matmul(
                        psum, xbf[bi][:, dc * 512 + st * 128:dc * 512 + (st + 1) * 128],
                        wv_s[:, dc * 512:(dc + 1) * 512],
                        start=(dc == 0), stop=(dc == NDC - 1))
                P.op("tensor", mmv, waits=ww, inc=sPE if dc == NDC - 1 else None)
            pe_v = sPE.n
            P.op("vector", lambda e, psum=psum, st=st: e.tensor_copy(
                vstage[:, st * 512:(st + 1) * 512], psum),
                waits=[(sPE, pe_v), stage_dm["v"]], inc=sDV)
            vfin.append(sDV.n)
            bank_free[gi % 3] = (sAC.n, sDV.n)
            gi += 1
        for hh in range(0 if ("nov" in ABLATE or "nostage" in ABLATE) else NH):
            dma("sync",
                vb[hh].rearrange("(st p) d -> p st d", p=128)[:, sc * 4:(sc + 1) * 4, :],
                vstage.rearrange("p (st h d) -> p st h d", st=4, h=NH)[:, :, hh, :],
                sSTV, waits=[(sDV, vfin[-1])])
        stage_dm["v"] = (sSTV, sSTV.n)
        xfree.append(sPE.n)
    phA_bar = [(sSTQ, sSTQ.n), (sSTK, sSTK.n), (sSTV, sSTV.n)]
    phA_pe = sPE.n

    if "qkv" in taps:
        dma("sync", tap_out["tap_q"].rearrange("h p s -> (h p) s"),
            qb.rearrange("h p s -> (h p) s"), sOUT, waits=phA_bar)
        dma("sync", tap_out["tap_k"].rearrange("h p s -> (h p) s"),
            kb.rearrange("h p s -> (h p) s"), sOUT, waits=phA_bar)
        dma("sync", tap_out["tap_v"].rearrange("h s d -> (h s) d"),
            vb.rearrange("h s d -> (h s) d"), sOUT, waits=phA_bar)

    if stop_after == "A":
        P.wait("sync", (sOUT, sOUT.n))
        return _finish(nc, P, stack, tap_out)

    dma("sync", msk_s.rearrange("p (q c) -> p q c", q=NQT),
        msk.rearrange("q p c -> p q c"), sDM,
        waits=[(sDV, sDV.n), (sGP, sGP.n)])
    mask_dm = sDM.n
    dma("sync", acc.rearrange("p (st d) -> p st d", st=8),
        xo.rearrange("(st p) d -> p st d", p=128), sDM)
    xo_dm = sDM.n

    head_pe_end = phA_pe
    ctx_evac_dv = 0
    exp_gate = [0]
    rs_gate = 0
    recb_v = 0
    for hl in range(NH):
        dma("sync", qh, qb[hl], sHLD,
            waits=phA_bar + [(sPE, head_pe_end)])
        dma("sync", kh, kb[hl], sHLD)
        dma("sync", vh.rearrange("p (st d) -> p st d", st=32),
            vb[hl].rearrange("(st p) d -> p st d", p=128), sHLD)
        hload = sHLD.n
        sc_gate = {0: (0, 0), 1: (0, 0), 2: (0, 0)}
        for qc in range(NQC):
            nkc = qc + 1
            exp_done = {}
            tp_gate = {0: sDV.n, 1: sDV.n, 2: sDV.n}
            atb_gate = [sPE.n] * 4
            tbc = [0]
            ctx_first = [True]

            def emit_scores(kc, qc=qc, exp_done=exp_done):
                for qt in range(NQT):
                    bk = (kc * 4 + qt) % 3
                    psum = (p0, p1, p3)[bk]
                    ww = [(sHLD, hload), (sAC, sc_gate[bk][0]), (sDV, sc_gate[bk][1])]

                    def mms(e, psum=psum, qc=qc, qt=qt, kc=kc):
                        return e.matmul(
                            psum, qh[:, qc * 512 + qt * 128:qc * 512 + (qt + 1) * 128],
                            kh[:, kc * 512:(kc + 1) * 512], start=True, stop=True)
                    P.op("tensor", mms, waits=ww, inc=sPE)
                    pe_v = sPE.n
                    acol = qt * 4096 + kc * 512
                    slot = rs[:, (qt * 16 + kc):(qt * 16 + kc) + 1]
                    if kc == qc:
                        P.op("vector", lambda e, psum=psum, qt=qt: e.tensor_add(
                            scd, psum, msk_s[:, qt * 512:(qt + 1) * 512]),
                            waits=[(sPE, pe_v), (sDM, xo_dm), (sAC, sAC.n)], inc=sDV)
                        dv_v = sDV.n
                        P.op("scalar", lambda e, acol=acol, slot=slot: e.activation(
                            attn[:, acol:acol + 512], scd, AF.Exp, scale=ISQ,
                            accum_out=slot),
                            waits=[(sDV, dv_v), (sPE2, exp_gate[0]), (sDV2, rs_gate)],
                            inc=sAC)
                        sc_gate[bk] = (sc_gate[bk][0], dv_v)
                    else:
                        P.op("scalar", lambda e, psum=psum, acol=acol, slot=slot: e.activation(
                            attn[:, acol:acol + 512], psum, AF.Exp, scale=ISQ,
                            accum_out=slot),
                            waits=[(sPE, pe_v), (sPE2, exp_gate[0]), (sDV2, rs_gate)],
                            inc=sAC)
                        sc_gate[bk] = (sAC.n, sc_gate[bk][1])
                    exp_done[(qt, kc)] = sAC.n

            def emit_trctx(kc, qc=qc, nkc=nkc, exp_done=exp_done,
                           tp_gate=tp_gate, atb_gate=atb_gate, tbc=tbc,
                           ctx_first=ctx_first):
                for skb in range(4):
                    tb = tbc[0] % 3
                    tbc[0] += 1
                    tp = (t0, t1, t2)[tb]
                    for qt in range(NQT):
                        acol = qt * 4096 + kc * 512 + skb * 128
                        ww = [(sAC, exp_done[(qt, kc)])]
                        if qt == 0:
                            ww.append((sDV, tp_gate[tb]))

                        def mtr(e, tp=tp, qt=qt, acol=acol):
                            return e.transpose(tp[:, qt * 128:(qt + 1) * 128],
                                               attn[:, acol:acol + 128], identb[:, :])
                        P.op("tensor", mtr, waits=ww,
                             inc=sPE2 if qt == NQT - 1 else None)
                    tr_v = sPE2.n
                    P.op("vector", lambda e, tp=tp, skb=skb: e.tensor_copy(
                        atb[:, skb * 512:(skb + 1) * 512], tp),
                        waits=[(sPE2, tr_v), (sPE, atb_gate[skb])], inc=sDV)
                    ev_v = sDV.n
                    tp_gate[tb] = ev_v
                    first = ctx_first[0]
                    ctx_first[0] = False

                    def mc(e, kc=kc, skb=skb, first=first,
                           last=(kc == nkc - 1 and skb == 3)):
                        return e.matmul(p2,
                                        vh[:, (kc * 4 + skb) * 128:(kc * 4 + skb + 1) * 128],
                                        atb[:, skb * 512:(skb + 1) * 512],
                                        start=first, stop=last)
                    ww = [(sDV, ev_v)]
                    if first:
                        ww.append((sDV, ctx_evac_dv))
                    P.op("tensor", mc, waits=ww, inc=sPE)
                    atb_gate[skb] = sPE.n

            for kc in range(nkc):
                emit_scores(kc)
                if kc >= 2:
                    emit_trctx(kc - 2)
            for qt in range(NQT):
                P.op("vector", lambda e, qt=qt, nkc=nkc: e.reduce_sum(
                    rrec_s[:, qt:qt + 1], rs[:, qt * 16:qt * 16 + nkc], axis=AX.X),
                    waits=[(sAC, exp_done[(qt, nkc - 1)]), (sDV2, sDV2.n)], inc=sDV2)
            rs_gate = sDV2.n
            P.op("vector", lambda e: e.reciprocal(rrec_s[:, 4:8], rrec_s[:, 0:4]),
                 waits=[(sDV2, sDV2.n)], inc=sDV2)
            rec_v = sDV2.n
            if nkc >= 2:
                emit_trctx(nkc - 2)
            emit_trctx(nkc - 1)
            exp_gate[0] = sPE2.n
            for qt in range(NQT):
                P.op("tensor", lambda e, qt=qt: e.transpose(
                    p7[0:1, qt * 128:(qt + 1) * 128], rrec_s[:, 4 + qt:5 + qt],
                    identf[:, :]),
                    waits=[(sDV2, rec_v), (sDV3, sDV3.n)], inc=sPE2)
            P.op("vector", lambda e: e.tensor_copy(rT_s[:, :], p7[0:1, 0:512]),
                 waits=[(sPE2, sPE2.n)], inc=sDV3)
            P.op("tensor", lambda e: e.matmul(p7, ones1f[:, :], rT_s[:, :],
                                              start=True, stop=True),
                 waits=[(sDV3, sDV3.n)], inc=sPE2)
            P.op("vector", lambda e: e.tensor_copy(recb, p7),
                 waits=[(sPE2, sPE2.n)], inc=sDV3)
            recb_v = sDV3.n
            P.op("vector", lambda e, hl=hl, qc=qc: e.tensor_mul(
                cst[:, hl * 4096 + qc * 512:hl * 4096 + (qc + 1) * 512], p2, recb),
                waits=[(sPE, sPE.n), (sDV3, recb_v)], inc=sDV)
            ctx_evac_dv = sDV.n
        head_pe_end = sPE.n

    if "ctx" in taps:
        dma("sync", tap_out["tap_ctx"], cst, sOUT, waits=[(sDV, ctx_evac_dv)])

    if stop_after == "B":
        P.wait("sync", (sOUT, sOUT.n))
        return _finish(nc, P, stack, tap_out)

    dma("gpsimd", woP_s.rearrange("p (dc o) -> p dc o", dc=NH),
        woP.rearrange("(dc p) o -> p dc o", p=128), sGD,
        waits=[(sAC, sAC.n), (sPE2, sPE2.n), (sPE, head_pe_end)])
    wop_gd = sGD.n
    cst_ready = ctx_evac_dv
    wo_gate = {0: sc_gate[0][0], 1: sc_gate[1][0]}
    wo_dv_gate = {0: 0, 1: 0}
    wostg_dm = [0, 0]
    rs_cc = []
    for oc in range(4):
        for sq in range(8):
            sg = sq % 2
            for j in range(4):
                st = sq * 4 + j
                bk = st % 2
                psum = p0 if bk == 0 else p1
                for hl in range(NH):
                    ww = [(sGD, wop_gd), (sDV, cst_ready)]
                    if hl == 0:
                        ww += [(sDV, wo_dv_gate[bk]), (sWS[sg], wostg_dm[sg])]

                    def mw(e, psum=psum, hl=hl, st=st, oc=oc):
                        return e.matmul(
                            psum, cst[:, hl * 4096 + st * 128:hl * 4096 + (st + 1) * 128],
                            woP_s[:, hl * 2048 + oc * 512:hl * 2048 + (oc + 1) * 512],
                            start=(hl == 0), stop=(hl == NH - 1))
                    P.op("tensor", mw, waits=ww, inc=sPE if hl == NH - 1 else None)
                pe_v = sPE.n
                P.op("vector", lambda e, psum=psum, sg=sg, j=j: e.tensor_copy(
                    wostg[sg][:, j * 512:(j + 1) * 512], psum),
                    waits=[(sPE, pe_v), (sWS[sg], wostg_dm[sg])], inc=sDV)
                wo_dv_gate[bk] = sDV.n
            dma("sync",
                rsin[oc].rearrange("(q p) c -> p q c", p=128)[:, sq * 4:(sq + 1) * 4, :],
                wostg[sg].rearrange("p (j c) -> p j c", j=4), sWS[sg],
                waits=[(sDV, sDV.n)])
            wostg_dm[sg] = sWS[sg].n
        P.op("gpsimd", lambda e, oc=oc: e.collective_compute(
            "ReduceScatter", ALU.add, replica_groups=[[0, 1, 2, 3], [4, 5, 6, 7]],
            ins=[rsin[oc]], outs=[rsout[oc]]),
            waits=[(sWS[0], sWS[0].n), (sWS[1], sWS[1].n)], inc=sCC)
        rs_cc.append(sCC.n)
    wo_pe_end = sPE.n

    if "rs" in taps:
        for oc in range(4):
            dma("sync", tap_out["tap_rs"][oc], rsout[oc], sOUT, waits=[(sCC, rs_cc[oc])])

    for oc in range(4):
        dma("gpsimd", acc.rearrange("p (st d) -> p st d", st=8)[:, :, oc * 512:(oc + 1) * 512],
            rsout[oc].rearrange("(st p) d -> p st d", p=128), sGD,
            waits=[(sCC, rs_cc[oc]), (sDM, xo_dm)], accum_op=ALU.add)
    acc_gd = sGD.n

    if stop_after == "wo":
        P.wait("sync", (sOUT, sOUT.n))
        return _finish(nc, P, stack, tap_out)

    tap_gate = [0]
    outslot_gate = {0: 0, 1: 0}
    stg_pe_gate = [0]
    tp_c_gate = {0: sDV3.n, 1: sDV3.n}
    stg_gp_gate = [0]

    def emit_ln(src_gates, write_hbf, out_to=None):
        for st in range(8):
            a_sl = acc[:, st * D:(st + 1) * D]
            mu = tiny[:, st * 8 + 0:st * 8 + 1]
            ng = tiny[:, st * 8 + 1:st * 8 + 2]
            ms = tiny[:, st * 8 + 2:st * 8 + 3]
            vv = tiny[:, st * 8 + 3:st * 8 + 4]
            sq = tiny[:, st * 8 + 4:st * 8 + 5]
            sd = tiny[:, st * 8 + 5:st * 8 + 6]
            rho = tiny[:, st * 8 + 6:st * 8 + 7]
            P.op("scalar", lambda e, a_sl=a_sl, sq=sq: e.activation(
                sqs, a_sl, AF.Square, accum_out=sq),
                waits=list(src_gates) + [(sAC2, sAC2.n)], inc=sAC2)
            sq_v = sAC2.n
            P.op("vector", lambda e, a_sl=a_sl, mu=mu: e.reduce_sum(mu, a_sl, axis=AX.X),
                 waits=list(src_gates) + [(sDV2, sDV2.n)], inc=sDV2)
            P.op("vector", lambda e, ng=ng, mu=mu: e.tensor_scalar_mul(ng, mu, -1.0 / D),
                 waits=[(sDV2, sDV2.n)], inc=sDV2)
            ng_v = sDV2.n
            P.op("gpsimd", lambda e, ms=ms, ng=ng: e.tensor_mul(ms, ng, ng),
                 waits=[(sDV2, ng_v), (sGP, sGP.n)], inc=sGP)
            ms_v = sGP.n
            P.op("vector", lambda e, vv=vv, sq=sq: e.tensor_scalar_mul(vv, sq, 1.0 / D),
                 waits=[(sAC2, sq_v), (sDV2, sDV2.n)], inc=sDV2)
            P.op("vector", lambda e, vv=vv, ms=ms: e.scalar_tensor_tensor(
                vv, vv, EPS, ms, op0=ALU.add, op1=ALU.subtract),
                 waits=[(sGP, ms_v), (sDV2, sDV2.n)], inc=sDV2)
            vv_v = sDV2.n
            P.op("scalar", lambda e, sd=sd, vv=vv: e.activation(sd, vv, AF.Sqrt),
                 waits=[(sDV2, vv_v), (sAC2, sAC2.n)], inc=sAC2)
            sd_v = sAC2.n
            P.op("vector", lambda e, rho=rho, sd=sd: e.reciprocal(rho, sd),
                 waits=[(sAC2, sd_v), (sDV2, sDV2.n)], inc=sDV2)
            ostg = stg if (write_hbf or st % 2 == 0) else stg2
            oslot = sWS[st % 2]
            ww = [(sDV2, sDV2.n), (sOUT, tap_gate[0]),
                  (sPE2, stg_pe_gate[0]), (sGP, stg_gp_gate[0])]
            if not write_hbf:
                ww.append((oslot, outslot_gate[st % 2]))
            P.op("vector", lambda e, a_sl=a_sl, ng=ng, rho=rho, ostg=ostg: e.tensor_scalar(
                ostg, a_sl, ng, rho, op0=ALU.add, op1=ALU.mult),
                waits=ww, inc=sDV2)
            nrm_v = sDV2.n
            if write_hbf:
                for dc in range(NDC):
                    bk = dc % 2
                    pp = p2 if bk == 0 else p7
                    ww = [(sDV2, nrm_v), (sDV3, tp_c_gate[bk]), (sDV, ctx_evac_dv)]

                    def mt(e, pp=pp, dc=dc):
                        return e.transpose(pp[:, 0:128], stg[:, dc * 128:(dc + 1) * 128],
                                           identf[:, :])
                    P.op("tensor", mt, waits=ww, inc=sPE2)
                    tr_v = sPE2.n
                    P.op("vector", lambda e, pp=pp, dc=dc, st=st: e.tensor_copy(
                        h_bf[:, dc * 1024 + st * 128:dc * 1024 + (st + 1) * 128],
                        pp[:, 0:128]),
                        waits=[(sPE2, tr_v), (sDV3, sDV3.n)], inc=sDV3)
                    tp_c_gate[bk] = sDV3.n
                stg_pe_gate[0] = sPE2.n
                P.op("gpsimd", lambda e, a_sl=a_sl: e.tensor_copy(a_sl, stg),
                     waits=[(sDV2, nrm_v), (sGP, sGP.n), (sDV3, sDV3.n)], inc=sGP)
                stg_gp_gate[0] = sGP.n
            else:
                dma("sync", out_to[st * 128:(st + 1) * 128, :], ostg, oslot,
                    waits=[(sDV2, nrm_v)])
                outslot_gate[st % 2] = oslot.n
                stg_gp_gate[0] = 0
                stg_pe_gate[0] = 0

    emit_ln([(sGD, acc_gd)], write_hbf=True)
    h_ready_gp = sGP.n
    h_bf_ready = sDV3.n

    if "h" in taps:
        dma("sync", tap_out["tap_h"].rearrange("(st p) d -> p st d", p=128),
            acc.rearrange("p (st d) -> p st d", st=8), sOUT,
            waits=[(sGP, h_ready_gp)])
        tap_gate[0] = sOUT.n

    if stop_after == "ln1":
        P.wait("sync", (sOUT, sOUT.n))
        return _finish(nc, P, stack, tap_out)

    rt_gate = 0
    lgv = {}
    for st in range(8):
        for dc in range(NDC):
            ww = [(sDV3, h_bf_ready), (sGD, prolog_gd)]
            if dc == 0:
                ww.append((sDV2, rt_gate))

            def ml(e, dc=dc, st=st):
                return e.matmul(p2[:, st * 8:(st + 1) * 8],
                                h_bf[:, dc * 1024 + st * 128:dc * 1024 + (st + 1) * 128],
                                rw_s[:, dc * 8:(dc + 1) * 8],
                                start=(dc == 0), stop=(dc == NDC - 1))
            P.op("tensor", ml, waits=ww, inc=sPE if dc == NDC - 1 else None)
        pe_v = sPE.n
        b0 = 1152 + st * 40
        lg = af[:, b0:b0 + 8]
        l2 = af[:, b0 + 8:b0 + 16]
        mk1 = af[:, b0 + 16:b0 + 24]
        mk2 = af[:, b0 + 24:b0 + 32]
        m1 = af[:, b0 + 32:b0 + 33]
        m2 = af[:, b0 + 33:b0 + 34]
        dd = af[:, b0 + 34:b0 + 35]
        wA = af[:, b0 + 35:b0 + 36]
        wB = af[:, b0 + 36:b0 + 37]
        P.op("vector", lambda e, lg=lg, st=st: e.tensor_copy(lg, p2[:, st * 8:(st + 1) * 8]),
             waits=[(sPE, pe_v), (sDV2, sDV2.n)], inc=sDV2)
        rt_gate = sDV2.n
        P.op("vector", lambda e, m1=m1, lg=lg: e.reduce_max(m1, lg, axis=AX.X),
             waits=[(sDV2, sDV2.n)], inc=sDV2)
        P.op("vector", lambda e, mk1=mk1, lg=lg, m1=m1: e.tensor_scalar(
            mk1, lg, m1, None, op0=ALU.is_equal), waits=[(sDV2, sDV2.n)], inc=sDV2)
        P.op("vector", lambda e, l2=l2, mk1=mk1, lg=lg: e.scalar_tensor_tensor(
            l2, mk1, -1e30, lg, op0=ALU.mult, op1=ALU.add),
            waits=[(sDV2, sDV2.n)], inc=sDV2)
        P.op("vector", lambda e, m2=m2, l2=l2: e.reduce_max(m2, l2, axis=AX.X),
             waits=[(sDV2, sDV2.n)], inc=sDV2)
        P.op("vector", lambda e, dd=dd, m1=m1, m2=m2: e.tensor_sub(dd, m1, m2),
             waits=[(sDV2, sDV2.n)], inc=sDV2)
        dd_v = sDV2.n
        P.op("scalar", lambda e, wA=wA, dd=dd: e.activation(wA, dd, AF.Sigmoid),
             waits=[(sDV2, dd_v), (sAC2, sAC2.n)], inc=sAC2)
        sg_v = sAC2.n
        P.op("vector", lambda e, wB=wB, wA=wA: e.tensor_scalar(
            wB, wA, -1.0, 1.0, op0=ALU.mult, op1=ALU.add),
            waits=[(sAC2, sg_v), (sDV2, sDV2.n)], inc=sDV2)
        P.op("vector", lambda e, mk2=mk2, l2=l2, m2=m2: e.tensor_scalar(
            mk2, l2, m2, None, op0=ALU.is_equal), waits=[(sDV2, sDV2.n)], inc=sDV2)
        P.op("vector", lambda e, st=st, mk1=mk1, wA=wA: e.tensor_scalar(
            rta[:, st * 8:(st + 1) * 8], mk1, wA, None, op0=ALU.mult),
            waits=[(sDV2, sDV2.n)], inc=sDV2)
        P.op("vector", lambda e, st=st, mk2=mk2, wB=wB: e.scalar_tensor_tensor(
            rta[:, st * 8:(st + 1) * 8], mk2, wB, rta[:, st * 8:(st + 1) * 8],
            op0=ALU.mult, op1=ALU.add), waits=[(sDV2, sDV2.n)], inc=sDV2)
    rta_v = sDV2.n

    if "r" in taps:
        dma("sync", tap_out["tap_r"], rta, sOUT, waits=[(sDV2, rta_v)])
        tap_gate[0] = sOUT.n

    if stop_after == "router":
        P.wait("sync", (sOUT, sOUT.n))
        return _finish(nc, P, stack, tap_out)

    w12_pe_gate = {0: 0, 1: 0}
    w12_dm_gate = {0: 0, 1: 0}
    w3_pe_gate = {0: wo_pe_end, 1: wo_pe_end}
    w3_dm_gate = {0: 0, 1: 0}
    w3_acc_gate = {0: wo_dv_gate[0], 1: wo_dv_gate[1]}
    sil_gate = 0
    zmul_gate = 0
    prev_sweep01 = [wo_pe_end]
    z_pe_gate = [wo_pe_end] * 16
    ffn_pe_end = wo_pe_end
    gate_p0_dv = wo_dv_gate[0]
    val_p1_dv = wo_dv_gate[1]
    for (gidx, grp) in enumerate(EGRPS):
        z_slot_dv = {}
        for (le, e_) in enumerate(grp):
            def emit_rb(e_=e_):
                for half in range(2):
                    for j2 in range(4):
                        st = half * 4 + j2

                        def mtr(e, st=st, e_=e_, j2=j2):
                            return e.transpose(p2[0:1, j2 * 128:(j2 + 1) * 128],
                                               rta[:, st * 8 + e_:st * 8 + e_ + 1],
                                               identf[:, :])
                        P.op("tensor", mtr,
                             waits=[(sDV2, rta_v), (sDV3, sDV3.n)], inc=sPE2)
                    P.op("vector", lambda e, half=half: e.tensor_copy(
                        rT2[:, half * 512:(half + 1) * 512], p2[0:1, 0:512]),
                        waits=[(sPE2, sPE2.n)], inc=sDV3)
                for half in range(2):
                    P.op("tensor", lambda e, half=half: e.matmul(
                        p2, ones1f[:, :], rT2[:, half * 512:(half + 1) * 512],
                        start=True, stop=True),
                        waits=[(sDV3, sDV3.n), (sGP, sGP.n)], inc=sPE2)
                    P.op("vector", lambda e, half=half: e.tensor_copy(
                        rb_f[:, half * 512:(half + 1) * 512], p2),
                        waits=[(sPE2, sPE2.n)], inc=sDV3)
                return sDV3.n
            rb_v = None
            for fb in range(NFB):
                bidx = fb % 2
                ww = [(sPE, w12_pe_gate[bidx]), (sPE, wo_pe_end)]
                dma("gpsimd", w1buf[bidx],
                    w1t[e_, fb].rearrange("p dc f -> p (dc f)"), sW1[bidx], waits=ww)
                dma("gpsimd", w2buf[bidx],
                    w2t[e_, fb].rearrange("p dc f -> p (dc f)"), sW1[bidx])
                wld = sW1[bidx].n
                for sh in range(2):
                    for dc in range(NDC):
                        ww = [(sW1[bidx], wld), (sDV3, h_bf_ready)]
                        if dc == 0:
                            ww += [(sAC, sil_gate), (sDV, gate_p0_dv)]

                        def mg(e, dc=dc, sh=sh, bidx=bidx):
                            return e.matmul(
                                p0, w1buf[bidx][:, dc * 128:(dc + 1) * 128],
                                h_bf[:, dc * 1024 + sh * 512:dc * 1024 + (sh + 1) * 512],
                                start=(dc == 0), stop=(dc == NDC - 1))
                        P.op("tensor", mg, waits=ww, inc=sPE if dc == NDC - 1 else None)
                    g_pe = sPE.n
                    for dc in range(NDC):
                        ww = [(sW1[bidx], wld)]
                        if dc == 0:
                            ww.append((sDV, val_p1_dv))

                        def mv(e, dc=dc, sh=sh, bidx=bidx):
                            return e.matmul(
                                p1, w2buf[bidx][:, dc * 128:(dc + 1) * 128],
                                h_bf[:, dc * 1024 + sh * 512:dc * 1024 + (sh + 1) * 512],
                                start=(dc == 0), stop=(dc == NDC - 1))
                        P.op("tensor", mv, waits=ww, inc=sPE if dc == NDC - 1 else None)
                    v_pe = sPE.n
                    if rb_v is None:
                        rb_v = emit_rb()
                    P.op("scalar", lambda e: e.activation(silA, p0, AF.Silu),
                         waits=[(sPE, g_pe), (sGP, sGP.n)], inc=sAC)
                    sil_gate = sAC.n
                    gate_p0_dv = 0
                    P.op("gpsimd", lambda e, sh=sh: e.tensor_mul(
                        silB, silA, rb_f[:, sh * 512:(sh + 1) * 512]),
                        waits=[(sAC, sil_gate), (sDV3, rb_v), (sDV, zmul_gate)], inc=sGP)
                    gp_v = sGP.n
                    zi = le * 8 + fb
                    zsl = zbuf[:, zi * 1024 + sh * 512:zi * 1024 + (sh + 1) * 512]
                    P.op("vector", lambda e, zsl=zsl: e.tensor_mul(zsl, p1, silB),
                         waits=[(sGP, gp_v), (sPE, v_pe), (sPE, z_pe_gate[zi])],
                         inc=sDV)
                    zmul_gate = sDV.n
                    val_p1_dv = sDV.n
                    z_slot_dv[zi] = sDV.n
                w12_pe_gate[bidx] = sPE.n
        z_all = sDV.n
        w3_pe_gate4 = {0: prev_sweep01[0], 1: prev_sweep01[0],
                       2: sPE.n, 3: sPE.n}
        for dchunk in range(4):
            for (le, e_) in enumerate(grp):
                bidx = (dchunk * 2 + le) % 4
                dma("gpsimd", w3buf[bidx].rearrange("p (fc d) -> p fc d", fc=NFC),
                    w3n[e_].rearrange("(fc p) d -> p fc d", p=128)[:, :, dchunk * 512:(dchunk + 1) * 512],
                    sW3[bidx], waits=[(sPE, w3_pe_gate4[bidx]), (sPE, wo_pe_end)])
            w3ld = [(sW3[i], sW3[i].n) for i in range(4)]
            for st in range(8):
                bk = st % 2
                psum = p0 if bk == 0 else p1
                nmm = 0
                for (le, e_) in enumerate(grp):
                    bidx = (dchunk * 2 + le) % 4
                    for fc in range(NFC):
                        ww = list(w3ld) + [(sDV, z_all)]
                        if nmm == 0:
                            ww.append((sDV, w3_acc_gate[bk]))
                        zi = le * 8 + fc

                        def m3(e, psum=psum, zi=zi, fc=fc, st=st, bidx=bidx, nmm=nmm):
                            return e.matmul(
                                psum,
                                zbuf[:, zi * 1024 + st * 128:zi * 1024 + (st + 1) * 128],
                                w3buf[bidx][:, fc * 512:(fc + 1) * 512],
                                start=(nmm == 0), stop=(nmm == 15))
                        P.op("tensor", m3, waits=ww, inc=sPE if nmm == 15 else None)
                        nmm += 1
                pe_v = sPE.n
                a_sl = acc[:, st * D + dchunk * 512:st * D + (dchunk + 1) * 512]
                P.op("vector", lambda e, a_sl=a_sl, psum=psum: e.tensor_add(
                    a_sl, psum, a_sl),
                    waits=[(sPE, pe_v), (sGP, h_ready_gp), (sOUT, tap_gate[0]),
                           (sDV, sDV.n)], inc=sDV)
                w3_acc_gate[bk] = sDV.n
            w3_pe_gate4[(dchunk * 2) % 4] = sPE.n
            w3_pe_gate4[(dchunk * 2 + 1) % 4] = sPE.n
            prev_sweep01[0] = sPE.n
        for zi in range(16):
            z_pe_gate[zi] = sPE.n
        gate_p0_dv = w3_acc_gate[0]
        val_p1_dv = w3_acc_gate[1]
        ffn_pe_end = sPE.n
        w12_pe_gate[0] = sPE.n
        w12_pe_gate[1] = sPE.n
    ffn_done_dv = sDV.n

    if "ffn" in taps:
        dma("sync", tap_out["tap_ffn"].rearrange("(st p) d -> p st d", p=128),
            acc.rearrange("p (st d) -> p st d", st=8), sOUT, waits=[(sDV, ffn_done_dv)])
        tap_gate[0] = sOUT.n

    if stop_after == "ffn":
        P.wait("sync", (sOUT, sOUT.n))
        return _finish(nc, P, stack, tap_out)

    stg_pe_gate[0] = 0
    stg_gp_gate[0] = 0
    emit_ln([(sDV, ffn_done_dv)], write_hbf=False, out_to=outT)

    P.wait("sync", (sOUT, sOUT.n))

    return _finish(nc, P, stack, tap_out)


def _finish(nc, P, stack, tap_out):
    with nc.Block() as block:
        P.replay(block)
    stack.close()
    return nc, list(tap_out.keys())


_HPERM = np.concatenate([np.arange(0, 128, 2), np.arange(1, 128, 2)])


def _host_inputs(L0_attn_x, Wq, Wk, Wv, Wo, router_w, w1, w2, w3):
    x = np.asarray(L0_attn_x, dtype=np.float32)
    Wq = np.asarray(Wq, np.float32); Wk = np.asarray(Wk, np.float32)
    Wv = np.asarray(Wv, np.float32); Wo = np.asarray(Wo, np.float32)
    router_w = np.asarray(router_w, np.float32)
    w1 = np.asarray(w1, np.float32); w2 = np.asarray(w2, np.float32)
    w3 = np.asarray(w3, np.float32)

    xT = [np.ascontiguousarray(x[b].T) for b in range(B)]
    j = np.arange(64, dtype=np.float64)
    inv = np.power(10000.0, -2.0 * j / DK)
    pos = np.arange(S, dtype=np.float64)
    ang = pos[None, :] * inv[:, None]
    cosv = np.cos(ang); sinv = np.sin(ang)
    c2 = np.concatenate([cosv, cosv], 0).astype(np.float32)
    s2 = np.concatenate([-sinv, sinv], 0).astype(np.float32)
    csh = np.ascontiguousarray(np.stack([c2, s2], 0))
    ii = np.arange(128)[:, None]
    jj = np.arange(512)[None, :]
    mskh = np.ascontiguousarray(np.stack(
        [np.where(jj <= qt * 128 + ii, 0.0, -1e30).astype(np.float32)
         for qt in range(NQT)], 0))
    idin = np.eye(128, dtype=np.float32)
    w1th = np.ascontiguousarray(
        w1.reshape(NE, NDC, 128, NFB, 128).transpose(0, 3, 2, 1, 4))
    w2th = np.ascontiguousarray(
        w2.reshape(NE, NDC, 128, NFB, 128).transpose(0, 3, 2, 1, 4))

    in_maps = []
    for c in range(8):
        b, r = c // 4, c % 4
        heads = [4 * r + i for i in range(NH)]
        qrows = np.concatenate([h * 128 + _HPERM for h in heads])
        vrows = np.concatenate([h * 128 + np.arange(128) for h in heads])
        in_maps.append({
            "xT": xT[b],
            "xo": np.ascontiguousarray(x[b][r * NT:(r + 1) * NT]),
            "wqT": np.ascontiguousarray(Wq[qrows].T),
            "wkT": np.ascontiguousarray(Wk[qrows].T),
            "wvT": np.ascontiguousarray(Wv[vrows].T),
            "woP": np.ascontiguousarray(Wo[:, vrows].T),
            "rwT": np.ascontiguousarray(router_w.T),
            "w1t": w1th, "w2t": w2th, "w3n": w3,
            "cs": csh, "msk": mskh, "idin": idin,
        })
    return in_maps


def kernel(L0_attn_x, L0_attn_Wq, L0_attn_Wk, L0_attn_Wv, L0_attn_Wo,
           L0_ffn_router_w, L0_ffn_w1, L0_ffn_w2, L0_ffn_w3):
    in_maps = _host_inputs(L0_attn_x, L0_attn_Wq, L0_attn_Wk, L0_attn_Wv,
                           L0_attn_Wo, L0_ffn_router_w, L0_ffn_w1, L0_ffn_w2,
                           L0_ffn_w3)
    nc, _ = build_program()
    res = run_bass_kernel_spmd(nc, in_maps, core_ids=list(range(8)))
    out = np.empty((B, S, D), np.float32)
    for c in range(8):
        b, r = c // 4, c % 4
        out[b, r * NT:(r + 1) * NT, :] = res.results[c]["outT"]
    return out
